# version 22
# baseline (speedup 1.0000x reference)
import numpy as np
import concourse.bass as bass
import concourse.mybir as mybir
from concourse.bass_utils import run_bass_kernel_spmd
from contextlib import ExitStack

F32 = mybir.dt.float32
BF16 = mybir.dt.bfloat16
F16 = mybir.dt.float16
AF = mybir.ActivationFunctionType
ALU = mybir.AluOpType
AX = mybir.AxisListType

TOK = 1024
NTT = 8
HID = 2048
NFC = 16
FF = 5632
NKC = 44
NH = 16
OS = 256
NOS = FF // OS
WMEAN_N = float(FF * HID)
LADDER = [128.0, 64.0, 32.0, 16.0, 8.0, 4.0, 2.0, 1.0]


class Ctr:
    def __init__(self, stack, nc, name):
        self.h = stack.enter_context(nc.semaphore(name))
        self.n = 0


class Q:

    def __init__(self, block):
        self.block = block
        self.qs = {k: [] for k in ("sync", "gpsimd", "vector", "scalar", "tensor")}

    def add(self, eng, fn):
        self.qs[eng].append(fn)

    def flush(self):
        for eng, cmds in self.qs.items():
            if not cmds:
                continue
            snapshot = list(cmds)
            cmds.clear()

            def body(e, snapshot=snapshot):
                for c in snapshot:
                    c(e)
            getattr(self.block, eng)(body)


def build(debug=False):
    nc = bass.Bass("TRN2", target_bir_lowering=False,
                   detect_race_conditions=False)
    dx = nc.dram_tensor("x_tok", [TOK, HID], F32, kind="ExternalInput").ap()
    dxT = nc.dram_tensor("xT", [HID, TOK], F32, kind="ExternalInput").ap()
    dwq = nc.dram_tensor("wqT", [HID, HID], F32, kind="ExternalInput").ap()
    dwo = nc.dram_tensor("woT", [HID, HID], F32, kind="ExternalInput").ap()
    dmpm = nc.dram_tensor("mpm", [NH, 128, 512], F32, kind="ExternalInput").ap()
    dmpn = nc.dram_tensor("mpn", [128, 64], F32, kind="ExternalInput").ap()
    dbw = nc.dram_tensor("bw", [16, 4], F32, kind="ExternalInput").ap()
    dw1 = nc.dram_tensor("w1T", [HID, FF], F32, kind="ExternalInput").ap()
    dw2 = nc.dram_tensor("w2T", [HID, FF], F32, kind="ExternalInput").ap()
    dw3 = nc.dram_tensor("w3T", [FF, HID], F32, kind="ExternalInput").ap()
    dout = nc.dram_tensor("out", [HID, TOK], F32, kind="ExternalOutput").ap()

    dwq_c = dwq.rearrange("(c p) o -> p c o", p=128)
    dwo_c = dwo.rearrange("(c p) o -> p c o", p=128)
    dw1_c = dw1.rearrange("(c p) f -> p c f", p=128)
    dw2_c = dw2.rearrange("(c p) f -> p c f", p=128)
    dw3_c = dw3.rearrange("(c p) o -> p c o", p=128)
    dmpm_t = dmpm.transpose([1, 0, 2])

    dbg = {}
    if debug:
        for nm, shape, dt_ in [
                ("dbg_xnT", [128, NFC * TOK], BF16),
                ("dbg_rqT", [128, NFC * TOK], BF16),
                ("dbg_stf1", [128, 96], F32),
                ("dbg_sqT", [128, NH * TOK], BF16),
                ("dbg_combT", [128, NFC * TOK], BF16),
                ("dbg_sw", [128, 16], F32),
                ("dbg_hTa", [128, 16 * TOK], F16),
                ("dbg_hTb", [128, 28 * TOK], F16),
                ("dbg_stf2", [128, 96], F32),
                ("dbg_q2a", [128, 16 * TOK], F16),
                ("dbg_q2b", [128, 28 * TOK], F16),
                ("dbg_qs2r", [128, TOK], F32),
                ("dbg_s3r", [128, TOK], F32),
                ("dbg_wab", [128, 64], F32)]:
            dbg[nm] = nc.dram_tensor(nm, shape, dt_, kind="ExternalOutput").ap()

    SG = ExitStack()
    with SG:
        sb = lambda *a, **k: SG.enter_context(nc.sbuf_tensor(*a, **k))
        ps = lambda *a: SG.enter_context(nc.psum_tensor(*a))
        ident = sb("ident", [128, 128], BF16)
        ident16 = sb("ident16", [128, 128], F16)
        iotaf = sb("iotaf", [128, 128], F32)
        onesr = sb("onesr", [1, 128], F32)
        onescol = sb("onescol", [128, 1], F32)
        wa_b = sb("wa_b", [128, 64], F32)
        stf = sb("stf", [128, 96], F32)
        sttmp = sb("sttmp", [128, 8], F32)
        swcol = sb("swcol", [128, 16], F32)
        hmx = sb("hmx", [128, 176], F32)
        sm_t = sb("sm_t", [16, 32], F32)
        act4 = sb("act4", [128, 4], F32)
        smallrow = sb("smallrow", [1, 64], F32)
        wsrun = sb("wsrun", [128, 4], F32)
        rn64 = sb("rn64", [128, 64], F32)
        rn64b = sb("rn64b", [128, 64], F32)
        ccol = sb("ccol", [128, 2], F32)
        sc11 = sb("sc11", [1, 8], F32)
        pb = [ps("pb%d" % i, [128, 512], F32) for i in range(8)]

        dS = Ctr(SG, nc, "dS")
        dS2 = Ctr(SG, nc, "dS2")
        dG = Ctr(SG, nc, "dG")
        cV = Ctr(SG, nc, "cV")
        cS = Ctr(SG, nc, "cS")
        cT = Ctr(SG, nc, "cT")
        cG = Ctr(SG, nc, "cG")
        dA = Ctr(SG, nc, "dA")
        ALLC = [dS, dS2, dG, cV, cS, cT, cG, dA]
        block = SG.enter_context(nc.Block())
        PQ = Q(block)

        def inc(ctr, k=1):
            ctr.n += k
            return ctr.n

        def barrier():
            PQ.flush()
            snap = [(c.h, c.n) for c in ALLC if c.n > 0]

            def mk(engdec):
                @engdec
                def _(eng):
                    for h, v in snap:
                        eng.wait_ge(h, v)
            for ed in (block.sync, block.gpsimd, block.vector, block.scalar,
                       block.tensor):
                mk(ed)

        def dbg_dump(pairs):
            if not dbg:
                return
            PQ.flush()
            snap = [(c.h, c.n) for c in ALLC if c.n > 0]

            @block.sync
            def _(sync):
                for h, v in snap:
                    sync.wait_ge(h, v)
                for nm, srcap in pairs:
                    sync.dma_start(out=dbg[nm], in_=srcap).then_inc(dS.h, 16)
            dS.n += 16 * len(pairs)

        dS.n += 16
        p0_bw = dS.n
        PQ.add("sync", lambda e: e.dma_start(out=sm_t[0:16, 0:4], in_=dbw[:]
                                             ).then_inc(dS.h, 16))

        cG.n += 1
        p0_iota = cG.n
        dG.n += 16
        p0_norm = dG.n

        def _gp0(e):
            e.iota(iotaf[:], pattern=[[1, 128]], base=0, channel_multiplier=-1,
                   allow_small_or_imprecise_dtypes=True)
            e.drain().then_inc(cG.h, 1)
            e.dma_start(out=wa_b[:], in_=dmpn[:]).then_inc(dG.h, 16)
        PQ.add("gpsimd", _gp0)

        cV.n += 1
        p0_v1 = cV.n

        def _v0(e):
            e.wait_ge(cG.h, p0_iota)
            e.tensor_single_scalar(ident[:], iotaf[:], 0.0, ALU.is_equal)
            e.tensor_single_scalar(ident16[:], iotaf[:], 0.0, ALU.is_equal)
            e.memset(onesr[:], 1.0)
            e.memset(onescol[:], 1.0)
            e.memset(ccol[:, 0:1], 1e-5)
            e.memset(stf[:, 40:48], 0.0)
            e.memset(stf[:, 72:80], 0.0)
            e.wait_ge(dS.h, p0_bw)
            e.reduce_max(sm_t[0:16, 16:17], sm_t[0:16, 0:4], axis=AX.X)
            e.drain()
            e.tensor_scalar_mul(sm_t[0:16, 19:20], sm_t[0:16, 16:17], -1.0)
            e.wait_ge(dG.h, p0_norm)
            nb = wa_b[:].rearrange("p (h n) -> p h n", n=4)
            for n in range(4):
                e.reduce_sum(act4[:, n:n + 1], nb[:, :, n], axis=AX.X)
            e.drain().then_inc(cV.h, 1)
        PQ.add("vector", _v0)

        cS.n += 1
        p0_exp = cS.n

        def _s0(e):
            e.wait_ge(cV.h, p0_v1)
            e.activation(sm_t[0:16, 4:8], sm_t[0:16, 0:4], AF.Exp,
                         bias=sm_t[0:16, 19:20], scale=1.0)
            e.drain().then_inc(cS.h, 1)
        PQ.add("scalar", _s0)

        cT.n += 1
        p0_mm1 = cT.n

        def _t0(e):
            e.wait_ge(cV.h, p0_v1)
            e.matmul(pb[5][0:4, 0:1], act4[:], onescol[:], start=True, stop=True)
            e.drain().then_inc(cT.h, 1)
        PQ.add("tensor", _t0)

        cV.n += 1
        p0_v2 = cV.n

        def _v1(e):
            e.wait_ge(cS.h, p0_exp)
            e.reduce_sum(sm_t[0:16, 17:18], sm_t[0:16, 4:8], axis=AX.X)
            e.drain()
            e.reciprocal(sm_t[0:16, 18:19], sm_t[0:16, 17:18])
            e.drain()
            e.tensor_scalar(sm_t[0:16, 8:12], sm_t[0:16, 4:8],
                            sm_t[0:16, 18:19], None, ALU.mult)
            e.wait_ge(cT.h, p0_mm1)
            e.tensor_single_scalar(act4[0:4, 0:1], pb[5][0:4, 0:1], 1e-6,
                                   ALU.is_ge)
            e.drain().then_inc(cV.h, 1)
        PQ.add("vector", _v1)

        dG.n += 16
        p0_arow = dG.n

        def _gp1(e):
            e.wait_ge(cV.h, p0_v2)
            e.dma_start(out=smallrow[0:1, 0:4], in_=act4[0:4, 0:1],
                        allow_slow_non_contiguous=True).then_inc(dG.h, 16)
        PQ.add("gpsimd", _gp1)

        cT.n += 1
        p0_mm2 = cT.n

        def _t1(e):
            e.wait_ge(dG.h, p0_arow)
            e.matmul(pb[5][0:16, 4:8], onesr[0:1, 0:16], smallrow[0:1, 0:4],
                     start=True, stop=True)
            e.drain().then_inc(cT.h, 1)
        PQ.add("tensor", _t1)

        cV.n += 1
        p0_v3 = cV.n

        def _v2(e):
            e.wait_ge(cT.h, p0_mm2)
            e.tensor_tensor(sm_t[0:16, 12:16], sm_t[0:16, 8:12],
                            pb[5][0:16, 4:8], ALU.mult)
            e.drain().then_inc(cV.h, 1)
        PQ.add("vector", _v2)

        dG.n += 16
        p0_wrow = dG.n

        def _gp2(e):
            e.wait_ge(cV.h, p0_v3)
            e.dma_start(out=smallrow[0:1, 0:64], in_=sm_t[0:16, 12:16],
                        allow_slow_non_contiguous=True).then_inc(dG.h, 16)
        PQ.add("gpsimd", _gp2)

        cT.n += 1
        p0_mm3 = cT.n

        def _t2(e):
            e.wait_ge(dG.h, p0_wrow)
            e.matmul(pb[5][:, 8:72], onesr[:], smallrow[0:1, 0:64],
                     start=True, stop=True)
            e.drain().then_inc(cT.h, 1)
        PQ.add("tensor", _t2)

        cV.n += 1

        def _v3(e):
            e.wait_ge(cT.h, p0_mm3)
            e.tensor_copy(wa_b[:], pb[5][:, 8:72])
            e.drain().then_inc(cV.h, 1)
        PQ.add("vector", _v3)

        L1 = ExitStack()
        with L1:
            rqT = L1.enter_context(nc.sbuf_tensor("rqT", [128, NFC * TOK], BF16))
            L2 = ExitStack()
            with L2:
                sqT = L2.enter_context(nc.sbuf_tensor("sqT", [128, NH * TOK], BF16))
                mp16m = L2.enter_context(
                    nc.sbuf_tensor("mp16m", [128, NH * 512], BF16))
                nrm16 = L2.enter_context(nc.sbuf_tensor("nrm16", [128, 64], BF16))

                dG.n += 32
                d_mp = dG.n

                def _gpl(e):
                    e.dma_start(out=mp16m[:].rearrange("p (h c) -> p h c", c=512),
                                in_=dmpm_t[:]).then_inc(dG.h, 16)
                    e.dma_start(out=nrm16[:], in_=dmpn[:]).then_inc(dG.h, 16)
                PQ.add("gpsimd", _gpl)

                L3 = ExitStack()
                with L3:
                    sb3 = lambda *a, **k: L3.enter_context(nc.sbuf_tensor(*a, **k))
                    xa = sb3("xa", [128, HID], F32)
                    xsq = sb3("xsq", [128, HID], F32)
                    xn16 = [sb3("xn16_%d" % i, [128, HID], BF16) for i in range(2)]
                    rq16 = [sb3("rq16_%d" % i, [128, HID], BF16) for i in range(2)]
                    ub = sb3("ub", [128, HID], F32)
                    y0b = sb3("y0b", [128, HID], F32)
                    ywk = sb3("ywk", [128, HID], F32)
                    wrapc = sb3("wrapc", [128, HID], F32)
                    xnT = sb3("xnT", [128, NFC * TOK], BF16)
                    wq_s = [sb3("wq_s%d" % i, [128, HID], BF16) for i in range(2)]
                    sigt = sb3("sigt", [128, 512], F32)
                    sige = sb3("sige", [128, 512], F32)

                    barrier()
                    d_x = [0] * NTT
                    v_var = [0] * NTT
                    s_ln = [0] * NTT
                    v_neg = [0] * NTT
                    s_rstd = [0] * NTT
                    v_p1 = [0] * NTT
                    t_tr = {}
                    s_tr = {}
                    for tt in range(NTT):
                        dS.n += 16
                        d_x[tt] = dS.n
                        v_var[tt] = inc(cV)
                        v_neg[tt] = inc(cV)
                        v_p1[tt] = inc(cV)
                    for tt in range(NTT):
                        for b_ in range(8):
                            t_tr[(tt, b_)] = inc(cT)
                    for tt in range(NTT):
                        s_ln[tt] = inc(cS)
                        s_rstd[tt] = inc(cS)
                        if tt >= 1:
                            for b_ in range(8):
                                s_tr[(tt - 1, b_)] = inc(cS)
                    for b_ in range(8):
                        s_tr[(NTT - 1, b_)] = inc(cS)
                    s_p1_done = inc(cS)

                    def _p1_sync(e):
                        for tt in range(NTT):
                            if tt >= 1:
                                e.wait_ge(cV.h, v_p1[tt - 1])
                            e.dma_start(out=xa[:],
                                        in_=dx[tt * 128:(tt + 1) * 128, :]
                                        ).then_inc(dS.h, 16)
                    PQ.add("sync", _p1_sync)

                    def _p1_vec(e):
                        for tt in range(NTT):
                            e.wait_ge(dS.h, d_x[tt])
                            if tt >= 2:
                                e.wait_ge(cS.h, s_tr[(tt - 2, 7)])
                            e.tensor_tensor(xsq[:], xa[:], xa[:], ALU.mult)
                            e.reduce_sum(sttmp[:, 0:1], xa[:], axis=AX.X)
                            e.reduce_sum(sttmp[:, 1:2], xsq[:], axis=AX.X)
                            e.reduce_max(sttmp[:, 2:3], xa[:], axis=AX.X)
                            e.tensor_reduce(sttmp[:, 3:4], xa[:], AX.X, ALU.min)
                            e.drain()
                            e.tensor_scalar_mul(stf[:, 0 + tt:1 + tt],
                                                sttmp[:, 0:1], -1.0 / HID)
                            e.tensor_scalar_mul(sttmp[:, 6:7], sttmp[:, 1:2],
                                                1.0 / HID)
                            e.drain()
                            e.tensor_tensor(sttmp[:, 7:8], stf[:, 0 + tt:1 + tt],
                                            stf[:, 0 + tt:1 + tt], ALU.mult)
                            e.drain()
                            e.tensor_tensor(sttmp[:, 4:5], sttmp[:, 6:7],
                                            sttmp[:, 7:8], ALU.subtract)
                            e.drain().then_inc(cV.h, 1)
                            e.wait_ge(cS.h, s_ln[tt])
                            e.tensor_scalar_mul(sttmp[:, 4:5], sttmp[:, 5:6], -0.5)
                            e.drain().then_inc(cV.h, 1)
                            e.wait_ge(cS.h, s_rstd[tt])
                            e.tensor_tensor(sttmp[:, 6:7], sttmp[:, 2:3],
                                            stf[:, 0 + tt:1 + tt], ALU.add)
                            e.tensor_tensor(sttmp[:, 7:8], sttmp[:, 3:4],
                                            stf[:, 0 + tt:1 + tt], ALU.add)
                            e.drain()
                            e.tensor_scalar_mul(sttmp[:, 5:6], sttmp[:, 7:8], -1.0)
                            e.drain()
                            e.tensor_tensor(sttmp[:, 2:3], sttmp[:, 6:7],
                                            sttmp[:, 5:6], ALU.max)
                            e.drain()
                            e.tensor_tensor(sttmp[:, 6:7], sttmp[:, 2:3],
                                            stf[:, 8 + tt:9 + tt], ALU.mult)
                            e.drain()
                            e.tensor_scalar_max(sttmp[:, 7:8], sttmp[:, 6:7], 1e-5)
                            e.drain()
                            e.reciprocal(sttmp[:, 6:7], sttmp[:, 7:8])
                            e.drain()
                            e.tensor_scalar_mul(stf[:, 16 + tt:17 + tt],
                                                sttmp[:, 6:7], 127.0)
                            e.drain()
                            e.reciprocal(stf[:, 24 + tt:25 + tt],
                                         stf[:, 16 + tt:17 + tt])
                            e.drain()
                            e.tensor_tensor(stf[:, 32 + tt:33 + tt],
                                            stf[:, 8 + tt:9 + tt],
                                            stf[:, 16 + tt:17 + tt], ALU.mult)
                            e.drain()
                            e.tensor_scalar(xn16[tt % 2][:], xa[:],
                                            stf[:, 0 + tt:1 + tt],
                                            stf[:, 8 + tt:9 + tt],
                                            ALU.add, ALU.mult)
                            e.tensor_scalar(ub[:], xa[:],
                                            stf[:, 0 + tt:1 + tt],
                                            stf[:, 32 + tt:33 + tt],
                                            ALU.add, ALU.mult)
                            e.tensor_single_scalar(y0b[:], ub[:], 128.5, ALU.add)
                            src = y0b
                            dsts = [ywk, ub]
                            for wi, P in enumerate(LADDER):
                                dst = dsts[wi % 2]
                                e.tensor_scalar(wrapc[:], src[:], P, -P,
                                                ALU.is_gt, ALU.mult)
                                e.tensor_tensor(dst[:], src[:], wrapc[:], ALU.add)
                                src = dst
                            e.scalar_tensor_tensor(rq16[tt % 2][:], y0b[:], 128.0,
                                                   src[:], ALU.subtract,
                                                   ALU.subtract)
                            e.drain().then_inc(cV.h, 1)
                    PQ.add("vector", _p1_vec)

                    def _p1_evac(e, tt):
                        for b_ in range(8):
                            srcsel, g = divmod(b_, 4)
                            dstT = xnT if srcsel == 0 else rqT
                            bank = pb[6 + ((tt * 8 + b_) % 2)]
                            e.wait_ge(cT.h, t_tr[(tt, b_)])
                            dv = dstT[:].rearrange("p (c t) -> p c t", t=TOK)
                            e.activation(
                                dv[:, 4 * g:4 * g + 4, tt * 128:(tt + 1) * 128],
                                bank[:, 0:512], AF.Identity
                                ).then_inc(cS.h, 1)

                    def _p1_scal(e):
                        for tt in range(NTT):
                            e.wait_ge(cV.h, v_var[tt])
                            e.activation(sttmp[:, 5:6], sttmp[:, 4:5], AF.Ln,
                                         bias=ccol[:, 0:1], scale=1.0)
                            e.drain().then_inc(cS.h, 1)
                            e.wait_ge(cV.h, v_neg[tt])
                            e.activation(stf[:, 8 + tt:9 + tt], sttmp[:, 4:5],
                                         AF.Exp, scale=1.0)
                            e.drain().then_inc(cS.h, 1)
                            if tt >= 1:
                                _p1_evac(e, tt - 1)
                        _p1_evac(e, NTT - 1)
                        e.drain().then_inc(cS.h, 1)
                    PQ.add("scalar", _p1_scal)

                    def _p1_tens(e):
                        for tt in range(NTT):
                            e.wait_ge(cV.h, v_p1[tt])
                            for b_ in range(8):
                                srcsel, g = divmod(b_, 4)
                                src16 = xn16[tt % 2] if srcsel == 0 else rq16[tt % 2]
                                bank = pb[6 + ((tt * 8 + b_) % 2)]
                                gb = tt * 8 + b_
                                if gb >= 2:
                                    pv = gb - 2
                                    e.wait_ge(cS.h, s_tr[(pv // 8, pv % 8)])
                                for j in range(4):
                                    c = 4 * g + j
                                    e.matmul(bank[:, j * 128:(j + 1) * 128],
                                             src16[:, c * 128:(c + 1) * 128],
                                             ident[:], start=True, stop=True)
                                e.drain().then_inc(cT.h, 1)
                    PQ.add("tensor", _p1_tens)

                    dbg_dump([("dbg_xnT", xnT[:]), ("dbg_rqT", rqT[:]),
                              ("dbg_stf1", stf[:])])
                    barrier()
                    d_wq = [0] * NFC
                    t_qp = [0] * NFC
                    v_min = {}
                    s_sig = {}
                    v_done = {}
                    for o in range(NFC):
                        dG.n += 16
                        d_wq[o] = dG.n
                        t_qp[o] = inc(cT)
                        for half in range(2):
                            v_min[(o, half)] = inc(cV)
                            v_done[(o, half)] = inc(cV)
                    v_p2a_done = inc(cV)
                    for o in range(NFC):
                        for half in range(2):
                            s_sig[(o, half)] = inc(cS)

                    def _p2a_gp(e):
                        for o in range(NFC):
                            if o >= 2:
                                e.wait_ge(cT.h, t_qp[o - 2])
                            e.dma_start(
                                out=wq_s[o % 2][:].rearrange("p (c o2) -> p c o2",
                                                             o2=128),
                                in_=dwq_c[:, :, o * 128:(o + 1) * 128]
                            ).then_inc(dG.h, 16)
                    PQ.add("gpsimd", _p2a_gp)

                    def _p2a_t(e):
                        for o in range(NFC):
                            e.wait_ge(dG.h, d_wq[o])
                            if o >= 2:
                                e.wait_ge(cV.h, v_done[(o - 2, 1)])
                            pa, pbk = pb[2 * (o % 2)], pb[2 * (o % 2) + 1]
                            for i in range(NFC):
                                lhsT = wq_s[o % 2][:, i * 128:(i + 1) * 128]
                                e.matmul(pa[:, 0:512], lhsT,
                                         xnT[:, i * TOK:i * TOK + 512],
                                         start=(i == 0), stop=(i == NFC - 1))
                                e.matmul(pbk[:, 0:512], lhsT,
                                         xnT[:, i * TOK + 512:(i + 1) * TOK],
                                         start=(i == 0), stop=(i == NFC - 1))
                            e.drain().then_inc(cT.h, 1)
                    PQ.add("tensor", _p2a_t)

                    def _p2a_v(e):
                        for o in range(NFC):
                            e.wait_ge(cT.h, t_qp[o])
                            for half in range(2):
                                pban = pb[2 * (o % 2) + half]
                                e.tensor_scalar_min(sigt[:], pban[:, 0:512], 0.0)
                                e.drain().then_inc(cV.h, 1)
                                e.wait_ge(cS.h, s_sig[(o, half)])
                                e.scalar_tensor_tensor(
                                    sqT[:, o * TOK + half * 512:
                                        o * TOK + (half + 1) * 512],
                                    pban[:, 0:512], 0.0, sige[:],
                                    ALU.max, ALU.add
                                    ).then_inc(cV.h, 1)
                        e.drain().then_inc(cV.h, 1)
                    PQ.add("vector", _p2a_v)

                    def _p2a_s(e):
                        for o in range(NFC):
                            for half in range(2):
                                e.wait_ge(cV.h, v_min[(o, half)])
                                e.activation(sige[:], sigt[:], AF.Exp)
                                e.drain().then_inc(cS.h, 1)
                    PQ.add("scalar", _p2a_s)

                L3b = ExitStack()
                with L3b:
                    sb3b = lambda *a, **k: L3b.enter_context(nc.sbuf_tensor(*a, **k))
                    combT = sb3b("combT", [128, NFC * TOK], BF16)
                    comb16 = [sb3b("comb16_%d" % i, [128, HID], BF16)
                              for i in range(2)]
                    cadd = sb3b("cadd", [128, 256], F32)
                    wo_s = [sb3b("wo_s%d" % i, [128, HID], BF16) for i in range(2)]
                    xt_b = [sb3b("xt_b%d" % i, [128, TOK], F32) for i in range(2)]
                    ot_b = [sb3b("ot_b%d" % i, [128, TOK], F32) for i in range(2)]

                    dbg_dump([("dbg_sqT", sqT[:]), ("dbg_wab", wa_b[:])])
                    barrier()
                    t_nrm = [0] * NTT
                    v_rn = [0] * NTT
                    t_A = {}
                    v_cb = {}
                    v_cbd = [0] * NTT
                    t_ct = {}
                    s_ct = {}
                    for tt in range(NTT):
                        t_nrm[tt] = inc(cT)
                        for h in range(NH):
                            t_A[(tt, h)] = inc(cT)
                        for g in range(4):
                            t_ct[(tt, g)] = inc(cT)
                    for tt in range(NTT):
                        v_rn[tt] = inc(cV)
                        for h in range(NH):
                            v_cb[(tt, h)] = inc(cV)
                        v_cbd[tt] = inc(cV)
                    for tt in range(NTT):
                        for g in range(4):
                            s_ct[(tt, g)] = inc(cS)
                    s_p2b_done = inc(cS)

                    def _p2b_t(e):
                        for tt in range(NTT):
                            if tt >= 1:
                                e.wait_ge(cV.h, v_rn[tt - 1])
                            for h in range(NH):
                                sq_blk = sqT[:, h * TOK + tt * 128:
                                             h * TOK + (tt + 1) * 128]
                                e.matmul(pb[4][:, 4 * h:4 * h + 4], sq_blk,
                                         nrm16[:, 4 * h:4 * h + 4],
                                         start=True, stop=True)
                            e.drain().then_inc(cT.h, 1)
                            for h in range(NH):
                                step = tt * NH + h
                                if step >= 4:
                                    pvs = step - 4
                                    e.wait_ge(cV.h, v_cb[(pvs // NH, pvs % NH)])
                                sq_blk = sqT[:, h * TOK + tt * 128:
                                             h * TOK + (tt + 1) * 128]
                                e.matmul(pb[step % 4][:, 0:512], sq_blk,
                                         mp16m[:, h * 512:(h + 1) * 512],
                                         start=True, stop=True)
                                e.drain().then_inc(cT.h, 1)
                            e.wait_ge(cV.h, v_cbd[tt])
                            for g in range(4):
                                ctb = tt * 4 + g
                                bank = pb[6 + (ctb % 2)]
                                if ctb >= 2:
                                    pv = ctb - 2
                                    e.wait_ge(cS.h, s_ct[(pv // 4, pv % 4)])
                                for j in range(4):
                                    c = 4 * g + j
                                    e.matmul(bank[:, j * 128:(j + 1) * 128],
                                             comb16[tt % 2][:, c * 128:(c + 1) * 128],
                                             ident[:], start=True, stop=True)
                                e.drain().then_inc(cT.h, 1)
                    PQ.add("tensor", _p2b_t)

                    def _p2b_v(e):
                        for tt in range(NTT):
                            e.wait_ge(cT.h, t_nrm[tt])
                            if tt >= 2:
                                e.wait_ge(cS.h, s_ct[(tt - 2, 3)])
                            e.tensor_scalar_max(rn64[:], pb[4][:, 0:64], 1e-6)
                            e.drain()
                            e.reciprocal(rn64b[:], rn64[:])
                            e.drain()
                            e.tensor_tensor(rn64[:], rn64b[:], wa_b[:], ALU.mult)
                            e.drain().then_inc(cV.h, 1)
                            for h in range(NH):
                                e.wait_ge(cT.h, t_A[(tt, h)])
                                pA = pb[(tt * NH + h) % 4]
                                e.tensor_scalar(cadd[:, 0:128], pA[:, 0:128],
                                                rn64[:, 4 * h:4 * h + 1], None,
                                                ALU.mult)
                                e.scalar_tensor_tensor(cadd[:, 128:256],
                                                       pA[:, 128:256],
                                                       rn64[:, 4 * h + 1:4 * h + 2],
                                                       cadd[:, 0:128],
                                                       ALU.mult, ALU.add)
                                e.scalar_tensor_tensor(cadd[:, 0:128],
                                                       pA[:, 256:384],
                                                       rn64[:, 4 * h + 2:4 * h + 3],
                                                       cadd[:, 128:256],
                                                       ALU.mult, ALU.add)
                                e.scalar_tensor_tensor(
                                    comb16[tt % 2][:, h * 128:(h + 1) * 128],
                                    pA[:, 384:512],
                                    rn64[:, 4 * h + 3:4 * h + 4],
                                    cadd[:, 0:128], ALU.mult, ALU.add
                                    ).then_inc(cV.h, 1)
                            e.drain().then_inc(cV.h, 1)
                    PQ.add("vector", _p2b_v)

                    def _p2b_s(e):
                        for tt in range(NTT):
                            for g in range(4):
                                e.wait_ge(cT.h, t_ct[(tt, g)])
                                bank = pb[6 + ((tt * 4 + g) % 2)]
                                dv = combT[:].rearrange("p (c t) -> p c t", t=TOK)
                                e.activation(
                                    dv[:, 4 * g:4 * g + 4, tt * 128:(tt + 1) * 128],
                                    bank[:, 0:512], AF.Identity
                                    ).then_inc(cS.h, 1)
                        e.drain().then_inc(cS.h, 1)
                    PQ.add("scalar", _p2b_s)

                    dbg_dump([("dbg_combT", combT[:])])
                    barrier()
                    d_wo = [0] * NFC
                    d_xt = [0] * NFC
                    t_wo = [0] * NFC
                    v_wo = [0] * NFC
                    d_park = [0] * NFC
                    for o in range(NFC):
                        dG.n += 16
                        d_wo[o] = dG.n
                        dS.n += 16
                        d_xt[o] = dS.n
                        dS.n += 16
                        d_park[o] = dS.n
                        t_wo[o] = inc(cT)
                        v_wo[o] = inc(cV)

                    def _p2c_gp(e):
                        for o in range(NFC):
                            if o >= 2:
                                e.wait_ge(cT.h, t_wo[o - 2])
                            e.dma_start(
                                out=wo_s[o % 2][:].rearrange("p (c o2) -> p c o2",
                                                             o2=128),
                                in_=dwo_c[:, :, o * 128:(o + 1) * 128]
                            ).then_inc(dG.h, 16)
                    PQ.add("gpsimd", _p2c_gp)

                    def _p2c_sync(e):
                        for o in range(NFC):
                            if o >= 2:
                                e.wait_ge(cV.h, v_wo[o - 2])
                            e.dma_start(out=xt_b[o % 2][:],
                                        in_=dxT[o * 128:(o + 1) * 128, :]
                                        ).then_inc(dS.h, 16)
                            e.wait_ge(cV.h, v_wo[o])
                            e.dma_start(out=dout[o * 128:(o + 1) * 128, :],
                                        in_=ot_b[o % 2][:]).then_inc(dS.h, 16)
                    PQ.add("sync", _p2c_sync)

                    def _p2c_t(e):
                        for o in range(NFC):
                            e.wait_ge(dG.h, d_wo[o])
                            if o >= 2:
                                e.wait_ge(cV.h, v_wo[o - 2])
                            pa, pbk = pb[2 * (o % 2)], pb[2 * (o % 2) + 1]
                            for k in range(NFC):
                                lhsT = wo_s[o % 2][:, k * 128:(k + 1) * 128]
                                e.matmul(pa[:, 0:512], lhsT,
                                         combT[:, k * TOK:k * TOK + 512],
                                         start=(k == 0), stop=(k == NFC - 1))
                                e.matmul(pbk[:, 0:512], lhsT,
                                         combT[:, k * TOK + 512:(k + 1) * TOK],
                                         start=(k == 0), stop=(k == NFC - 1))
                            e.drain().then_inc(cT.h, 1)
                    PQ.add("tensor", _p2c_t)

                    def _p2c_v(e):
                        for o in range(NFC):
                            e.wait_ge(cT.h, t_wo[o])
                            e.wait_ge(dS.h, d_xt[o])
                            if o >= 2:
                                e.wait_ge(dS.h, d_park[o - 2])
                            pa, pbk = pb[2 * (o % 2)], pb[2 * (o % 2) + 1]
                            e.tensor_tensor(ot_b[o % 2][:, 0:512],
                                            xt_b[o % 2][:, 0:512],
                                            pa[:, 0:512], ALU.add)
                            e.tensor_tensor(ot_b[o % 2][:, 512:1024],
                                            xt_b[o % 2][:, 512:1024],
                                            pbk[:, 0:512], ALU.add)
                            e.drain().then_inc(cV.h, 1)
                    PQ.add("vector", _p2c_v)

            L3c = ExitStack()
            with L3c:
                wsraw = [L3c.enter_context(nc.sbuf_tensor("wsraw%d" % i,
                                                          [128, 2816], F32))
                         for i in range(2)]
                barrier()
                mats = [(dw1, NFC, FF), (dw2, NFC, FF), (dw3, NKC, HID)]
                strips_all = []
                for (dmat, nchunks, width) in mats:
                    st = []
                    for c in range(nchunks):
                        if width == FF:
                            st.append((c, 0, 2816))
                            st.append((c, 2816, 2816))
                        else:
                            st.append((c, 0, 2048))
                    strips_all.append(st)
                d_ws = {}
                v_ws = {}
                t_sum = [0] * 3
                v_sw1 = [0] * 3
                t_bc = [0] * 3
                v_sw2 = [0] * 3
                flat = []
                for m in range(3):
                    for j in range(len(strips_all[m])):
                        flat.append((m, j))
                for idx, (m, j) in enumerate(flat):
                    dS.n += 16
                    d_ws[(m, j)] = dS.n
                for m in range(3):
                    for j in range(len(strips_all[m])):
                        v_ws[(m, j)] = inc(cV)
                    v_sw1[m] = inc(cV)
                    v_sw2[m] = inc(cV)
                for m in range(3):
                    t_sum[m] = inc(cT)
                    t_bc[m] = inc(cT)

                def _p3_sync(e):
                    for idx, (m, j) in enumerate(flat):
                        if idx >= 2:
                            pm, pj = flat[idx - 2]
                            e.wait_ge(cV.h, v_ws[(pm, pj)])
                        c, off, wdt = strips_all[m][j]
                        e.dma_start(out=wsraw[idx % 2][:, 0:wdt],
                                    in_=mats[m][0][c * 128:(c + 1) * 128,
                                                   off:off + wdt]
                                    ).then_inc(dS.h, 16)
                PQ.add("sync", _p3_sync)

                def _p3_v(e):
                    idx = 0
                    for m in range(3):
                        acc = 0
                        e.memset(wsrun[:, 0:1], 0.0)
                        e.memset(wsrun[:, 3:4], 0.0)
                        for j in range(len(strips_all[m])):
                            wdt = strips_all[m][j][2]
                            nxt = 3 - acc
                            e.wait_ge(dS.h, d_ws[(m, j)])
                            e.reduce_sum(wsrun[:, 1:2], wsraw[idx % 2][:, 0:wdt],
                                         axis=AX.X, apply_absolute_value=True)
                            e.drain()
                            ins = e.tensor_tensor(wsrun[:, nxt:nxt + 1],
                                                  wsrun[:, acc:acc + 1],
                                                  wsrun[:, 1:2], ALU.add)
                            if j == len(strips_all[m]) - 1:
                                e.drain().then_inc(cV.h, 1)
                            else:
                                ins.then_inc(cV.h, 1)
                            acc = nxt
                            idx += 1
                        e.wait_ge(cT.h, t_sum[m])
                        e.tensor_scalar_mul(sc11[0:1, 0:1], pb[5][0:1, 0:1],
                                            1.0 / WMEAN_N)
                        e.drain()
                        e.tensor_scalar_max(sc11[0:1, 1:2], sc11[0:1, 0:1], 1e-5)
                        e.drain()
                        e.reciprocal(sc11[0:1, 2:3], sc11[0:1, 1:2])
                        e.drain().then_inc(cV.h, 1)
                        e.wait_ge(cT.h, t_bc[m])
                        e.tensor_copy(swcol[:, 3 * m:3 * m + 1], pb[5][:, 100:101])
                        e.tensor_scalar_mul(swcol[:, 3 * m + 1:3 * m + 2],
                                            pb[5][:, 100:101], -1.0)
                        e.tensor_copy(swcol[:, 3 * m + 2:3 * m + 3],
                                      pb[5][:, 101:102])
                        e.tensor_scalar_mul(swcol[:, 9 + 2 * m:10 + 2 * m],
                                            pb[5][:, 101:102], 0.5)
                        e.tensor_scalar_mul(swcol[:, 10 + 2 * m:11 + 2 * m],
                                            pb[5][:, 101:102], -0.5)
                        e.drain()
                        if m < 2:
                            e.tensor_scalar(stf[:, 56 + 8 * m:64 + 8 * m],
                                            stf[:, 24:32],
                                            swcol[:, 3 * m + 2:3 * m + 3], None,
                                            ALU.mult)
                        e.drain().then_inc(cV.h, 1)
                PQ.add("vector", _p3_v)

                def _p3_t(e):
                    for m in range(3):
                        nst = len(strips_all[m])
                        acc_final = 0
                        for _ in range(nst):
                            acc_final = 3 - acc_final
                        e.wait_ge(cV.h, v_ws[(m, nst - 1)])
                        e.matmul(pb[5][0:1, 0:1],
                                 wsrun[:, acc_final:acc_final + 1], onescol[:],
                                 start=True, stop=True)
                        e.drain().then_inc(cT.h, 1)
                        e.wait_ge(cV.h, v_sw1[m])
                        e.matmul(pb[5][:, 100:101], onesr[:], sc11[0:1, 2:3],
                                 start=True, stop=True)
                        e.matmul(pb[5][:, 101:102], onesr[:], sc11[0:1, 1:2],
                                 start=True, stop=True)
                        e.drain().then_inc(cT.h, 1)
                PQ.add("tensor", _p3_t)

            L4 = ExitStack()
            with L4:
                hTa = L4.enter_context(nc.sbuf_tensor("hTa", [128, 16 * TOK], F16))
                hTb = L4.enter_context(nc.sbuf_tensor("hTb", [128, 28 * TOK], F16))

                def hT(k):
                    if k < 16:
                        return hTa[:, k * TOK:(k + 1) * TOK]
                    return hTb[:, (k - 16) * TOK:(k - 15) * TOK]

                def hTpair(kk, tt):
                    base = hTa if kk < 16 else hTb
                    k0 = kk if kk < 16 else kk - 16
                    dv = base[:].rearrange("p (k t) -> p k t", t=TOK)
                    return dv[:, k0:k0 + 2, tt * 128:(tt + 1) * 128]

                L5 = ExitStack()
                with L5:
                    sb5 = lambda *a, **k: L5.enter_context(nc.sbuf_tensor(*a, **k))
                    t1s = [sb5("t1s%d" % i, [128, NFC * OS], BF16) for i in range(2)]
                    t2s = [sb5("t2s%d" % i, [128, NFC * OS], BF16) for i in range(2)]
                    raw1 = [sb5("raw1%d" % i, [128, 1024], F32) for i in range(2)]
                    raw2 = [sb5("raw2%d" % i, [128, 1024], F32) for i in range(2)]
                    qa = sb5("qa", [128, 512], BF16)
                    qb = sb5("qb", [128, 512], BF16)
                    pby1 = [pb[0], pb[1], pb[4]]
                    pby2 = [pb[2], pb[3], pb[5]]
                    tmp1 = [sb5("tmp1%d" % i, [128, OS], F32) for i in range(4)]
                    tmp2 = [sb5("tmp2%d" % i, [128, OS], F32) for i in range(4)]
                    hsl = [sb5("hsl%d" % i, [128, OS], F16) for i in range(4)]

                    barrier()
                    d_r1 = {}
                    d_r2 = {}
                    v_q = {}
                    t_y = {}
                    s_y = {}
                    v_h = {}
                    t_h = {}
                    s_h = {}
                    for o in range(NOS):
                        for dg in range(4):
                            dS.n += 16
                            d_r1[o * 4 + dg] = dS.n
                            dS2.n += 16
                            d_r2[o * 4 + dg] = dS2.n
                    for o in range(NOS + 1):
                        if o < NOS:
                            for g in range(8):
                                v_q[o * 8 + g] = inc(cV)
                        if o >= 1:
                            for tt in range(NTT):
                                v_h[(o - 1, tt)] = inc(cV)
                    for step in range(NOS * NTT):
                        o, tt = divmod(step, NTT)
                        t_y[(o, tt)] = inc(cT)
                        if step >= 3:
                            po, ptt = divmod(step - 3, NTT)
                            t_h[(po, ptt)] = inc(cT)
                    for ps in range(NOS * NTT - 3, NOS * NTT):
                        t_h[(ps // 8, ps % 8)] = inc(cT)
                    for step in range(NOS * NTT):
                        s_y[(step // 8, step % 8)] = inc(cS)
                        if step >= 2:
                            s_h[((step - 2) // 8, (step - 2) % 8)] = inc(cS)
                    for ps in range(NOS * NTT - 2, NOS * NTT):
                        s_h[(ps // 8, ps % 8)] = inc(cS)
                    s_p4_done = inc(cS)

                    def _p4_sync(e):
                        for o in range(NOS):
                            for dg in range(4):
                                dgidx = o * 4 + dg
                                if dgidx >= 2:
                                    e.wait_ge(cV.h, v_q[(dgidx - 2) * 2 + 1])
                                e.dma_start(
                                    out=raw1[dgidx % 2][:].rearrange(
                                        "p (c f) -> p c f", f=OS),
                                    in_=dw1_c[:, 4 * dg:4 * dg + 4,
                                              o * OS:(o + 1) * OS]
                                ).then_inc(dS.h, 16)
                    PQ.add("sync", _p4_sync)

                    def _p4_gp(e):
                        for o in range(NOS):
                            for dg in range(4):
                                dgidx = o * 4 + dg
                                if dgidx >= 2:
                                    e.wait_ge(cV.h, v_q[(dgidx - 2) * 2 + 1])
                                e.dma_start(
                                    out=raw2[dgidx % 2][:].rearrange(
                                        "p (c f) -> p c f", f=OS),
                                    in_=dw2_c[:, 4 * dg:4 * dg + 4,
                                              o * OS:(o + 1) * OS]
                                ).then_inc(dS2.h, 16)
                    PQ.add("gpsimd", _p4_gp)

                    def _p4_s(e):
                        nstep = NOS * NTT

                        def hTev(ps):
                            po, ptt = divmod(ps, NTT)
                            e.wait_ge(cT.h, t_h[(po, ptt)] +
                                      (1 if ps < nstep - 2 else 0))
                            tco = 256 * ((ps // 2) % 2)
                            bank = pb[6 + (ps % 2)]
                            e.activation(hTpair(2 * po, ptt),
                                         bank[:, tco:tco + 256],
                                         AF.Identity).then_inc(cS.h, 1)

                        for step in range(nstep):
                            o, tt = divmod(step, NTT)
                            if step < nstep - 1:
                                nx = step + 1
                                e.wait_ge(cT.h, t_y[(nx // 8, nx % 8)])
                            else:
                                e.wait_ge(cT.h, t_y[(o, tt)])
                            if step >= 4:
                                prev = step - 4
                                e.wait_ge(cV.h, v_h[(prev // 8, prev % 8)])
                            pY1 = pby1[step % 3]
                            pY2 = pby2[step % 3]
                            e.activation(tmp1[step % 4][:], pY1[:, 0:OS],
                                         AF.Silu, scale=stf[:, 56 + tt:57 + tt])
                            e.activation(tmp2[step % 4][:], pY2[:, 0:OS],
                                         AF.Identity,
                                         scale=stf[:, 64 + tt:65 + tt])
                            e.drain().then_inc(cS.h, 1)
                            if step >= 2:
                                hTev(step - 2)
                        for ps in range(nstep - 2, nstep):
                            hTev(ps)
                        e.drain().then_inc(cS.h, 1)
                    PQ.add("scalar", _p4_s)

                    def _p4_v(e):
                        for o in range(NOS + 1):
                            if o < NOS:
                                for g in range(8):
                                    gidx = o * 8 + g
                                    dgidx = o * 4 + g // 2
                                    if g % 2 == 0:
                                        e.wait_ge(dS.h, d_r1[dgidx])
                                        e.wait_ge(dS2.h, d_r2[dgidx])
                                    if o >= 2 and g == 0:
                                        e.wait_ge(cT.h, t_y[(o - 2, NTT - 1)])
                                    half = (g % 2) * 512
                                    for mat, rawb, tds, thrc in (
                                            (0, raw1[dgidx % 2], t1s[o % 2], 9),
                                            (1, raw2[dgidx % 2], t2s[o % 2], 11)):
                                        raw = rawb[:, half:half + 512]
                                        e.tensor_scalar(
                                            qa[:], raw,
                                            swcol[:, thrc + 1:thrc + 2],
                                            None, ALU.is_lt)
                                        ins = e.scalar_tensor_tensor(
                                            tds[:, 2 * g * OS:(2 * g + 2) * OS],
                                            raw, swcol[:, thrc:thrc + 1],
                                            qa[:], ALU.is_gt, ALU.subtract)
                                    if g == 7:
                                        e.drain().then_inc(cV.h, 1)
                                    else:
                                        ins.then_inc(cV.h, 1)
                            if o >= 1:
                                oo = o - 1
                                for tt in range(NTT):
                                    step = oo * 8 + tt
                                    e.wait_ge(cS.h, s_y[(oo, tt)])
                                    if step >= 4:
                                        prev = step - 4
                                        e.wait_ge(cS.h, s_h[(prev // 8, prev % 8)])
                                    e.tensor_tensor(hsl[step % 4][:],
                                                    tmp1[step % 4][:],
                                                    tmp2[step % 4][:], ALU.mult)
                                    e.reduce_max(
                                        hmx[:, tt * 22 + oo:tt * 22 + oo + 1],
                                        hsl[step % 4][:], axis=AX.X,
                                        apply_absolute_value=True)
                                    e.drain().then_inc(cV.h, 1)
                    PQ.add("vector", _p4_v)

                    def _p4_t(e):
                        nstep = NOS * NTT

                        def trans(ps):
                            po, ptt = divmod(ps, NTT)
                            e.wait_ge(cV.h, v_h[(po, ptt)])
                            if ps >= 2:
                                p2 = ps - 2
                                e.wait_ge(cS.h, s_h[(p2 // 8, p2 % 8)])
                            tco = 256 * ((ps // 2) % 2)
                            bank = pb[6 + (ps % 2)]
                            for j in range(2):
                                ins = e.matmul(
                                    bank[:, tco + j * 128:tco + (j + 1) * 128],
                                    hsl[ps % 4][:, j * 128:(j + 1) * 128],
                                    ident16[:], start=True, stop=True)
                            if ps >= nstep - 3:
                                e.drain().then_inc(cT.h, 1)
                            else:
                                ins.then_inc(cT.h, 1)

                        for step in range(nstep):
                            o, tt = divmod(step, NTT)
                            e.wait_ge(cV.h, v_q[o * 8 + 7])
                            if step >= 3:
                                prev = step - 3
                                e.wait_ge(cS.h, s_y[(prev // 8, prev % 8)])
                            pY1 = pby1[step % 3]
                            pY2 = pby2[step % 3]
                            for i in range(NFC):
                                lhsT = rqT[:, i * TOK + tt * 128:
                                           i * TOK + (tt + 1) * 128]
                                e.matmul(pY1[:, 0:OS], lhsT,
                                         t1s[o % 2][:, i * OS:(i + 1) * OS],
                                         start=(i == 0), stop=(i == NFC - 1))
                                ins = e.matmul(pY2[:, 0:OS], lhsT,
                                               t2s[o % 2][:, i * OS:(i + 1) * OS],
                                               start=(i == 0), stop=(i == NFC - 1))
                            if step == nstep - 1:
                                e.drain().then_inc(cT.h, 1)
                            else:
                                ins.then_inc(cT.h, 1)
                            if step >= 3:
                                trans(step - 3)
                        for ps in range(nstep - 3, nstep):
                            trans(ps)
                    PQ.add("tensor", _p4_t)

                dbg_dump([("dbg_hTa", hTa[:]), ("dbg_hTb", hTb[:]),
                          ("dbg_stf2", stf[:]), ("dbg_sw", swcol[:])])
                L6 = ExitStack()
                with L6:
                    sb6 = lambda *a, **k: L6.enter_context(nc.sbuf_tensor(*a, **k))
                    p56 = sb6("p56", [128, 5632], F32)
                    lad6 = sb6("lad6", [128, 2048], F32)
                    xt6 = [sb6("xt6_%d" % i, [128, 512], F32) for i in range(2)]
                    ev6 = sb6("ev6", [128, 512], F32)
                    fin6 = sb6("fin6", [128, 512], F32)
                    qa3 = sb6("qa3", [128, 2816], F16)
                    srow = sb6("srow", [1, 2048], F32)
                    s3r_b = sb6("s3r_b", [128, 1024], F32)
                    qs2r_b = sb6("qs2r_b", [128, 1024], F32)
                    t3s = [rqT[:, i * FF:(i + 1) * FF].bitcast(F16)
                           for i in range(2)]

                    def lad_chunk(e, k, c0, scr, bare=True):
                        u = scr[:, 0:512]
                        y0 = scr[:, 512:1024]
                        w1_ = scr[:, 1024:1536]
                        w2_ = scr[:, 1536:2048]
                        hk = hT(k)[:, c0:c0 + 512]
                        e.tensor_tensor(w1_[:], hk, qs2r_b[:, c0:c0 + 512],
                                        ALU.mult)
                        e.tensor_single_scalar(y0[:], w1_[:], 128.5, ALU.add)
                        srk = y0
                        dsts = [w1_, w2_]
                        for wi, P in enumerate(LADDER):
                            dst = dsts[wi % 2]
                            e.tensor_scalar(u[:], srk[:], P, -P,
                                            ALU.is_gt, ALU.mult)
                            e.tensor_tensor(dst[:], srk[:], u[:], ALU.add)
                            srk = dst
                        ins = e.scalar_tensor_tensor(hk, y0[:], 128.0, srk[:],
                                                     ALU.subtract, ALU.subtract)
                        if bare:
                            ins.then_inc(cV.h, 1)
                        else:
                            e.drain().then_inc(cV.h, 1)

                    barrier()
                    v_qs2 = inc(cV)
                    dG.n += 16 * 16
                    d_srow = dG.n
                    t_row = inc(cT)
                    v_rows = inc(cV)
                    for k in range(NKC):
                        inc(cV)
                    v_p5_done = inc(cV)
                    d_t3 = {}
                    dS.n += 16
                    d_t3[(0, 0, 0)] = dS.n
                    dS2.n += 16
                    d_t3[(0, 0, 1)] = dS2.n

                    def _p5_v(e):
                        for tt in range(NTT):
                            e.reduce_max(stf[:, 40 + tt:41 + tt],
                                         hmx[:, tt * 22:(tt + 1) * 22],
                                         axis=AX.X, apply_absolute_value=True)
                        e.drain()
                        e.tensor_scalar_max(stf[:, 88:96], stf[:, 40:48], 1e-5)
                        e.drain()
                        e.reciprocal(stf[:, 72:80], stf[:, 88:96])
                        e.drain()
                        e.tensor_scalar_mul(stf[:, 48:56], stf[:, 72:80], 127.0)
                        e.drain()
                        e.reciprocal(stf[:, 72:80], stf[:, 48:56])
                        e.drain()
                        e.tensor_scalar(stf[:, 80:88], stf[:, 72:80],
                                        swcol[:, 8:9], None, ALU.mult)
                        e.drain().then_inc(cV.h, 1)
                        e.wait_ge(cT.h, t_row)
                        e.tensor_copy(qs2r_b[:, 0:512], pb[4][:, 0:512])
                        e.tensor_copy(qs2r_b[:, 512:1024], pb[5][:, 0:512])
                        e.tensor_copy(s3r_b[:, 0:512], pb[6][:, 0:512])
                        e.tensor_copy(s3r_b[:, 512:1024], pb[7][:, 0:512])
                        e.drain().then_inc(cV.h, 1)
                        for k in range(NKC):
                            lad_chunk(e, k, 0, lad6)
                        e.drain().then_inc(cV.h, 1)
                    PQ.add("vector", _p5_v)

                    def _p5_gp(e):
                        e.dma_start(
                            out=p56[:, 2816:5632].rearrange(
                                "p (c f) -> p c f", f=128),
                            in_=dw3_c[:, 22:44, 0:128]).then_inc(dS2.h, 16)
                        e.wait_ge(cV.h, v_qs2)
                        for tt in range(NTT):
                            e.dma_start(out=srow[0:1, tt * 128:(tt + 1) * 128],
                                        in_=stf[:, 48 + tt:49 + tt],
                                        allow_slow_non_contiguous=True
                                        ).then_inc(dG.h, 16)
                            e.dma_start(out=srow[0:1, 1024 + tt * 128:
                                                1152 + tt * 128],
                                        in_=stf[:, 80 + tt:81 + tt],
                                        allow_slow_non_contiguous=True
                                        ).then_inc(dG.h, 16)
                    PQ.add("gpsimd", _p5_gp)

                    def _p5_sync(e):
                        e.dma_start(
                            out=p56[:, 0:2816].rearrange(
                                "p (c f) -> p c f", f=128),
                            in_=dw3_c[:, 0:22, 0:128]).then_inc(dS.h, 16)
                    PQ.add("sync", _p5_sync)

                    def _p5_t(e):
                        e.wait_ge(dG.h, d_srow)
                        e.matmul(pb[4][:, 0:512], onesr[:], srow[0:1, 0:512],
                                 start=True, stop=True)
                        e.matmul(pb[5][:, 0:512], onesr[:], srow[0:1, 512:1024],
                                 start=True, stop=True)
                        e.matmul(pb[6][:, 0:512], onesr[:], srow[0:1, 1024:1536],
                                 start=True, stop=True)
                        e.matmul(pb[7][:, 0:512], onesr[:], srow[0:1, 1536:2048],
                                 start=True, stop=True)
                        e.drain().then_inc(cT.h, 1)
                    PQ.add("tensor", _p5_t)

                    dbg_dump([("dbg_q2a", hTa[:]), ("dbg_q2b", hTb[:]),
                              ("dbg_qs2r", qs2r_b[:]), ("dbg_s3r", s3r_b[:])])
                    barrier()
                    sched = [3] * 12 + [2] * 4
                    slot_ks = []
                    kk = 0
                    for o3 in range(NFC):
                        slot_ks.append(list(range(kk, kk + sched[o3])))
                        kk += sched[o3]
                    assert kk == NKC
                    d_x6 = {}
                    d_fst = {}
                    v_q3 = {}
                    v_ladB = {}
                    v_f = {}
                    t_y3 = {}
                    for p in range(2):
                        for o3 in range(NFC):
                            if (p, o3) == (0, 0):
                                continue
                            dS.n += 16
                            d_t3[(p, o3, 0)] = dS.n
                            dS2.n += 16
                            d_t3[(p, o3, 1)] = dS2.n
                    for p in range(2):
                        for o3 in range(NFC):
                            dA.n += 16
                            d_x6[(p, o3)] = dA.n
                            dA.n += 16
                            d_fst[(p, o3)] = dA.n
                    n_out = d_fst[(1, NFC - 1)]
                    for p in range(2):
                        for o3 in range(NFC):
                            idx = p * NFC + o3
                            if idx == 0:
                                for j in range(2):
                                    v_q3[(0, 0, j)] = inc(cV)
                            if p == 0:
                                for k in slot_ks[o3]:
                                    v_ladB[k] = inc(cV)
                            if idx + 1 < 2 * NFC:
                                np_, no3 = (idx + 1) // NFC, (idx + 1) % NFC
                                for j in range(2):
                                    v_q3[(np_, no3, j)] = inc(cV)
                            v_f[(p, o3)] = inc(cV)
                    for p in range(2):
                        for o3 in range(NFC):
                            t_y3[(p, o3)] = inc(cT)

                    def skey(n):
                        return (n // 32, (n % 32) // 2, n % 2)

                    def _p6_sync(e):
                        for idx in range(1, 2 * NFC):
                            p, o3 = idx // NFC, idx % NFC
                            e.wait_ge(cV.h, v_q3[(( idx - 1) // NFC,
                                                  (idx - 1) % NFC, 0)])
                            e.dma_start(
                                out=p56[:, 0:2816].rearrange(
                                    "p (c f) -> p c f", f=128),
                                in_=dw3_c[:, 0:22,
                                          o3 * 128:(o3 + 1) * 128]
                            ).then_inc(dS.h, 16)
                    PQ.add("sync", _p6_sync)

                    def _p6_s(e):
                        for idx in range(2 * NFC):
                            p, o3 = idx // NFC, idx % NFC
                            if idx >= 2:
                                pi = idx - 2
                                e.wait_ge(cV.h, v_f[(pi // NFC, pi % NFC)])
                            e.dma_start(
                                out=xt6[idx % 2][:],
                                in_=dout[o3 * 128:(o3 + 1) * 128,
                                         p * 512:(p + 1) * 512]
                            ).then_inc(dA.h, 16)
                            e.wait_ge(cV.h, v_f[(p, o3)])
                            e.dma_start(
                                out=dout[o3 * 128:(o3 + 1) * 128,
                                         p * 512:(p + 1) * 512],
                                in_=fin6[:]).then_inc(dA.h, 16)
                        e.wait_ge(dA.h, n_out)
                    PQ.add("scalar", _p6_s)

                    def _p6_gp(e):
                        for idx in range(1, 2 * NFC):
                            p, o3 = idx // NFC, idx % NFC
                            e.wait_ge(cV.h, v_q3[((idx - 1) // NFC,
                                                  (idx - 1) % NFC, 1)])
                            e.dma_start(
                                out=p56[:, 2816:5632].rearrange(
                                    "p (c f) -> p c f", f=128),
                                in_=dw3_c[:, 22:44,
                                          o3 * 128:(o3 + 1) * 128]
                            ).then_inc(dS2.h, 16)
                    PQ.add("gpsimd", _p6_gp)

                    def _p6_v(e):
                        def quant(idx):
                            p, o3 = idx // NFC, idx % NFC
                            for j in range(2):
                                e.wait_ge(dS.h if j == 0 else dS2.h,
                                          d_t3[(p, o3, j)])
                                if j == 0 and idx >= 2:
                                    pi = idx - 2
                                    e.wait_ge(cT.h, t_y3[(pi // NFC,
                                                          pi % NFC)])
                                raw = p56[:, j * 2816:(j + 1) * 2816]
                                dst = t3s[idx % 2][:, j * 2816:(j + 1) * 2816]
                                e.tensor_scalar(qa3[:], raw, swcol[:, 14:15],
                                                None, ALU.is_lt)
                                ins = e.scalar_tensor_tensor(
                                    dst, raw, swcol[:, 13:14],
                                    qa3[:], ALU.is_gt, ALU.subtract)
                                if j == 0:
                                    ins.then_inc(cV.h, 1)
                                else:
                                    e.drain().then_inc(cV.h, 1)

                        for p in range(2):
                            for o3 in range(NFC):
                                idx = p * NFC + o3
                                if idx == 0:
                                    quant(0)
                                if p == 0:
                                    for ki, k in enumerate(slot_ks[o3]):
                                        last = (o3 == NFC - 1 and
                                                ki == len(slot_ks[o3]) - 1)
                                        lad_chunk(e, k, 512, lad6,
                                                  bare=not last)
                                if idx + 1 < 2 * NFC:
                                    quant(idx + 1)
                                e.wait_ge(cT.h, t_y3[(p, o3)])
                                e.wait_ge(dA.h, d_x6[(p, o3)])
                                if idx >= 1:
                                    pidx = idx - 1
                                    e.wait_ge(dA.h, d_fst[(pidx // NFC,
                                                           pidx % NFC)])
                                bank = pb[idx % 4]
                                e.tensor_tensor(ev6[:], bank[:, 0:512],
                                                s3r_b[:, p * 512:(p + 1) * 512],
                                                ALU.mult)
                                e.tensor_tensor(fin6[:], xt6[idx % 2][:],
                                                ev6[:], ALU.add)
                                e.drain().then_inc(cV.h, 1)
                    PQ.add("vector", _p6_v)

                    def _p6_t(e):
                        v_ladB_last = max(v_ladB.values())
                        for p in range(2):
                            for o3 in range(NFC):
                                idx = p * NFC + o3
                                e.wait_ge(cV.h, v_q3[(p, o3, 1)])
                                if p == 1 and o3 == 0:
                                    e.wait_ge(cV.h, v_ladB_last)
                                if idx >= 4:
                                    pidx = idx - 4
                                    e.wait_ge(cV.h, v_f[(pidx // NFC,
                                                         pidx % NFC)])
                                bank = pb[idx % 4]
                                for k in range(NKC):
                                    lhsT = t3s[idx % 2][:, k * 128:(k + 1) * 128]
                                    hk = hT(k)[:, p * 512:(p + 1) * 512]
                                    e.matmul(bank[:, 0:512], lhsT, hk,
                                             start=(k == 0), stop=(k == NKC - 1))
                                e.drain().then_inc(cT.h, 1)
                    PQ.add("tensor", _p6_t)

                    PQ.flush()
    return nc


def kernel(L0_attn_hidden_states, L0_attn_Wq, L0_attn_Wk, L0_attn_Wv, L0_attn_Wo,
           L0_attn_bank_weights, L0_attn_memories, L0_attn_memory_norms,
           L0_ffn_w1, L0_ffn_w2, L0_ffn_w3):
    f32 = np.float32
    x = np.ascontiguousarray(np.asarray(L0_attn_hidden_states, f32).reshape(8192, HID))
    wqT = np.ascontiguousarray(np.asarray(L0_attn_Wq, f32).T)
    woT = np.ascontiguousarray(np.asarray(L0_attn_Wo, f32).T)
    w1T = np.ascontiguousarray(np.asarray(L0_ffn_w1, f32).T)
    w2T = np.ascontiguousarray(np.asarray(L0_ffn_w2, f32).T)
    w3T = np.ascontiguousarray(np.asarray(L0_ffn_w3, f32).T)
    bw = np.ascontiguousarray(np.asarray(L0_attn_bank_weights, f32))
    mem = np.asarray(L0_attn_memories, f32)
    mn = np.asarray(L0_attn_memory_norms, f32)
    mpm = np.empty((NH, 128, 512), f32)
    mpn = np.empty((128, 64), f32)
    for h in range(NH):
        mpm[h] = np.concatenate([mem[0, h], mem[1, h], mem[2, h], mem[3, h]],
                                axis=1)
        for n in range(4):
            mpn[:, 4 * h + n] = mn[n, h]

    nc = build()
    in_maps = []
    for c in range(8):
        xs = x[c * TOK:(c + 1) * TOK]
        in_maps.append(dict(x_tok=np.ascontiguousarray(xs),
                            xT=np.ascontiguousarray(xs.T),
                            wqT=wqT, woT=woT, mpm=mpm, mpn=mpn, bw=bw,
                            w1T=w1T, w2T=w2T, w3T=w3T))
    res = run_bass_kernel_spmd(nc, in_maps, core_ids=list(range(8)))
    out = np.empty((8192, HID), f32)
    for c in range(8):
        out[c * TOK:(c + 1) * TOK] = np.asarray(res.results[c]["out"]).T
    return out.reshape(2, 4096, HID)


# version 23
# speedup vs baseline: 1.0230x; 1.0230x over previous
import numpy as np
import concourse.bass as bass
import concourse.mybir as mybir
from concourse.bass_utils import run_bass_kernel_spmd
from contextlib import ExitStack

F32 = mybir.dt.float32
BF16 = mybir.dt.bfloat16
F16 = mybir.dt.float16
AF = mybir.ActivationFunctionType
ALU = mybir.AluOpType
AX = mybir.AxisListType

TOK = 1024
NTT = 8
HID = 2048
NFC = 16
FF = 5632
NKC = 44
NH = 16
OS = 256
NOS = FF // OS
WMEAN_N = float(FF * HID)
LADDER = [128.0, 64.0, 32.0, 16.0, 8.0, 4.0, 2.0, 1.0]


class Ctr:
    def __init__(self, stack, nc, name):
        self.h = stack.enter_context(nc.semaphore(name))
        self.n = 0


class Q:

    def __init__(self, block):
        self.block = block
        self.qs = {k: [] for k in ("sync", "gpsimd", "vector", "scalar", "tensor")}

    def add(self, eng, fn):
        self.qs[eng].append(fn)

    def flush(self):
        for eng, cmds in self.qs.items():
            if not cmds:
                continue
            snapshot = list(cmds)
            cmds.clear()

            def body(e, snapshot=snapshot):
                for c in snapshot:
                    c(e)
            getattr(self.block, eng)(body)


def build(debug=False):
    nc = bass.Bass("TRN2", target_bir_lowering=False,
                   detect_race_conditions=False)
    dx = nc.dram_tensor("x_tok", [TOK, HID], F32, kind="ExternalInput").ap()
    dxT = nc.dram_tensor("xT", [HID, TOK], F32, kind="ExternalInput").ap()
    dwq = nc.dram_tensor("wqT", [HID, HID], F32, kind="ExternalInput").ap()
    dwo = nc.dram_tensor("woT", [HID, HID], F32, kind="ExternalInput").ap()
    dmpm = nc.dram_tensor("mpm", [NH, 128, 512], F32, kind="ExternalInput").ap()
    dmpn = nc.dram_tensor("mpn", [128, 64], F32, kind="ExternalInput").ap()
    dbw = nc.dram_tensor("bw", [16, 4], F32, kind="ExternalInput").ap()
    dw1 = nc.dram_tensor("w1T", [HID, FF], F32, kind="ExternalInput").ap()
    dw2 = nc.dram_tensor("w2T", [HID, FF], F32, kind="ExternalInput").ap()
    dw3 = nc.dram_tensor("w3T", [FF, HID], F32, kind="ExternalInput").ap()
    dout = nc.dram_tensor("out", [HID, TOK], F32, kind="ExternalOutput").ap()

    dwq_c = dwq.rearrange("(c p) o -> p c o", p=128)
    dwo_c = dwo.rearrange("(c p) o -> p c o", p=128)
    dw1_c = dw1.rearrange("(c p) f -> p c f", p=128)
    dw2_c = dw2.rearrange("(c p) f -> p c f", p=128)
    dw3_c = dw3.rearrange("(c p) o -> p c o", p=128)
    dmpm_t = dmpm.transpose([1, 0, 2])

    dbg = {}
    if debug:
        for nm, shape, dt_ in [
                ("dbg_xnT", [128, NFC * TOK], BF16),
                ("dbg_rqT", [128, NFC * TOK], BF16),
                ("dbg_stf1", [128, 96], F32),
                ("dbg_sqT", [128, NH * TOK], BF16),
                ("dbg_combT", [128, NFC * TOK], BF16),
                ("dbg_sw", [128, 16], F32),
                ("dbg_hTa", [128, 16 * TOK], F16),
                ("dbg_hTb", [128, 28 * TOK], F16),
                ("dbg_stf2", [128, 96], F32),
                ("dbg_q2a", [128, 16 * TOK], F16),
                ("dbg_q2b", [128, 28 * TOK], F16),
                ("dbg_qs2r", [128, TOK], F32),
                ("dbg_s3r", [128, TOK], F32),
                ("dbg_wab", [128, 64], F32)]:
            dbg[nm] = nc.dram_tensor(nm, shape, dt_, kind="ExternalOutput").ap()

    SG = ExitStack()
    with SG:
        sb = lambda *a, **k: SG.enter_context(nc.sbuf_tensor(*a, **k))
        ps = lambda *a: SG.enter_context(nc.psum_tensor(*a))
        ident = sb("ident", [128, 128], BF16)
        ident16 = sb("ident16", [128, 128], F16)
        iotaf = sb("iotaf", [128, 128], F32)
        onesr = sb("onesr", [1, 128], F32)
        onescol = sb("onescol", [128, 1], F32)
        wa_b = sb("wa_b", [128, 64], F32)
        stf = sb("stf", [128, 96], F32)
        sttmp = sb("sttmp", [128, 8], F32)
        swcol = sb("swcol", [128, 16], F32)
        hmx = sb("hmx", [128, 176], F32)
        sm_t = sb("sm_t", [16, 32], F32)
        act4 = sb("act4", [128, 4], F32)
        smallrow = sb("smallrow", [1, 64], F32)
        wsrun = sb("wsrun", [128, 4], F32)
        rn64 = sb("rn64", [128, 64], F32)
        rn64b = sb("rn64b", [128, 64], F32)
        ccol = sb("ccol", [128, 2], F32)
        sc11 = sb("sc11", [1, 8], F32)
        pb = [ps("pb%d" % i, [128, 512], F32) for i in range(8)]

        dS = Ctr(SG, nc, "dS")
        dS2 = Ctr(SG, nc, "dS2")
        dG = Ctr(SG, nc, "dG")
        cV = Ctr(SG, nc, "cV")
        cS = Ctr(SG, nc, "cS")
        cT = Ctr(SG, nc, "cT")
        cG = Ctr(SG, nc, "cG")
        dA = Ctr(SG, nc, "dA")
        ALLC = [dS, dS2, dG, cV, cS, cT, cG, dA]
        block = SG.enter_context(nc.Block())
        PQ = Q(block)

        def inc(ctr, k=1):
            ctr.n += k
            return ctr.n

        def barrier():
            PQ.flush()
            snap = [(c.h, c.n) for c in ALLC if c.n > 0]

            def mk(engdec):
                @engdec
                def _(eng):
                    for h, v in snap:
                        eng.wait_ge(h, v)
            for ed in (block.sync, block.gpsimd, block.vector, block.scalar,
                       block.tensor):
                mk(ed)

        def dbg_dump(pairs):
            if not dbg:
                return
            PQ.flush()
            snap = [(c.h, c.n) for c in ALLC if c.n > 0]

            @block.sync
            def _(sync):
                for h, v in snap:
                    sync.wait_ge(h, v)
                for nm, srcap in pairs:
                    sync.dma_start(out=dbg[nm], in_=srcap).then_inc(dS.h, 16)
            dS.n += 16 * len(pairs)

        dS.n += 16
        p0_bw = dS.n
        PQ.add("sync", lambda e: e.dma_start(out=sm_t[0:16, 0:4], in_=dbw[:]
                                             ).then_inc(dS.h, 16))

        cG.n += 1
        p0_iota = cG.n
        dG.n += 16
        p0_norm = dG.n

        def _gp0(e):
            e.iota(iotaf[:], pattern=[[1, 128]], base=0, channel_multiplier=-1,
                   allow_small_or_imprecise_dtypes=True)
            e.drain().then_inc(cG.h, 1)
            e.dma_start(out=wa_b[:], in_=dmpn[:]).then_inc(dG.h, 16)
        PQ.add("gpsimd", _gp0)

        cV.n += 1
        p0_v1 = cV.n

        def _v0(e):
            e.wait_ge(cG.h, p0_iota)
            e.tensor_single_scalar(ident[:], iotaf[:], 0.0, ALU.is_equal)
            e.tensor_single_scalar(ident16[:], iotaf[:], 0.0, ALU.is_equal)
            e.memset(onesr[:], 1.0)
            e.memset(onescol[:], 1.0)
            e.memset(ccol[:, 0:1], 1e-5)
            e.memset(stf[:, 40:48], 0.0)
            e.memset(stf[:, 72:80], 0.0)
            e.wait_ge(dS.h, p0_bw)
            e.reduce_max(sm_t[0:16, 16:17], sm_t[0:16, 0:4], axis=AX.X)
            e.drain()
            e.tensor_scalar_mul(sm_t[0:16, 19:20], sm_t[0:16, 16:17], -1.0)
            e.wait_ge(dG.h, p0_norm)
            nb = wa_b[:].rearrange("p (h n) -> p h n", n=4)
            for n in range(4):
                e.reduce_sum(act4[:, n:n + 1], nb[:, :, n], axis=AX.X)
            e.drain().then_inc(cV.h, 1)
        PQ.add("vector", _v0)

        cS.n += 1
        p0_exp = cS.n

        def _s0(e):
            e.wait_ge(cV.h, p0_v1)
            e.activation(sm_t[0:16, 4:8], sm_t[0:16, 0:4], AF.Exp,
                         bias=sm_t[0:16, 19:20], scale=1.0)
            e.drain().then_inc(cS.h, 1)
        PQ.add("scalar", _s0)

        cT.n += 1
        p0_mm1 = cT.n

        def _t0(e):
            e.wait_ge(cV.h, p0_v1)
            e.matmul(pb[5][0:4, 0:1], act4[:], onescol[:], start=True, stop=True)
            e.drain().then_inc(cT.h, 1)
        PQ.add("tensor", _t0)

        cV.n += 1
        p0_v2 = cV.n

        def _v1(e):
            e.wait_ge(cS.h, p0_exp)
            e.reduce_sum(sm_t[0:16, 17:18], sm_t[0:16, 4:8], axis=AX.X)
            e.drain()
            e.reciprocal(sm_t[0:16, 18:19], sm_t[0:16, 17:18])
            e.drain()
            e.tensor_scalar(sm_t[0:16, 8:12], sm_t[0:16, 4:8],
                            sm_t[0:16, 18:19], None, ALU.mult)
            e.wait_ge(cT.h, p0_mm1)
            e.tensor_single_scalar(act4[0:4, 0:1], pb[5][0:4, 0:1], 1e-6,
                                   ALU.is_ge)
            e.drain().then_inc(cV.h, 1)
        PQ.add("vector", _v1)

        dG.n += 16
        p0_arow = dG.n

        def _gp1(e):
            e.wait_ge(cV.h, p0_v2)
            e.dma_start(out=smallrow[0:1, 0:4], in_=act4[0:4, 0:1],
                        allow_slow_non_contiguous=True).then_inc(dG.h, 16)
        PQ.add("gpsimd", _gp1)

        cT.n += 1
        p0_mm2 = cT.n

        def _t1(e):
            e.wait_ge(dG.h, p0_arow)
            e.matmul(pb[5][0:16, 4:8], onesr[0:1, 0:16], smallrow[0:1, 0:4],
                     start=True, stop=True)
            e.drain().then_inc(cT.h, 1)
        PQ.add("tensor", _t1)

        cV.n += 1
        p0_v3 = cV.n

        def _v2(e):
            e.wait_ge(cT.h, p0_mm2)
            e.tensor_tensor(sm_t[0:16, 12:16], sm_t[0:16, 8:12],
                            pb[5][0:16, 4:8], ALU.mult)
            e.drain().then_inc(cV.h, 1)
        PQ.add("vector", _v2)

        dG.n += 16
        p0_wrow = dG.n

        def _gp2(e):
            e.wait_ge(cV.h, p0_v3)
            e.dma_start(out=smallrow[0:1, 0:64], in_=sm_t[0:16, 12:16],
                        allow_slow_non_contiguous=True).then_inc(dG.h, 16)
        PQ.add("gpsimd", _gp2)

        cT.n += 1
        p0_mm3 = cT.n

        def _t2(e):
            e.wait_ge(dG.h, p0_wrow)
            e.matmul(pb[5][:, 8:72], onesr[:], smallrow[0:1, 0:64],
                     start=True, stop=True)
            e.drain().then_inc(cT.h, 1)
        PQ.add("tensor", _t2)

        cV.n += 1

        def _v3(e):
            e.wait_ge(cT.h, p0_mm3)
            e.tensor_copy(wa_b[:], pb[5][:, 8:72])
            e.drain().then_inc(cV.h, 1)
        PQ.add("vector", _v3)

        L1 = ExitStack()
        with L1:
            rqT = L1.enter_context(nc.sbuf_tensor("rqT", [128, NFC * TOK], BF16))
            L2 = ExitStack()
            with L2:
                sqT = L2.enter_context(nc.sbuf_tensor("sqT", [128, NH * TOK], BF16))
                mp16m = L2.enter_context(
                    nc.sbuf_tensor("mp16m", [128, NH * 512], BF16))
                nrm16 = L2.enter_context(nc.sbuf_tensor("nrm16", [128, 64], BF16))

                dG.n += 32
                d_mp = dG.n

                def _gpl(e):
                    e.dma_start(out=mp16m[:].rearrange("p (h c) -> p h c", c=512),
                                in_=dmpm_t[:]).then_inc(dG.h, 16)
                    e.dma_start(out=nrm16[:], in_=dmpn[:]).then_inc(dG.h, 16)
                PQ.add("gpsimd", _gpl)

                L3 = ExitStack()
                with L3:
                    sb3 = lambda *a, **k: L3.enter_context(nc.sbuf_tensor(*a, **k))
                    xa = sb3("xa", [128, HID], F32)
                    xsq = sb3("xsq", [128, HID], F32)
                    xn16 = [sb3("xn16_%d" % i, [128, HID], BF16) for i in range(2)]
                    rq16 = [sb3("rq16_%d" % i, [128, HID], BF16) for i in range(2)]
                    ub = sb3("ub", [128, HID], F32)
                    y0b = sb3("y0b", [128, HID], F32)
                    ywk = sb3("ywk", [128, HID], F32)
                    wrapc = sb3("wrapc", [128, HID], F32)
                    xnT = sb3("xnT", [128, NFC * TOK], BF16)
                    wq_s = [sb3("wq_s%d" % i, [128, HID], BF16) for i in range(2)]
                    sigt = sb3("sigt", [128, 512], F32)
                    sige = sb3("sige", [128, 512], F32)

                    barrier()
                    d_x = [0] * NTT
                    v_var = [0] * NTT
                    s_ln = [0] * NTT
                    v_neg = [0] * NTT
                    s_rstd = [0] * NTT
                    v_p1 = [0] * NTT
                    t_tr = {}
                    s_tr = {}
                    for tt in range(NTT):
                        dS.n += 16
                        d_x[tt] = dS.n
                        v_var[tt] = inc(cV)
                        v_neg[tt] = inc(cV)
                        v_p1[tt] = inc(cV)
                    for tt in range(NTT):
                        for b_ in range(8):
                            t_tr[(tt, b_)] = inc(cT)
                    for tt in range(NTT):
                        s_ln[tt] = inc(cS)
                        s_rstd[tt] = inc(cS)
                        if tt >= 1:
                            for b_ in range(8):
                                s_tr[(tt - 1, b_)] = inc(cS)
                    for b_ in range(8):
                        s_tr[(NTT - 1, b_)] = inc(cS)
                    s_p1_done = inc(cS)

                    def _p1_sync(e):
                        for tt in range(NTT):
                            if tt >= 1:
                                e.wait_ge(cV.h, v_p1[tt - 1])
                            e.dma_start(out=xa[:],
                                        in_=dx[tt * 128:(tt + 1) * 128, :]
                                        ).then_inc(dS.h, 16)
                    PQ.add("sync", _p1_sync)

                    def _p1_vec(e):
                        for tt in range(NTT):
                            e.wait_ge(dS.h, d_x[tt])
                            if tt >= 2:
                                e.wait_ge(cS.h, s_tr[(tt - 2, 7)])
                            e.tensor_tensor(xsq[:], xa[:], xa[:], ALU.mult)
                            e.reduce_sum(sttmp[:, 0:1], xa[:], axis=AX.X)
                            e.reduce_sum(sttmp[:, 1:2], xsq[:], axis=AX.X)
                            e.reduce_max(sttmp[:, 2:3], xa[:], axis=AX.X)
                            e.tensor_reduce(sttmp[:, 3:4], xa[:], AX.X, ALU.min)
                            e.drain()
                            e.tensor_scalar_mul(stf[:, 0 + tt:1 + tt],
                                                sttmp[:, 0:1], -1.0 / HID)
                            e.tensor_scalar_mul(sttmp[:, 6:7], sttmp[:, 1:2],
                                                1.0 / HID)
                            e.drain()
                            e.tensor_tensor(sttmp[:, 7:8], stf[:, 0 + tt:1 + tt],
                                            stf[:, 0 + tt:1 + tt], ALU.mult)
                            e.drain()
                            e.tensor_tensor(sttmp[:, 4:5], sttmp[:, 6:7],
                                            sttmp[:, 7:8], ALU.subtract)
                            e.drain().then_inc(cV.h, 1)
                            e.wait_ge(cS.h, s_ln[tt])
                            e.tensor_scalar_mul(sttmp[:, 4:5], sttmp[:, 5:6], -0.5)
                            e.drain().then_inc(cV.h, 1)
                            e.wait_ge(cS.h, s_rstd[tt])
                            e.tensor_tensor(sttmp[:, 6:7], sttmp[:, 2:3],
                                            stf[:, 0 + tt:1 + tt], ALU.add)
                            e.tensor_tensor(sttmp[:, 7:8], sttmp[:, 3:4],
                                            stf[:, 0 + tt:1 + tt], ALU.add)
                            e.drain()
                            e.tensor_scalar_mul(sttmp[:, 5:6], sttmp[:, 7:8], -1.0)
                            e.drain()
                            e.tensor_tensor(sttmp[:, 2:3], sttmp[:, 6:7],
                                            sttmp[:, 5:6], ALU.max)
                            e.drain()
                            e.tensor_tensor(sttmp[:, 6:7], sttmp[:, 2:3],
                                            stf[:, 8 + tt:9 + tt], ALU.mult)
                            e.drain()
                            e.tensor_scalar_max(sttmp[:, 7:8], sttmp[:, 6:7], 1e-5)
                            e.drain()
                            e.reciprocal(sttmp[:, 6:7], sttmp[:, 7:8])
                            e.drain()
                            e.tensor_scalar_mul(stf[:, 16 + tt:17 + tt],
                                                sttmp[:, 6:7], 127.0)
                            e.drain()
                            e.reciprocal(stf[:, 24 + tt:25 + tt],
                                         stf[:, 16 + tt:17 + tt])
                            e.drain()
                            e.tensor_tensor(stf[:, 32 + tt:33 + tt],
                                            stf[:, 8 + tt:9 + tt],
                                            stf[:, 16 + tt:17 + tt], ALU.mult)
                            e.drain()
                            e.tensor_scalar(xn16[tt % 2][:], xa[:],
                                            stf[:, 0 + tt:1 + tt],
                                            stf[:, 8 + tt:9 + tt],
                                            ALU.add, ALU.mult)
                            e.tensor_scalar(ub[:], xa[:],
                                            stf[:, 0 + tt:1 + tt],
                                            stf[:, 32 + tt:33 + tt],
                                            ALU.add, ALU.mult)
                            e.tensor_single_scalar(y0b[:], ub[:], 128.5, ALU.add)
                            src = y0b
                            dsts = [ywk, ub]
                            for wi, P in enumerate(LADDER):
                                dst = dsts[wi % 2]
                                e.tensor_scalar(wrapc[:], src[:], P, -P,
                                                ALU.is_gt, ALU.mult)
                                e.tensor_tensor(dst[:], src[:], wrapc[:], ALU.add)
                                src = dst
                            e.scalar_tensor_tensor(rq16[tt % 2][:], y0b[:], 128.0,
                                                   src[:], ALU.subtract,
                                                   ALU.subtract)
                            e.drain().then_inc(cV.h, 1)
                    PQ.add("vector", _p1_vec)

                    def _p1_evac(e, tt):
                        for b_ in range(8):
                            srcsel, g = divmod(b_, 4)
                            dstT = xnT if srcsel == 0 else rqT
                            bank = pb[6 + ((tt * 8 + b_) % 2)]
                            e.wait_ge(cT.h, t_tr[(tt, b_)])
                            dv = dstT[:].rearrange("p (c t) -> p c t", t=TOK)
                            e.activation(
                                dv[:, 4 * g:4 * g + 4, tt * 128:(tt + 1) * 128],
                                bank[:, 0:512], AF.Identity
                                ).then_inc(cS.h, 1)

                    def _p1_scal(e):
                        for tt in range(NTT):
                            e.wait_ge(cV.h, v_var[tt])
                            e.activation(sttmp[:, 5:6], sttmp[:, 4:5], AF.Ln,
                                         bias=ccol[:, 0:1], scale=1.0)
                            e.drain().then_inc(cS.h, 1)
                            e.wait_ge(cV.h, v_neg[tt])
                            e.activation(stf[:, 8 + tt:9 + tt], sttmp[:, 4:5],
                                         AF.Exp, scale=1.0)
                            e.drain().then_inc(cS.h, 1)
                            if tt >= 1:
                                _p1_evac(e, tt - 1)
                        _p1_evac(e, NTT - 1)
                        e.drain().then_inc(cS.h, 1)
                    PQ.add("scalar", _p1_scal)

                    def _p1_tens(e):
                        for tt in range(NTT):
                            e.wait_ge(cV.h, v_p1[tt])
                            for b_ in range(8):
                                srcsel, g = divmod(b_, 4)
                                src16 = xn16[tt % 2] if srcsel == 0 else rq16[tt % 2]
                                bank = pb[6 + ((tt * 8 + b_) % 2)]
                                gb = tt * 8 + b_
                                if gb >= 2:
                                    pv = gb - 2
                                    e.wait_ge(cS.h, s_tr[(pv // 8, pv % 8)])
                                for j in range(4):
                                    c = 4 * g + j
                                    e.matmul(bank[:, j * 128:(j + 1) * 128],
                                             src16[:, c * 128:(c + 1) * 128],
                                             ident[:], start=True, stop=True)
                                e.drain().then_inc(cT.h, 1)
                    PQ.add("tensor", _p1_tens)

                    dbg_dump([("dbg_xnT", xnT[:]), ("dbg_rqT", rqT[:]),
                              ("dbg_stf1", stf[:])])
                    barrier()
                    d_wq = [0] * NFC
                    t_qp = [0] * NFC
                    v_min = {}
                    s_sig = {}
                    v_done = {}
                    for o in range(NFC):
                        dG.n += 16
                        d_wq[o] = dG.n
                        t_qp[o] = inc(cT)
                        for half in range(2):
                            v_min[(o, half)] = inc(cV)
                            v_done[(o, half)] = inc(cV)
                    v_p2a_done = inc(cV)
                    for o in range(NFC):
                        for half in range(2):
                            s_sig[(o, half)] = inc(cS)

                    def _p2a_gp(e):
                        for o in range(NFC):
                            if o >= 2:
                                e.wait_ge(cT.h, t_qp[o - 2])
                            e.dma_start(
                                out=wq_s[o % 2][:].rearrange("p (c o2) -> p c o2",
                                                             o2=128),
                                in_=dwq_c[:, :, o * 128:(o + 1) * 128]
                            ).then_inc(dG.h, 16)
                    PQ.add("gpsimd", _p2a_gp)

                    def _p2a_t(e):
                        for o in range(NFC):
                            e.wait_ge(dG.h, d_wq[o])
                            if o >= 2:
                                e.wait_ge(cV.h, v_done[(o - 2, 1)])
                            pa, pbk = pb[2 * (o % 2)], pb[2 * (o % 2) + 1]
                            for i in range(NFC):
                                lhsT = wq_s[o % 2][:, i * 128:(i + 1) * 128]
                                e.matmul(pa[:, 0:512], lhsT,
                                         xnT[:, i * TOK:i * TOK + 512],
                                         start=(i == 0), stop=(i == NFC - 1))
                                e.matmul(pbk[:, 0:512], lhsT,
                                         xnT[:, i * TOK + 512:(i + 1) * TOK],
                                         start=(i == 0), stop=(i == NFC - 1))
                            e.drain().then_inc(cT.h, 1)
                    PQ.add("tensor", _p2a_t)

                    def _p2a_v(e):
                        for o in range(NFC):
                            e.wait_ge(cT.h, t_qp[o])
                            for half in range(2):
                                pban = pb[2 * (o % 2) + half]
                                e.tensor_scalar_min(sigt[:], pban[:, 0:512], 0.0)
                                e.drain().then_inc(cV.h, 1)
                                e.wait_ge(cS.h, s_sig[(o, half)])
                                e.scalar_tensor_tensor(
                                    sqT[:, o * TOK + half * 512:
                                        o * TOK + (half + 1) * 512],
                                    pban[:, 0:512], 0.0, sige[:],
                                    ALU.max, ALU.add
                                    ).then_inc(cV.h, 1)
                        e.drain().then_inc(cV.h, 1)
                    PQ.add("vector", _p2a_v)

                    def _p2a_s(e):
                        for o in range(NFC):
                            for half in range(2):
                                e.wait_ge(cV.h, v_min[(o, half)])
                                e.activation(sige[:], sigt[:], AF.Exp)
                                e.drain().then_inc(cS.h, 1)
                    PQ.add("scalar", _p2a_s)

                L3b = ExitStack()
                with L3b:
                    sb3b = lambda *a, **k: L3b.enter_context(nc.sbuf_tensor(*a, **k))
                    combT = sb3b("combT", [128, NFC * TOK], BF16)
                    comb16 = [sb3b("comb16_%d" % i, [128, HID], BF16)
                              for i in range(2)]
                    cadd = sb3b("cadd", [128, 256], F32)
                    wo_s = [sb3b("wo_s%d" % i, [128, HID], BF16) for i in range(2)]
                    xt_b = [sb3b("xt_b%d" % i, [128, TOK], F32) for i in range(2)]
                    ot_b = [sb3b("ot_b%d" % i, [128, TOK], F32) for i in range(2)]

                    dbg_dump([("dbg_sqT", sqT[:]), ("dbg_wab", wa_b[:])])
                    barrier()
                    t_nrm = [0] * NTT
                    v_rn = [0] * NTT
                    t_A = {}
                    v_cb = {}
                    v_cbd = [0] * NTT
                    t_ct = {}
                    s_ct = {}
                    for tt in range(NTT):
                        t_nrm[tt] = inc(cT)
                        for h in range(NH):
                            t_A[(tt, h)] = inc(cT)
                        for g in range(4):
                            t_ct[(tt, g)] = inc(cT)
                    for tt in range(NTT):
                        v_rn[tt] = inc(cV)
                        for h in range(NH):
                            v_cb[(tt, h)] = inc(cV)
                        v_cbd[tt] = inc(cV)
                    for tt in range(NTT):
                        for g in range(4):
                            s_ct[(tt, g)] = inc(cS)
                    s_p2b_done = inc(cS)

                    def _p2b_t(e):
                        for tt in range(NTT):
                            if tt >= 1:
                                e.wait_ge(cV.h, v_rn[tt - 1])
                            for h in range(NH):
                                sq_blk = sqT[:, h * TOK + tt * 128:
                                             h * TOK + (tt + 1) * 128]
                                e.matmul(pb[4][:, 4 * h:4 * h + 4], sq_blk,
                                         nrm16[:, 4 * h:4 * h + 4],
                                         start=True, stop=True)
                            e.drain().then_inc(cT.h, 1)
                            for h in range(NH):
                                step = tt * NH + h
                                if step >= 4:
                                    pvs = step - 4
                                    e.wait_ge(cV.h, v_cb[(pvs // NH, pvs % NH)])
                                sq_blk = sqT[:, h * TOK + tt * 128:
                                             h * TOK + (tt + 1) * 128]
                                e.matmul(pb[step % 4][:, 0:512], sq_blk,
                                         mp16m[:, h * 512:(h + 1) * 512],
                                         start=True, stop=True)
                                e.drain().then_inc(cT.h, 1)
                            e.wait_ge(cV.h, v_cbd[tt])
                            for g in range(4):
                                ctb = tt * 4 + g
                                bank = pb[6 + (ctb % 2)]
                                if ctb >= 2:
                                    pv = ctb - 2
                                    e.wait_ge(cS.h, s_ct[(pv // 4, pv % 4)])
                                for j in range(4):
                                    c = 4 * g + j
                                    e.matmul(bank[:, j * 128:(j + 1) * 128],
                                             comb16[tt % 2][:, c * 128:(c + 1) * 128],
                                             ident[:], start=True, stop=True)
                                e.drain().then_inc(cT.h, 1)
                    PQ.add("tensor", _p2b_t)

                    def _p2b_v(e):
                        for tt in range(NTT):
                            e.wait_ge(cT.h, t_nrm[tt])
                            if tt >= 2:
                                e.wait_ge(cS.h, s_ct[(tt - 2, 3)])
                            e.tensor_scalar_max(rn64[:], pb[4][:, 0:64], 1e-6)
                            e.drain()
                            e.reciprocal(rn64b[:], rn64[:])
                            e.drain()
                            e.tensor_tensor(rn64[:], rn64b[:], wa_b[:], ALU.mult)
                            e.drain().then_inc(cV.h, 1)
                            for h in range(NH):
                                e.wait_ge(cT.h, t_A[(tt, h)])
                                pA = pb[(tt * NH + h) % 4]
                                e.tensor_scalar(cadd[:, 0:128], pA[:, 0:128],
                                                rn64[:, 4 * h:4 * h + 1], None,
                                                ALU.mult)
                                e.scalar_tensor_tensor(cadd[:, 128:256],
                                                       pA[:, 128:256],
                                                       rn64[:, 4 * h + 1:4 * h + 2],
                                                       cadd[:, 0:128],
                                                       ALU.mult, ALU.add)
                                e.scalar_tensor_tensor(cadd[:, 0:128],
                                                       pA[:, 256:384],
                                                       rn64[:, 4 * h + 2:4 * h + 3],
                                                       cadd[:, 128:256],
                                                       ALU.mult, ALU.add)
                                e.scalar_tensor_tensor(
                                    comb16[tt % 2][:, h * 128:(h + 1) * 128],
                                    pA[:, 384:512],
                                    rn64[:, 4 * h + 3:4 * h + 4],
                                    cadd[:, 0:128], ALU.mult, ALU.add
                                    ).then_inc(cV.h, 1)
                            e.drain().then_inc(cV.h, 1)
                    PQ.add("vector", _p2b_v)

                    def _p2b_s(e):
                        for tt in range(NTT):
                            for g in range(4):
                                e.wait_ge(cT.h, t_ct[(tt, g)])
                                bank = pb[6 + ((tt * 4 + g) % 2)]
                                dv = combT[:].rearrange("p (c t) -> p c t", t=TOK)
                                e.activation(
                                    dv[:, 4 * g:4 * g + 4, tt * 128:(tt + 1) * 128],
                                    bank[:, 0:512], AF.Identity
                                    ).then_inc(cS.h, 1)
                        e.drain().then_inc(cS.h, 1)
                    PQ.add("scalar", _p2b_s)

                    dbg_dump([("dbg_combT", combT[:])])
                    barrier()
                    d_wo = [0] * NFC
                    d_xt = [0] * NFC
                    t_wo = [0] * NFC
                    v_wo = [0] * NFC
                    d_park = [0] * NFC
                    for o in range(NFC):
                        dG.n += 16
                        d_wo[o] = dG.n
                        dS.n += 16
                        d_xt[o] = dS.n
                        dS.n += 16
                        d_park[o] = dS.n
                        t_wo[o] = inc(cT)
                        v_wo[o] = inc(cV)

                    def _p2c_gp(e):
                        for o in range(NFC):
                            if o >= 2:
                                e.wait_ge(cT.h, t_wo[o - 2])
                            e.dma_start(
                                out=wo_s[o % 2][:].rearrange("p (c o2) -> p c o2",
                                                             o2=128),
                                in_=dwo_c[:, :, o * 128:(o + 1) * 128]
                            ).then_inc(dG.h, 16)
                    PQ.add("gpsimd", _p2c_gp)

                    def _p2c_sync(e):
                        for o in range(NFC):
                            if o >= 2:
                                e.wait_ge(cV.h, v_wo[o - 2])
                            e.dma_start(out=xt_b[o % 2][:],
                                        in_=dxT[o * 128:(o + 1) * 128, :]
                                        ).then_inc(dS.h, 16)
                            e.wait_ge(cV.h, v_wo[o])
                            e.dma_start(out=dout[o * 128:(o + 1) * 128, :],
                                        in_=ot_b[o % 2][:]).then_inc(dS.h, 16)
                    PQ.add("sync", _p2c_sync)

                    def _p2c_t(e):
                        for o in range(NFC):
                            e.wait_ge(dG.h, d_wo[o])
                            if o >= 2:
                                e.wait_ge(cV.h, v_wo[o - 2])
                            pa, pbk = pb[2 * (o % 2)], pb[2 * (o % 2) + 1]
                            for k in range(NFC):
                                lhsT = wo_s[o % 2][:, k * 128:(k + 1) * 128]
                                e.matmul(pa[:, 0:512], lhsT,
                                         combT[:, k * TOK:k * TOK + 512],
                                         start=(k == 0), stop=(k == NFC - 1))
                                e.matmul(pbk[:, 0:512], lhsT,
                                         combT[:, k * TOK + 512:(k + 1) * TOK],
                                         start=(k == 0), stop=(k == NFC - 1))
                            e.drain().then_inc(cT.h, 1)
                    PQ.add("tensor", _p2c_t)

                    def _p2c_v(e):
                        for o in range(NFC):
                            e.wait_ge(cT.h, t_wo[o])
                            e.wait_ge(dS.h, d_xt[o])
                            if o >= 2:
                                e.wait_ge(dS.h, d_park[o - 2])
                            pa, pbk = pb[2 * (o % 2)], pb[2 * (o % 2) + 1]
                            e.tensor_tensor(ot_b[o % 2][:, 0:512],
                                            xt_b[o % 2][:, 0:512],
                                            pa[:, 0:512], ALU.add)
                            e.tensor_tensor(ot_b[o % 2][:, 512:1024],
                                            xt_b[o % 2][:, 512:1024],
                                            pbk[:, 0:512], ALU.add)
                            e.drain().then_inc(cV.h, 1)
                    PQ.add("vector", _p2c_v)

            L3c = ExitStack()
            with L3c:
                wsraw = [L3c.enter_context(nc.sbuf_tensor("wsraw%d" % i,
                                                          [128, 2816], F32))
                         for i in range(2)]
                barrier()
                mats = [(dw1, NFC, FF), (dw2, NFC, FF), (dw3, NKC, HID)]
                strips_all = []
                for (dmat, nchunks, width) in mats:
                    st = []
                    for c in range(nchunks):
                        if width == FF:
                            st.append((c, 0, 2816))
                            st.append((c, 2816, 2816))
                        else:
                            st.append((c, 0, 2048))
                    strips_all.append(st)
                d_ws = {}
                v_ws = {}
                t_sum = [0] * 3
                v_sw1 = [0] * 3
                t_bc = [0] * 3
                v_sw2 = [0] * 3
                flat = []
                for m in range(3):
                    for j in range(len(strips_all[m])):
                        flat.append((m, j))
                for idx, (m, j) in enumerate(flat):
                    dS.n += 16
                    d_ws[(m, j)] = dS.n
                for m in range(3):
                    for j in range(len(strips_all[m])):
                        v_ws[(m, j)] = inc(cV)
                    v_sw1[m] = inc(cV)
                    v_sw2[m] = inc(cV)
                for m in range(3):
                    t_sum[m] = inc(cT)
                    t_bc[m] = inc(cT)

                def _p3_sync(e):
                    for idx, (m, j) in enumerate(flat):
                        if idx >= 2:
                            pm, pj = flat[idx - 2]
                            e.wait_ge(cV.h, v_ws[(pm, pj)])
                        c, off, wdt = strips_all[m][j]
                        e.dma_start(out=wsraw[idx % 2][:, 0:wdt],
                                    in_=mats[m][0][c * 128:(c + 1) * 128,
                                                   off:off + wdt]
                                    ).then_inc(dS.h, 16)
                PQ.add("sync", _p3_sync)

                def _p3_v(e):
                    idx = 0
                    for m in range(3):
                        acc = 0
                        e.memset(wsrun[:, 0:1], 0.0)
                        e.memset(wsrun[:, 3:4], 0.0)
                        for j in range(len(strips_all[m])):
                            wdt = strips_all[m][j][2]
                            nxt = 3 - acc
                            e.wait_ge(dS.h, d_ws[(m, j)])
                            e.reduce_sum(wsrun[:, 1:2], wsraw[idx % 2][:, 0:wdt],
                                         axis=AX.X, apply_absolute_value=True)
                            e.drain()
                            ins = e.tensor_tensor(wsrun[:, nxt:nxt + 1],
                                                  wsrun[:, acc:acc + 1],
                                                  wsrun[:, 1:2], ALU.add)
                            if j == len(strips_all[m]) - 1:
                                e.drain().then_inc(cV.h, 1)
                            else:
                                ins.then_inc(cV.h, 1)
                            acc = nxt
                            idx += 1
                        e.wait_ge(cT.h, t_sum[m])
                        e.tensor_scalar_mul(sc11[0:1, 0:1], pb[5][0:1, 0:1],
                                            1.0 / WMEAN_N)
                        e.drain()
                        e.tensor_scalar_max(sc11[0:1, 1:2], sc11[0:1, 0:1], 1e-5)
                        e.drain()
                        e.reciprocal(sc11[0:1, 2:3], sc11[0:1, 1:2])
                        e.drain().then_inc(cV.h, 1)
                        e.wait_ge(cT.h, t_bc[m])
                        e.tensor_copy(swcol[:, 3 * m:3 * m + 1], pb[5][:, 100:101])
                        e.tensor_scalar_mul(swcol[:, 3 * m + 1:3 * m + 2],
                                            pb[5][:, 100:101], -1.0)
                        e.tensor_copy(swcol[:, 3 * m + 2:3 * m + 3],
                                      pb[5][:, 101:102])
                        e.tensor_scalar_mul(swcol[:, 9 + 2 * m:10 + 2 * m],
                                            pb[5][:, 101:102], 0.5)
                        e.tensor_scalar_mul(swcol[:, 10 + 2 * m:11 + 2 * m],
                                            pb[5][:, 101:102], -0.5)
                        e.drain()
                        if m < 2:
                            e.tensor_scalar(stf[:, 56 + 8 * m:64 + 8 * m],
                                            stf[:, 24:32],
                                            swcol[:, 3 * m + 2:3 * m + 3], None,
                                            ALU.mult)
                        e.drain().then_inc(cV.h, 1)
                PQ.add("vector", _p3_v)

                def _p3_t(e):
                    for m in range(3):
                        nst = len(strips_all[m])
                        acc_final = 0
                        for _ in range(nst):
                            acc_final = 3 - acc_final
                        e.wait_ge(cV.h, v_ws[(m, nst - 1)])
                        e.matmul(pb[5][0:1, 0:1],
                                 wsrun[:, acc_final:acc_final + 1], onescol[:],
                                 start=True, stop=True)
                        e.drain().then_inc(cT.h, 1)
                        e.wait_ge(cV.h, v_sw1[m])
                        e.matmul(pb[5][:, 100:101], onesr[:], sc11[0:1, 2:3],
                                 start=True, stop=True)
                        e.matmul(pb[5][:, 101:102], onesr[:], sc11[0:1, 1:2],
                                 start=True, stop=True)
                        e.drain().then_inc(cT.h, 1)
                PQ.add("tensor", _p3_t)

            L4 = ExitStack()
            with L4:
                hTa = L4.enter_context(nc.sbuf_tensor("hTa", [128, 16 * TOK], F16))
                hTb = L4.enter_context(nc.sbuf_tensor("hTb", [128, 28 * TOK], F16))

                def hT(k):
                    if k < 16:
                        return hTa[:, k * TOK:(k + 1) * TOK]
                    return hTb[:, (k - 16) * TOK:(k - 15) * TOK]

                def hTpair(kk, tt):
                    base = hTa if kk < 16 else hTb
                    k0 = kk if kk < 16 else kk - 16
                    dv = base[:].rearrange("p (k t) -> p k t", t=TOK)
                    return dv[:, k0:k0 + 2, tt * 128:(tt + 1) * 128]

                L5 = ExitStack()
                with L5:
                    sb5 = lambda *a, **k: L5.enter_context(nc.sbuf_tensor(*a, **k))
                    t1s = [sb5("t1s%d" % i, [128, NFC * OS], BF16) for i in range(2)]
                    t2s = [sb5("t2s%d" % i, [128, NFC * OS], BF16) for i in range(2)]
                    raw1 = [sb5("raw1%d" % i, [128, 1024], F32) for i in range(2)]
                    raw2 = [sb5("raw2%d" % i, [128, 1024], F32) for i in range(2)]
                    qa = sb5("qa", [128, 512], BF16)
                    qb = sb5("qb", [128, 512], BF16)
                    pby1 = [pb[0], pb[1], pb[4]]
                    pby2 = [pb[2], pb[3], pb[5]]
                    tmp1 = [sb5("tmp1%d" % i, [128, OS], F32) for i in range(4)]
                    tmp2 = [sb5("tmp2%d" % i, [128, OS], F32) for i in range(4)]
                    hsl = [sb5("hsl%d" % i, [128, OS], F16) for i in range(6)]

                    barrier()
                    d_r1 = {}
                    d_r2 = {}
                    v_q = {}
                    t_y = {}
                    s_y = {}
                    v_h = {}
                    t_h = {}
                    s_h = {}
                    for o in range(NOS):
                        for dg in range(4):
                            dS.n += 16
                            d_r1[o * 4 + dg] = dS.n
                            dS2.n += 16
                            d_r2[o * 4 + dg] = dS2.n
                    for o in range(NOS + 1):
                        if o < NOS:
                            for g in range(8):
                                v_q[o * 8 + g] = inc(cV)
                        if o >= 1:
                            for tt in range(NTT):
                                v_h[(o - 1, tt)] = inc(cV)
                    for step in range(NOS * NTT):
                        o, tt = divmod(step, NTT)
                        t_y[(o, tt)] = inc(cT)
                        if step >= 5:
                            po, ptt = divmod(step - 5, NTT)
                            t_h[(po, ptt)] = inc(cT)
                    for ps in range(NOS * NTT - 5, NOS * NTT):
                        t_h[(ps // 8, ps % 8)] = inc(cT)
                    for step in range(NOS * NTT):
                        s_y[(step // 8, step % 8)] = inc(cS)
                        if step >= 4:
                            s_h[((step - 4) // 8, (step - 4) % 8)] = inc(cS)
                    for ps in range(NOS * NTT - 4, NOS * NTT):
                        s_h[(ps // 8, ps % 8)] = inc(cS)
                    s_p4_done = inc(cS)

                    def _p4_sync(e):
                        for o in range(NOS):
                            for dg in range(4):
                                dgidx = o * 4 + dg
                                if dgidx >= 2:
                                    e.wait_ge(cV.h, v_q[(dgidx - 2) * 2 + 1])
                                e.dma_start(
                                    out=raw1[dgidx % 2][:].rearrange(
                                        "p (c f) -> p c f", f=OS),
                                    in_=dw1_c[:, 4 * dg:4 * dg + 4,
                                              o * OS:(o + 1) * OS]
                                ).then_inc(dS.h, 16)
                    PQ.add("sync", _p4_sync)

                    def _p4_gp(e):
                        for o in range(NOS):
                            for dg in range(4):
                                dgidx = o * 4 + dg
                                if dgidx >= 2:
                                    e.wait_ge(cV.h, v_q[(dgidx - 2) * 2 + 1])
                                e.dma_start(
                                    out=raw2[dgidx % 2][:].rearrange(
                                        "p (c f) -> p c f", f=OS),
                                    in_=dw2_c[:, 4 * dg:4 * dg + 4,
                                              o * OS:(o + 1) * OS]
                                ).then_inc(dS2.h, 16)
                    PQ.add("gpsimd", _p4_gp)

                    def _p4_s(e):
                        nstep = NOS * NTT

                        def hTev(ps):
                            po, ptt = divmod(ps, NTT)
                            e.wait_ge(cT.h, t_h[(po, ptt)] +
                                      (1 if ps < nstep - 2 else 0))
                            tco = 256 * ((ps // 2) % 2)
                            bank = pb[6 + (ps % 2)]
                            e.activation(hTpair(2 * po, ptt),
                                         bank[:, tco:tco + 256],
                                         AF.Identity).then_inc(cS.h, 1)

                        for step in range(nstep):
                            o, tt = divmod(step, NTT)
                            if step < nstep - 1:
                                nx = step + 1
                                e.wait_ge(cT.h, t_y[(nx // 8, nx % 8)])
                            else:
                                e.wait_ge(cT.h, t_y[(o, tt)])
                            if step >= 4:
                                prev = step - 4
                                e.wait_ge(cV.h, v_h[(prev // 8, prev % 8)])
                            pY1 = pby1[step % 3]
                            pY2 = pby2[step % 3]
                            e.activation(tmp1[step % 4][:], pY1[:, 0:OS],
                                         AF.Silu, scale=stf[:, 56 + tt:57 + tt])
                            e.activation(tmp2[step % 4][:], pY2[:, 0:OS],
                                         AF.Identity,
                                         scale=stf[:, 64 + tt:65 + tt])
                            e.drain().then_inc(cS.h, 1)
                            if step >= 4:
                                hTev(step - 4)
                        for ps in range(nstep - 4, nstep):
                            hTev(ps)
                        e.drain().then_inc(cS.h, 1)
                    PQ.add("scalar", _p4_s)

                    def _p4_v(e):
                        for o in range(NOS + 1):
                            if o < NOS:
                                for g in range(8):
                                    gidx = o * 8 + g
                                    dgidx = o * 4 + g // 2
                                    if g % 2 == 0:
                                        e.wait_ge(dS.h, d_r1[dgidx])
                                        e.wait_ge(dS2.h, d_r2[dgidx])
                                    if o >= 2 and g == 0:
                                        e.wait_ge(cT.h, t_y[(o - 2, NTT - 1)])
                                    half = (g % 2) * 512
                                    for mat, rawb, tds, thrc in (
                                            (0, raw1[dgidx % 2], t1s[o % 2], 9),
                                            (1, raw2[dgidx % 2], t2s[o % 2], 11)):
                                        raw = rawb[:, half:half + 512]
                                        e.tensor_scalar(
                                            qa[:], raw,
                                            swcol[:, thrc + 1:thrc + 2],
                                            None, ALU.is_lt)
                                        ins = e.scalar_tensor_tensor(
                                            tds[:, 2 * g * OS:(2 * g + 2) * OS],
                                            raw, swcol[:, thrc:thrc + 1],
                                            qa[:], ALU.is_gt, ALU.subtract)
                                    if g == 7:
                                        e.drain().then_inc(cV.h, 1)
                                    else:
                                        ins.then_inc(cV.h, 1)
                            if o >= 1:
                                oo = o - 1
                                for tt in range(NTT):
                                    step = oo * 8 + tt
                                    e.wait_ge(cS.h, s_y[(oo, tt)])
                                    if step >= 6:
                                        prev = step - 6
                                        e.wait_ge(cS.h, s_h[(prev // 8, prev % 8)])
                                    e.tensor_tensor(hsl[step % 6][:],
                                                    tmp1[step % 4][:],
                                                    tmp2[step % 4][:], ALU.mult)
                                    e.reduce_max(
                                        hmx[:, tt * 22 + oo:tt * 22 + oo + 1],
                                        hsl[step % 6][:], axis=AX.X,
                                        apply_absolute_value=True)
                                    e.drain().then_inc(cV.h, 1)
                    PQ.add("vector", _p4_v)

                    def _p4_t(e):
                        nstep = NOS * NTT

                        def trans(ps):
                            po, ptt = divmod(ps, NTT)
                            e.wait_ge(cV.h, v_h[(po, ptt)])
                            if ps >= 2:
                                p2 = ps - 2
                                e.wait_ge(cS.h, s_h[(p2 // 8, p2 % 8)])
                            tco = 256 * ((ps // 2) % 2)
                            bank = pb[6 + (ps % 2)]
                            for j in range(2):
                                ins = e.matmul(
                                    bank[:, tco + j * 128:tco + (j + 1) * 128],
                                    hsl[ps % 6][:, j * 128:(j + 1) * 128],
                                    ident16[:], start=True, stop=True)
                            if ps >= nstep - 3:
                                e.drain().then_inc(cT.h, 1)
                            else:
                                ins.then_inc(cT.h, 1)

                        for step in range(nstep):
                            o, tt = divmod(step, NTT)
                            e.wait_ge(cV.h, v_q[o * 8 + 7])
                            if step >= 3:
                                prev = step - 3
                                e.wait_ge(cS.h, s_y[(prev // 8, prev % 8)])
                            pY1 = pby1[step % 3]
                            pY2 = pby2[step % 3]
                            for i in range(NFC):
                                lhsT = rqT[:, i * TOK + tt * 128:
                                           i * TOK + (tt + 1) * 128]
                                e.matmul(pY1[:, 0:OS], lhsT,
                                         t1s[o % 2][:, i * OS:(i + 1) * OS],
                                         start=(i == 0), stop=(i == NFC - 1))
                                ins = e.matmul(pY2[:, 0:OS], lhsT,
                                               t2s[o % 2][:, i * OS:(i + 1) * OS],
                                               start=(i == 0), stop=(i == NFC - 1))
                            if step == nstep - 1:
                                e.drain().then_inc(cT.h, 1)
                            else:
                                ins.then_inc(cT.h, 1)
                            if step >= 5:
                                trans(step - 5)
                        for ps in range(nstep - 5, nstep):
                            trans(ps)
                    PQ.add("tensor", _p4_t)

                dbg_dump([("dbg_hTa", hTa[:]), ("dbg_hTb", hTb[:]),
                          ("dbg_stf2", stf[:]), ("dbg_sw", swcol[:])])
                L6 = ExitStack()
                with L6:
                    sb6 = lambda *a, **k: L6.enter_context(nc.sbuf_tensor(*a, **k))
                    p56 = sb6("p56", [128, 5632], F32)
                    lad6 = sb6("lad6", [128, 2048], F32)
                    xt6 = [sb6("xt6_%d" % i, [128, 512], F32) for i in range(2)]
                    ev6 = sb6("ev6", [128, 512], F32)
                    fin6 = sb6("fin6", [128, 512], F32)
                    qa3 = sb6("qa3", [128, 2816], F16)
                    srow = sb6("srow", [1, 2048], F32)
                    s3r_b = sb6("s3r_b", [128, 1024], F32)
                    qs2r_b = sb6("qs2r_b", [128, 1024], F32)
                    t3s = [rqT[:, i * FF:(i + 1) * FF].bitcast(F16)
                           for i in range(2)]

                    def lad_chunk(e, k, c0, scr, bare=True):
                        u = scr[:, 0:512]
                        y0 = scr[:, 512:1024]
                        w1_ = scr[:, 1024:1536]
                        w2_ = scr[:, 1536:2048]
                        hk = hT(k)[:, c0:c0 + 512]
                        e.tensor_tensor(w1_[:], hk, qs2r_b[:, c0:c0 + 512],
                                        ALU.mult)
                        e.tensor_single_scalar(y0[:], w1_[:], 128.5, ALU.add)
                        srk = y0
                        dsts = [w1_, w2_]
                        for wi, P in enumerate(LADDER):
                            dst = dsts[wi % 2]
                            e.tensor_scalar(u[:], srk[:], P, -P,
                                            ALU.is_gt, ALU.mult)
                            e.tensor_tensor(dst[:], srk[:], u[:], ALU.add)
                            srk = dst
                        ins = e.scalar_tensor_tensor(hk, y0[:], 128.0, srk[:],
                                                     ALU.subtract, ALU.subtract)
                        if bare:
                            ins.then_inc(cV.h, 1)
                        else:
                            e.drain().then_inc(cV.h, 1)

                    barrier()
                    v_qs2 = inc(cV)
                    dG.n += 16 * 16
                    d_srow = dG.n
                    t_row = inc(cT)
                    v_rows = inc(cV)
                    for k in range(NKC):
                        inc(cV)
                    v_p5_done = inc(cV)
                    d_t3 = {}
                    dS.n += 16
                    d_t3[(0, 0, 0)] = dS.n
                    dS2.n += 16
                    d_t3[(0, 0, 1)] = dS2.n

                    def _p5_v(e):
                        for tt in range(NTT):
                            e.reduce_max(stf[:, 40 + tt:41 + tt],
                                         hmx[:, tt * 22:(tt + 1) * 22],
                                         axis=AX.X, apply_absolute_value=True)
                        e.drain()
                        e.tensor_scalar_max(stf[:, 88:96], stf[:, 40:48], 1e-5)
                        e.drain()
                        e.reciprocal(stf[:, 72:80], stf[:, 88:96])
                        e.drain()
                        e.tensor_scalar_mul(stf[:, 48:56], stf[:, 72:80], 127.0)
                        e.drain()
                        e.reciprocal(stf[:, 72:80], stf[:, 48:56])
                        e.drain()
                        e.tensor_scalar(stf[:, 80:88], stf[:, 72:80],
                                        swcol[:, 8:9], None, ALU.mult)
                        e.drain().then_inc(cV.h, 1)
                        e.wait_ge(cT.h, t_row)
                        e.tensor_copy(qs2r_b[:, 0:512], pb[4][:, 0:512])
                        e.tensor_copy(qs2r_b[:, 512:1024], pb[5][:, 0:512])
                        e.tensor_copy(s3r_b[:, 0:512], pb[6][:, 0:512])
                        e.tensor_copy(s3r_b[:, 512:1024], pb[7][:, 0:512])
                        e.drain().then_inc(cV.h, 1)
                        for k in range(NKC):
                            lad_chunk(e, k, 0, lad6)
                        e.drain().then_inc(cV.h, 1)
                    PQ.add("vector", _p5_v)

                    def _p5_gp(e):
                        e.dma_start(
                            out=p56[:, 2816:5632].rearrange(
                                "p (c f) -> p c f", f=128),
                            in_=dw3_c[:, 22:44, 0:128]).then_inc(dS2.h, 16)
                        e.wait_ge(cV.h, v_qs2)
                        for tt in range(NTT):
                            e.dma_start(out=srow[0:1, tt * 128:(tt + 1) * 128],
                                        in_=stf[:, 48 + tt:49 + tt],
                                        allow_slow_non_contiguous=True
                                        ).then_inc(dG.h, 16)
                            e.dma_start(out=srow[0:1, 1024 + tt * 128:
                                                1152 + tt * 128],
                                        in_=stf[:, 80 + tt:81 + tt],
                                        allow_slow_non_contiguous=True
                                        ).then_inc(dG.h, 16)
                    PQ.add("gpsimd", _p5_gp)

                    def _p5_sync(e):
                        e.dma_start(
                            out=p56[:, 0:2816].rearrange(
                                "p (c f) -> p c f", f=128),
                            in_=dw3_c[:, 0:22, 0:128]).then_inc(dS.h, 16)
                    PQ.add("sync", _p5_sync)

                    def _p5_t(e):
                        e.wait_ge(dG.h, d_srow)
                        e.matmul(pb[4][:, 0:512], onesr[:], srow[0:1, 0:512],
                                 start=True, stop=True)
                        e.matmul(pb[5][:, 0:512], onesr[:], srow[0:1, 512:1024],
                                 start=True, stop=True)
                        e.matmul(pb[6][:, 0:512], onesr[:], srow[0:1, 1024:1536],
                                 start=True, stop=True)
                        e.matmul(pb[7][:, 0:512], onesr[:], srow[0:1, 1536:2048],
                                 start=True, stop=True)
                        e.drain().then_inc(cT.h, 1)
                    PQ.add("tensor", _p5_t)

                    dbg_dump([("dbg_q2a", hTa[:]), ("dbg_q2b", hTb[:]),
                              ("dbg_qs2r", qs2r_b[:]), ("dbg_s3r", s3r_b[:])])
                    barrier()
                    sched = [3] * 12 + [2] * 4
                    slot_ks = []
                    kk = 0
                    for o3 in range(NFC):
                        slot_ks.append(list(range(kk, kk + sched[o3])))
                        kk += sched[o3]
                    assert kk == NKC
                    d_x6 = {}
                    d_fst = {}
                    v_q3 = {}
                    v_ladB = {}
                    v_f = {}
                    t_y3 = {}
                    for p in range(2):
                        for o3 in range(NFC):
                            if (p, o3) == (0, 0):
                                continue
                            dS.n += 16
                            d_t3[(p, o3, 0)] = dS.n
                            dS2.n += 16
                            d_t3[(p, o3, 1)] = dS2.n
                    for p in range(2):
                        for o3 in range(NFC):
                            dA.n += 16
                            d_x6[(p, o3)] = dA.n
                            dA.n += 16
                            d_fst[(p, o3)] = dA.n
                    n_out = d_fst[(1, NFC - 1)]
                    for p in range(2):
                        for o3 in range(NFC):
                            idx = p * NFC + o3
                            if idx == 0:
                                for j in range(2):
                                    v_q3[(0, 0, j)] = inc(cV)
                            if p == 0:
                                for k in slot_ks[o3]:
                                    v_ladB[k] = inc(cV)
                            if idx + 1 < 2 * NFC:
                                np_, no3 = (idx + 1) // NFC, (idx + 1) % NFC
                                for j in range(2):
                                    v_q3[(np_, no3, j)] = inc(cV)
                            v_f[(p, o3)] = inc(cV)
                    for p in range(2):
                        for o3 in range(NFC):
                            t_y3[(p, o3)] = inc(cT)

                    def skey(n):
                        return (n // 32, (n % 32) // 2, n % 2)

                    def _p6_sync(e):
                        for idx in range(1, 2 * NFC):
                            p, o3 = idx // NFC, idx % NFC
                            e.wait_ge(cV.h, v_q3[(( idx - 1) // NFC,
                                                  (idx - 1) % NFC, 0)])
                            e.dma_start(
                                out=p56[:, 0:2816].rearrange(
                                    "p (c f) -> p c f", f=128),
                                in_=dw3_c[:, 0:22,
                                          o3 * 128:(o3 + 1) * 128]
                            ).then_inc(dS.h, 16)
                    PQ.add("sync", _p6_sync)

                    def _p6_s(e):
                        for idx in range(2 * NFC):
                            p, o3 = idx // NFC, idx % NFC
                            if idx >= 2:
                                pi = idx - 2
                                e.wait_ge(cV.h, v_f[(pi // NFC, pi % NFC)])
                            e.dma_start(
                                out=xt6[idx % 2][:],
                                in_=dout[o3 * 128:(o3 + 1) * 128,
                                         p * 512:(p + 1) * 512]
                            ).then_inc(dA.h, 16)
                            e.wait_ge(cV.h, v_f[(p, o3)])
                            e.dma_start(
                                out=dout[o3 * 128:(o3 + 1) * 128,
                                         p * 512:(p + 1) * 512],
                                in_=fin6[:]).then_inc(dA.h, 16)
                        e.wait_ge(dA.h, n_out)
                    PQ.add("scalar", _p6_s)

                    def _p6_gp(e):
                        for idx in range(1, 2 * NFC):
                            p, o3 = idx // NFC, idx % NFC
                            e.wait_ge(cV.h, v_q3[((idx - 1) // NFC,
                                                  (idx - 1) % NFC, 1)])
                            e.dma_start(
                                out=p56[:, 2816:5632].rearrange(
                                    "p (c f) -> p c f", f=128),
                                in_=dw3_c[:, 22:44,
                                          o3 * 128:(o3 + 1) * 128]
                            ).then_inc(dS2.h, 16)
                    PQ.add("gpsimd", _p6_gp)

                    def _p6_v(e):
                        def quant(idx):
                            p, o3 = idx // NFC, idx % NFC
                            for j in range(2):
                                e.wait_ge(dS.h if j == 0 else dS2.h,
                                          d_t3[(p, o3, j)])
                                if j == 0 and idx >= 2:
                                    pi = idx - 2
                                    e.wait_ge(cT.h, t_y3[(pi // NFC,
                                                          pi % NFC)])
                                raw = p56[:, j * 2816:(j + 1) * 2816]
                                dst = t3s[idx % 2][:, j * 2816:(j + 1) * 2816]
                                e.tensor_scalar(qa3[:], raw, swcol[:, 14:15],
                                                None, ALU.is_lt)
                                ins = e.scalar_tensor_tensor(
                                    dst, raw, swcol[:, 13:14],
                                    qa3[:], ALU.is_gt, ALU.subtract)
                                if j == 0:
                                    ins.then_inc(cV.h, 1)
                                else:
                                    e.drain().then_inc(cV.h, 1)

                        for p in range(2):
                            for o3 in range(NFC):
                                idx = p * NFC + o3
                                if idx == 0:
                                    quant(0)
                                if p == 0:
                                    for ki, k in enumerate(slot_ks[o3]):
                                        last = (o3 == NFC - 1 and
                                                ki == len(slot_ks[o3]) - 1)
                                        lad_chunk(e, k, 512, lad6,
                                                  bare=not last)
                                if idx + 1 < 2 * NFC:
                                    quant(idx + 1)
                                e.wait_ge(cT.h, t_y3[(p, o3)])
                                e.wait_ge(dA.h, d_x6[(p, o3)])
                                if idx >= 1:
                                    pidx = idx - 1
                                    e.wait_ge(dA.h, d_fst[(pidx // NFC,
                                                           pidx % NFC)])
                                bank = pb[idx % 4]
                                e.tensor_tensor(ev6[:], bank[:, 0:512],
                                                s3r_b[:, p * 512:(p + 1) * 512],
                                                ALU.mult)
                                e.tensor_tensor(fin6[:], xt6[idx % 2][:],
                                                ev6[:], ALU.add)
                                e.drain().then_inc(cV.h, 1)
                    PQ.add("vector", _p6_v)

                    def _p6_t(e):
                        v_ladB_last = max(v_ladB.values())
                        for p in range(2):
                            for o3 in range(NFC):
                                idx = p * NFC + o3
                                e.wait_ge(cV.h, v_q3[(p, o3, 1)])
                                if p == 1 and o3 == 0:
                                    e.wait_ge(cV.h, v_ladB_last)
                                if idx >= 4:
                                    pidx = idx - 4
                                    e.wait_ge(cV.h, v_f[(pidx // NFC,
                                                         pidx % NFC)])
                                bank = pb[idx % 4]
                                for k in range(NKC):
                                    lhsT = t3s[idx % 2][:, k * 128:(k + 1) * 128]
                                    hk = hT(k)[:, p * 512:(p + 1) * 512]
                                    e.matmul(bank[:, 0:512], lhsT, hk,
                                             start=(k == 0), stop=(k == NKC - 1))
                                e.drain().then_inc(cT.h, 1)
                    PQ.add("tensor", _p6_t)

                    PQ.flush()
    return nc


def kernel(L0_attn_hidden_states, L0_attn_Wq, L0_attn_Wk, L0_attn_Wv, L0_attn_Wo,
           L0_attn_bank_weights, L0_attn_memories, L0_attn_memory_norms,
           L0_ffn_w1, L0_ffn_w2, L0_ffn_w3):
    f32 = np.float32
    x = np.ascontiguousarray(np.asarray(L0_attn_hidden_states, f32).reshape(8192, HID))
    wqT = np.ascontiguousarray(np.asarray(L0_attn_Wq, f32).T)
    woT = np.ascontiguousarray(np.asarray(L0_attn_Wo, f32).T)
    w1T = np.ascontiguousarray(np.asarray(L0_ffn_w1, f32).T)
    w2T = np.ascontiguousarray(np.asarray(L0_ffn_w2, f32).T)
    w3T = np.ascontiguousarray(np.asarray(L0_ffn_w3, f32).T)
    bw = np.ascontiguousarray(np.asarray(L0_attn_bank_weights, f32))
    mem = np.asarray(L0_attn_memories, f32)
    mn = np.asarray(L0_attn_memory_norms, f32)
    mpm = np.empty((NH, 128, 512), f32)
    mpn = np.empty((128, 64), f32)
    for h in range(NH):
        mpm[h] = np.concatenate([mem[0, h], mem[1, h], mem[2, h], mem[3, h]],
                                axis=1)
        for n in range(4):
            mpn[:, 4 * h + n] = mn[n, h]

    nc = build()
    in_maps = []
    for c in range(8):
        xs = x[c * TOK:(c + 1) * TOK]
        in_maps.append(dict(x_tok=np.ascontiguousarray(xs),
                            xT=np.ascontiguousarray(xs.T),
                            wqT=wqT, woT=woT, mpm=mpm, mpn=mpn, bw=bw,
                            w1T=w1T, w2T=w2T, w3T=w3T))
    res = run_bass_kernel_spmd(nc, in_maps, core_ids=list(range(8)))
    out = np.empty((8192, HID), f32)
    for c in range(8):
        out[c * TOK:(c + 1) * TOK] = np.asarray(res.results[c]["out"]).T
    return out.reshape(2, 4096, HID)


# version 24
# speedup vs baseline: 1.0481x; 1.0246x over previous
import numpy as np
import concourse.bass as bass
import concourse.mybir as mybir
from concourse.bass_utils import run_bass_kernel_spmd
from contextlib import ExitStack

F32 = mybir.dt.float32
BF16 = mybir.dt.bfloat16
F16 = mybir.dt.float16
AF = mybir.ActivationFunctionType
ALU = mybir.AluOpType
AX = mybir.AxisListType

TOK = 1024
NTT = 8
HID = 2048
NFC = 16
FF = 5632
NKC = 44
NH = 16
OS = 256
NOS = FF // OS
WMEAN_N = float(FF * HID)
LADDER = [128.0, 64.0, 32.0, 16.0, 8.0, 4.0, 2.0, 1.0]


class Ctr:
    def __init__(self, stack, nc, name):
        self.h = stack.enter_context(nc.semaphore(name))
        self.n = 0


class Q:

    def __init__(self, block):
        self.block = block
        self.qs = {k: [] for k in ("sync", "gpsimd", "vector", "scalar", "tensor")}

    def add(self, eng, fn):
        self.qs[eng].append(fn)

    def flush(self):
        for eng, cmds in self.qs.items():
            if not cmds:
                continue
            snapshot = list(cmds)
            cmds.clear()

            def body(e, snapshot=snapshot):
                for c in snapshot:
                    c(e)
            getattr(self.block, eng)(body)


def build(debug=False):
    nc = bass.Bass("TRN2", target_bir_lowering=False,
                   detect_race_conditions=False)
    dx = nc.dram_tensor("x_tok", [TOK, HID], F32, kind="ExternalInput").ap()
    dxT = nc.dram_tensor("xT", [HID, TOK], F32, kind="ExternalInput").ap()
    dwq = nc.dram_tensor("wqT", [HID, HID], F32, kind="ExternalInput").ap()
    dwo = nc.dram_tensor("woT", [HID, HID], F32, kind="ExternalInput").ap()
    dmpm = nc.dram_tensor("mpm", [NH, 128, 512], F32, kind="ExternalInput").ap()
    dmpn = nc.dram_tensor("mpn", [128, 64], F32, kind="ExternalInput").ap()
    dbw = nc.dram_tensor("bw", [16, 4], F32, kind="ExternalInput").ap()
    dw1 = nc.dram_tensor("w1T", [HID, FF], F32, kind="ExternalInput").ap()
    dw2 = nc.dram_tensor("w2T", [HID, FF], F32, kind="ExternalInput").ap()
    dw3 = nc.dram_tensor("w3T", [FF, HID], F32, kind="ExternalInput").ap()
    dout = nc.dram_tensor("out", [HID, TOK], F32, kind="ExternalOutput").ap()

    dwq_c = dwq.rearrange("(c p) o -> p c o", p=128)
    dwo_c = dwo.rearrange("(c p) o -> p c o", p=128)
    dw1_c = dw1.rearrange("(c p) f -> p c f", p=128)
    dw2_c = dw2.rearrange("(c p) f -> p c f", p=128)
    dw3_c = dw3.rearrange("(c p) o -> p c o", p=128)
    dmpm_t = dmpm.transpose([1, 0, 2])

    dbg = {}
    if debug:
        for nm, shape, dt_ in [
                ("dbg_xnT", [128, NFC * TOK], BF16),
                ("dbg_rqT", [128, NFC * TOK], BF16),
                ("dbg_stf1", [128, 96], F32),
                ("dbg_sqT", [128, NH * TOK], BF16),
                ("dbg_combT", [128, NFC * TOK], BF16),
                ("dbg_sw", [128, 16], F32),
                ("dbg_hTa", [128, 16 * TOK], F16),
                ("dbg_hTb", [128, 28 * TOK], F16),
                ("dbg_stf2", [128, 96], F32),
                ("dbg_q2a", [128, 16 * TOK], F16),
                ("dbg_q2b", [128, 28 * TOK], F16),
                ("dbg_qs2r", [128, TOK], F32),
                ("dbg_s3r", [128, TOK], F32),
                ("dbg_wab", [128, 64], F32)]:
            dbg[nm] = nc.dram_tensor(nm, shape, dt_, kind="ExternalOutput").ap()

    SG = ExitStack()
    with SG:
        sb = lambda *a, **k: SG.enter_context(nc.sbuf_tensor(*a, **k))
        ps = lambda *a: SG.enter_context(nc.psum_tensor(*a))
        ident = sb("ident", [128, 128], BF16)
        ident16 = sb("ident16", [128, 128], F16)
        iotaf = sb("iotaf", [128, 128], F32)
        onesr = sb("onesr", [1, 128], F32)
        onescol = sb("onescol", [128, 1], F32)
        wa_b = sb("wa_b", [128, 64], F32)
        stf = sb("stf", [128, 96], F32)
        sttmp = sb("sttmp", [128, 8], F32)
        swcol = sb("swcol", [128, 16], F32)
        hmx = sb("hmx", [128, 176], F32)
        wsums = sb("wsums", [128, 144], F32)
        wsp = [sb("wsp%d" % i, [128, 2048], F32) for i in range(2)]
        sm_t = sb("sm_t", [16, 32], F32)
        act4 = sb("act4", [128, 4], F32)
        smallrow = sb("smallrow", [1, 64], F32)
        wsrun = sb("wsrun", [128, 4], F32)
        rn64 = sb("rn64", [128, 64], F32)
        rn64b = sb("rn64b", [128, 64], F32)
        ccol = sb("ccol", [128, 2], F32)
        sc11 = sb("sc11", [1, 8], F32)
        pb = [ps("pb%d" % i, [128, 512], F32) for i in range(8)]

        dS = Ctr(SG, nc, "dS")
        dS2 = Ctr(SG, nc, "dS2")
        dG = Ctr(SG, nc, "dG")
        cV = Ctr(SG, nc, "cV")
        cS = Ctr(SG, nc, "cS")
        cT = Ctr(SG, nc, "cT")
        cG = Ctr(SG, nc, "cG")
        dA = Ctr(SG, nc, "dA")
        ALLC = [dS, dS2, dG, cV, cS, cT, cG, dA]
        dW = Ctr(SG, nc, "dW")
        cW = Ctr(SG, nc, "cW")
        block = SG.enter_context(nc.Block())
        PQ = Q(block)

        def inc(ctr, k=1):
            ctr.n += k
            return ctr.n

        def barrier():
            PQ.flush()
            snap = [(c.h, c.n) for c in ALLC if c.n > 0]

            def mk(engdec):
                @engdec
                def _(eng):
                    for h, v in snap:
                        eng.wait_ge(h, v)
            for ed in (block.sync, block.gpsimd, block.vector, block.scalar,
                       block.tensor):
                mk(ed)

        def dbg_dump(pairs):
            if not dbg:
                return
            PQ.flush()
            snap = [(c.h, c.n) for c in ALLC if c.n > 0]

            @block.sync
            def _(sync):
                for h, v in snap:
                    sync.wait_ge(h, v)
                for nm, srcap in pairs:
                    sync.dma_start(out=dbg[nm], in_=srcap).then_inc(dS.h, 16)
            dS.n += 16 * len(pairs)

        wstrips = []
        for m, (dmat, nchunks, width) in enumerate(
                [(dw1, NFC, FF), (dw2, NFC, FF), (dw3, NKC, HID)]):
            for c in range(nchunks):
                if width == FF:
                    wstrips.append((m, dmat, c, 0, 2048))
                    wstrips.append((m, dmat, c, 2048, 2048))
                    wstrips.append((m, dmat, c, 4096, 1536))
                else:
                    wstrips.append((m, dmat, c, 0, 2048))
        NSTRIPS = len(wstrips)
        ws_pos = [0]
        wg_pos = [0]

        def emit_strips(e, count):
            for _ in range(count):
                n = ws_pos[0]
                if n >= NSTRIPS:
                    return
                ws_pos[0] += 1
                m, dmat, c, off, wdt = wstrips[n]
                if n >= 2:
                    e.wait_ge(cW.h, n - 1)
                e.dma_start(out=wsp[n % 2][:, 0:wdt],
                            in_=dmat[c * 128:(c + 1) * 128, off:off + wdt]
                            ).then_inc(dW.h, 16)

        def emit_reduces(e, count, upto=None):
            cap = NSTRIPS if upto is None else min(upto, NSTRIPS)
            for _ in range(count):
                n = wg_pos[0]
                if n >= cap:
                    return
                wg_pos[0] += 1
                wdt = wstrips[n][4]
                e.wait_ge(dW.h, 16 * (n + 1))
                ins = e.reduce_sum(wsums[:, n:n + 1], wsp[n % 2][:, 0:wdt],
                                   axis=AX.X, apply_absolute_value=True)
                if n == NSTRIPS - 1:
                    e.drain().then_inc(cW.h, 1)
                else:
                    ins.then_inc(cW.h, 1)

        dS.n += 16
        p0_bw = dS.n
        PQ.add("sync", lambda e: e.dma_start(out=sm_t[0:16, 0:4], in_=dbw[:]
                                             ).then_inc(dS.h, 16))

        cG.n += 1
        p0_iota = cG.n
        dG.n += 16
        p0_norm = dG.n

        def _gp0(e):
            e.iota(iotaf[:], pattern=[[1, 128]], base=0, channel_multiplier=-1,
                   allow_small_or_imprecise_dtypes=True)
            e.drain().then_inc(cG.h, 1)
            e.dma_start(out=wa_b[:], in_=dmpn[:]).then_inc(dG.h, 16)
        PQ.add("gpsimd", _gp0)

        cV.n += 1
        p0_v1 = cV.n

        def _v0(e):
            e.wait_ge(cG.h, p0_iota)
            e.tensor_single_scalar(ident[:], iotaf[:], 0.0, ALU.is_equal)
            e.tensor_single_scalar(ident16[:], iotaf[:], 0.0, ALU.is_equal)
            e.memset(onesr[:], 1.0)
            e.memset(onescol[:], 1.0)
            e.memset(ccol[:, 0:1], 1e-5)
            e.memset(stf[:, 40:48], 0.0)
            e.memset(stf[:, 72:80], 0.0)
            e.wait_ge(dS.h, p0_bw)
            e.reduce_max(sm_t[0:16, 16:17], sm_t[0:16, 0:4], axis=AX.X)
            e.drain()
            e.tensor_scalar_mul(sm_t[0:16, 19:20], sm_t[0:16, 16:17], -1.0)
            e.wait_ge(dG.h, p0_norm)
            nb = wa_b[:].rearrange("p (h n) -> p h n", n=4)
            for n in range(4):
                e.reduce_sum(act4[:, n:n + 1], nb[:, :, n], axis=AX.X)
            e.drain().then_inc(cV.h, 1)
        PQ.add("vector", _v0)

        cS.n += 1
        p0_exp = cS.n

        def _s0(e):
            e.wait_ge(cV.h, p0_v1)
            e.activation(sm_t[0:16, 4:8], sm_t[0:16, 0:4], AF.Exp,
                         bias=sm_t[0:16, 19:20], scale=1.0)
            e.drain().then_inc(cS.h, 1)
        PQ.add("scalar", _s0)

        cT.n += 1
        p0_mm1 = cT.n

        def _t0(e):
            e.wait_ge(cV.h, p0_v1)
            e.matmul(pb[5][0:4, 0:1], act4[:], onescol[:], start=True, stop=True)
            e.drain().then_inc(cT.h, 1)
        PQ.add("tensor", _t0)

        cV.n += 1
        p0_v2 = cV.n

        def _v1(e):
            e.wait_ge(cS.h, p0_exp)
            e.reduce_sum(sm_t[0:16, 17:18], sm_t[0:16, 4:8], axis=AX.X)
            e.drain()
            e.reciprocal(sm_t[0:16, 18:19], sm_t[0:16, 17:18])
            e.drain()
            e.tensor_scalar(sm_t[0:16, 8:12], sm_t[0:16, 4:8],
                            sm_t[0:16, 18:19], None, ALU.mult)
            e.wait_ge(cT.h, p0_mm1)
            e.tensor_single_scalar(act4[0:4, 0:1], pb[5][0:4, 0:1], 1e-6,
                                   ALU.is_ge)
            e.drain().then_inc(cV.h, 1)
        PQ.add("vector", _v1)

        dG.n += 16
        p0_arow = dG.n

        def _gp1(e):
            e.wait_ge(cV.h, p0_v2)
            e.dma_start(out=smallrow[0:1, 0:4], in_=act4[0:4, 0:1],
                        allow_slow_non_contiguous=True).then_inc(dG.h, 16)
        PQ.add("gpsimd", _gp1)

        cT.n += 1
        p0_mm2 = cT.n

        def _t1(e):
            e.wait_ge(dG.h, p0_arow)
            e.matmul(pb[5][0:16, 4:8], onesr[0:1, 0:16], smallrow[0:1, 0:4],
                     start=True, stop=True)
            e.drain().then_inc(cT.h, 1)
        PQ.add("tensor", _t1)

        cV.n += 1
        p0_v3 = cV.n

        def _v2(e):
            e.wait_ge(cT.h, p0_mm2)
            e.tensor_tensor(sm_t[0:16, 12:16], sm_t[0:16, 8:12],
                            pb[5][0:16, 4:8], ALU.mult)
            e.drain().then_inc(cV.h, 1)
        PQ.add("vector", _v2)

        dG.n += 16
        p0_wrow = dG.n

        def _gp2(e):
            e.wait_ge(cV.h, p0_v3)
            e.dma_start(out=smallrow[0:1, 0:64], in_=sm_t[0:16, 12:16],
                        allow_slow_non_contiguous=True).then_inc(dG.h, 16)
        PQ.add("gpsimd", _gp2)

        cT.n += 1
        p0_mm3 = cT.n

        def _t2(e):
            e.wait_ge(dG.h, p0_wrow)
            e.matmul(pb[5][:, 8:72], onesr[:], smallrow[0:1, 0:64],
                     start=True, stop=True)
            e.drain().then_inc(cT.h, 1)
        PQ.add("tensor", _t2)

        cV.n += 1

        def _v3(e):
            e.wait_ge(cT.h, p0_mm3)
            e.tensor_copy(wa_b[:], pb[5][:, 8:72])
            e.drain().then_inc(cV.h, 1)
        PQ.add("vector", _v3)

        L1 = ExitStack()
        with L1:
            rqT = L1.enter_context(nc.sbuf_tensor("rqT", [128, NFC * TOK], BF16))
            L2 = ExitStack()
            with L2:
                sqT = L2.enter_context(nc.sbuf_tensor("sqT", [128, NH * TOK], BF16))
                mp16m = L2.enter_context(
                    nc.sbuf_tensor("mp16m", [128, NH * 512], BF16))
                nrm16 = L2.enter_context(nc.sbuf_tensor("nrm16", [128, 64], BF16))

                dG.n += 32
                d_mp = dG.n

                def _gpl(e):
                    e.dma_start(out=mp16m[:].rearrange("p (h c) -> p h c", c=512),
                                in_=dmpm_t[:]).then_inc(dG.h, 16)
                    e.dma_start(out=nrm16[:], in_=dmpn[:]).then_inc(dG.h, 16)
                PQ.add("gpsimd", _gpl)

                L3 = ExitStack()
                with L3:
                    sb3 = lambda *a, **k: L3.enter_context(nc.sbuf_tensor(*a, **k))
                    xa = sb3("xa", [128, HID], F32)
                    xsq = sb3("xsq", [128, HID], F32)
                    xn16 = [sb3("xn16_%d" % i, [128, HID], BF16) for i in range(2)]
                    rq16 = [sb3("rq16_%d" % i, [128, HID], BF16) for i in range(2)]
                    ub = sb3("ub", [128, HID], F32)
                    y0b = sb3("y0b", [128, HID], F32)
                    ywk = sb3("ywk", [128, HID], F32)
                    wrapc = xsq
                    xnT = sb3("xnT", [128, NFC * TOK], BF16)
                    wq_s = [sb3("wq_s%d" % i, [128, HID], BF16) for i in range(2)]
                    sigt = sb3("sigt", [128, 512], F32)
                    sige = sb3("sige", [128, 512], F32)

                    barrier()
                    d_x = [0] * NTT
                    v_var = [0] * NTT
                    s_ln = [0] * NTT
                    v_neg = [0] * NTT
                    s_rstd = [0] * NTT
                    v_p1 = [0] * NTT
                    t_tr = {}
                    s_tr = {}
                    for tt in range(NTT):
                        dS.n += 16
                        d_x[tt] = dS.n
                        v_var[tt] = inc(cV)
                        v_neg[tt] = inc(cV)
                        v_p1[tt] = inc(cV)
                    for tt in range(NTT):
                        for b_ in range(8):
                            t_tr[(tt, b_)] = inc(cT)
                    for tt in range(NTT):
                        s_ln[tt] = inc(cS)
                        s_rstd[tt] = inc(cS)
                        if tt >= 1:
                            for b_ in range(8):
                                s_tr[(tt - 1, b_)] = inc(cS)
                    for b_ in range(8):
                        s_tr[(NTT - 1, b_)] = inc(cS)
                    s_p1_done = inc(cS)

                    def _p1_sync(e):
                        for tt in range(NTT):
                            if tt >= 1:
                                e.wait_ge(cV.h, v_p1[tt - 1])
                            e.dma_start(out=xa[:],
                                        in_=dx[tt * 128:(tt + 1) * 128, :]
                                        ).then_inc(dS.h, 16)
                            if tt >= 1:
                                emit_strips(e, 6)
                        emit_strips(e, 4)
                    PQ.add("sync", _p1_sync)



                    def _p1_vec(e):
                        for tt in range(NTT):
                            e.wait_ge(dS.h, d_x[tt])
                            if tt >= 2:
                                e.wait_ge(cS.h, s_tr[(tt - 2, 7)])
                            e.tensor_tensor(xsq[:], xa[:], xa[:], ALU.mult)
                            e.reduce_sum(sttmp[:, 0:1], xa[:], axis=AX.X)
                            e.reduce_sum(sttmp[:, 1:2], xsq[:], axis=AX.X)
                            e.reduce_max(sttmp[:, 2:3], xa[:], axis=AX.X)
                            e.tensor_reduce(sttmp[:, 3:4], xa[:], AX.X, ALU.min)
                            e.drain()
                            e.tensor_scalar_mul(stf[:, 0 + tt:1 + tt],
                                                sttmp[:, 0:1], -1.0 / HID)
                            e.tensor_scalar_mul(sttmp[:, 6:7], sttmp[:, 1:2],
                                                1.0 / HID)
                            e.drain()
                            e.tensor_tensor(sttmp[:, 7:8], stf[:, 0 + tt:1 + tt],
                                            stf[:, 0 + tt:1 + tt], ALU.mult)
                            e.drain()
                            e.tensor_tensor(sttmp[:, 4:5], sttmp[:, 6:7],
                                            sttmp[:, 7:8], ALU.subtract)
                            e.drain().then_inc(cV.h, 1)
                            if tt >= 1:
                                emit_reduces(e, 6, upto=46)
                            e.wait_ge(cS.h, s_ln[tt])
                            e.tensor_scalar_mul(sttmp[:, 4:5], sttmp[:, 5:6], -0.5)
                            e.drain().then_inc(cV.h, 1)
                            e.wait_ge(cS.h, s_rstd[tt])
                            e.tensor_tensor(sttmp[:, 6:7], sttmp[:, 2:3],
                                            stf[:, 0 + tt:1 + tt], ALU.add)
                            e.tensor_tensor(sttmp[:, 7:8], sttmp[:, 3:4],
                                            stf[:, 0 + tt:1 + tt], ALU.add)
                            e.drain()
                            e.tensor_scalar_mul(sttmp[:, 5:6], sttmp[:, 7:8], -1.0)
                            e.drain()
                            e.tensor_tensor(sttmp[:, 2:3], sttmp[:, 6:7],
                                            sttmp[:, 5:6], ALU.max)
                            e.drain()
                            e.tensor_tensor(sttmp[:, 6:7], sttmp[:, 2:3],
                                            stf[:, 8 + tt:9 + tt], ALU.mult)
                            e.drain()
                            e.tensor_scalar_max(sttmp[:, 7:8], sttmp[:, 6:7], 1e-5)
                            e.drain()
                            e.reciprocal(sttmp[:, 6:7], sttmp[:, 7:8])
                            e.drain()
                            e.tensor_scalar_mul(stf[:, 16 + tt:17 + tt],
                                                sttmp[:, 6:7], 127.0)
                            e.drain()
                            e.reciprocal(stf[:, 24 + tt:25 + tt],
                                         stf[:, 16 + tt:17 + tt])
                            e.drain()
                            e.tensor_tensor(stf[:, 32 + tt:33 + tt],
                                            stf[:, 8 + tt:9 + tt],
                                            stf[:, 16 + tt:17 + tt], ALU.mult)
                            e.drain()
                            e.tensor_scalar(xn16[tt % 2][:], xa[:],
                                            stf[:, 0 + tt:1 + tt],
                                            stf[:, 8 + tt:9 + tt],
                                            ALU.add, ALU.mult)
                            e.tensor_scalar(ub[:], xa[:],
                                            stf[:, 0 + tt:1 + tt],
                                            stf[:, 32 + tt:33 + tt],
                                            ALU.add, ALU.mult)
                            e.tensor_single_scalar(y0b[:], ub[:], 128.5, ALU.add)
                            src = y0b
                            dsts = [ywk, ub]
                            for wi, P in enumerate(LADDER):
                                dst = dsts[wi % 2]
                                e.tensor_scalar(wrapc[:], src[:], P, -P,
                                                ALU.is_gt, ALU.mult)
                                e.tensor_tensor(dst[:], src[:], wrapc[:], ALU.add)
                                src = dst
                            e.scalar_tensor_tensor(rq16[tt % 2][:], y0b[:], 128.0,
                                                   src[:], ALU.subtract,
                                                   ALU.subtract)
                            e.drain().then_inc(cV.h, 1)
                    PQ.add("vector", _p1_vec)

                    def _p1_evac(e, tt):
                        for b_ in range(8):
                            srcsel, g = divmod(b_, 4)
                            dstT = xnT if srcsel == 0 else rqT
                            bank = pb[6 + ((tt * 8 + b_) % 2)]
                            e.wait_ge(cT.h, t_tr[(tt, b_)])
                            dv = dstT[:].rearrange("p (c t) -> p c t", t=TOK)
                            e.activation(
                                dv[:, 4 * g:4 * g + 4, tt * 128:(tt + 1) * 128],
                                bank[:, 0:512], AF.Identity
                                ).then_inc(cS.h, 1)

                    def _p1_scal(e):
                        for tt in range(NTT):
                            e.wait_ge(cV.h, v_var[tt])
                            e.activation(sttmp[:, 5:6], sttmp[:, 4:5], AF.Ln,
                                         bias=ccol[:, 0:1], scale=1.0)
                            e.drain().then_inc(cS.h, 1)
                            e.wait_ge(cV.h, v_neg[tt])
                            e.activation(stf[:, 8 + tt:9 + tt], sttmp[:, 4:5],
                                         AF.Exp, scale=1.0)
                            e.drain().then_inc(cS.h, 1)
                            if tt >= 1:
                                _p1_evac(e, tt - 1)
                        _p1_evac(e, NTT - 1)
                        e.drain().then_inc(cS.h, 1)
                    PQ.add("scalar", _p1_scal)

                    def _p1_tens(e):
                        for tt in range(NTT):
                            e.wait_ge(cV.h, v_p1[tt])
                            for b_ in range(8):
                                srcsel, g = divmod(b_, 4)
                                src16 = xn16[tt % 2] if srcsel == 0 else rq16[tt % 2]
                                bank = pb[6 + ((tt * 8 + b_) % 2)]
                                gb = tt * 8 + b_
                                if gb >= 2:
                                    pv = gb - 2
                                    e.wait_ge(cS.h, s_tr[(pv // 8, pv % 8)])
                                for j in range(4):
                                    c = 4 * g + j
                                    e.matmul(bank[:, j * 128:(j + 1) * 128],
                                             src16[:, c * 128:(c + 1) * 128],
                                             ident[:], start=True, stop=True)
                                e.drain().then_inc(cT.h, 1)
                    PQ.add("tensor", _p1_tens)

                    dbg_dump([("dbg_xnT", xnT[:]), ("dbg_rqT", rqT[:]),
                              ("dbg_stf1", stf[:])])
                    barrier()
                    d_wq = [0] * NFC
                    t_qp = [0] * NFC
                    v_min = {}
                    s_sig = {}
                    v_done = {}
                    for o in range(NFC):
                        dG.n += 16
                        d_wq[o] = dG.n
                        t_qp[o] = inc(cT)
                        for half in range(2):
                            v_min[(o, half)] = inc(cV)
                            v_done[(o, half)] = inc(cV)
                    v_p2a_done = inc(cV)
                    for o in range(NFC):
                        for half in range(2):
                            s_sig[(o, half)] = inc(cS)

                    def _p2a_gp(e):
                        for o in range(NFC):
                            if o >= 2:
                                e.wait_ge(cT.h, t_qp[o - 2])
                            e.dma_start(
                                out=wq_s[o % 2][:].rearrange("p (c o2) -> p c o2",
                                                             o2=128),
                                in_=dwq_c[:, :, o * 128:(o + 1) * 128]
                            ).then_inc(dG.h, 16)
                    PQ.add("gpsimd", _p2a_gp)

                    def _p2a_sync(e):
                        emit_strips(e, 50)
                    PQ.add("sync", _p2a_sync)

                    def _p2a_t(e):
                        for o in range(NFC):
                            e.wait_ge(dG.h, d_wq[o])
                            if o >= 2:
                                e.wait_ge(cV.h, v_done[(o - 2, 1)])
                            pa, pbk = pb[2 * (o % 2)], pb[2 * (o % 2) + 1]
                            for i in range(NFC):
                                lhsT = wq_s[o % 2][:, i * 128:(i + 1) * 128]
                                e.matmul(pa[:, 0:512], lhsT,
                                         xnT[:, i * TOK:i * TOK + 512],
                                         start=(i == 0), stop=(i == NFC - 1))
                                e.matmul(pbk[:, 0:512], lhsT,
                                         xnT[:, i * TOK + 512:(i + 1) * TOK],
                                         start=(i == 0), stop=(i == NFC - 1))
                            e.drain().then_inc(cT.h, 1)
                    PQ.add("tensor", _p2a_t)

                    def _p2a_v(e):
                        for o in range(NFC):
                            e.wait_ge(cT.h, t_qp[o])
                            for half in range(2):
                                pban = pb[2 * (o % 2) + half]
                                e.tensor_scalar_min(sigt[:], pban[:, 0:512], 0.0)
                                e.drain().then_inc(cV.h, 1)
                                emit_reduces(e, 3, upto=96)
                                e.wait_ge(cS.h, s_sig[(o, half)])
                                e.scalar_tensor_tensor(
                                    sqT[:, o * TOK + half * 512:
                                        o * TOK + (half + 1) * 512],
                                    pban[:, 0:512], 0.0, sige[:],
                                    ALU.max, ALU.add
                                    ).then_inc(cV.h, 1)
                        e.drain().then_inc(cV.h, 1)
                    PQ.add("vector", _p2a_v)

                    def _p2a_s(e):
                        for o in range(NFC):
                            for half in range(2):
                                e.wait_ge(cV.h, v_min[(o, half)])
                                e.activation(sige[:], sigt[:], AF.Exp)
                                e.drain().then_inc(cS.h, 1)
                    PQ.add("scalar", _p2a_s)

                L3b = ExitStack()
                with L3b:
                    sb3b = lambda *a, **k: L3b.enter_context(nc.sbuf_tensor(*a, **k))
                    combT = sb3b("combT", [128, NFC * TOK], BF16)
                    comb16 = [sb3b("comb16_%d" % i, [128, HID], BF16)
                              for i in range(2)]
                    cadd = sb3b("cadd", [128, 256], F32)
                    wo_s = [sb3b("wo_s%d" % i, [128, HID], BF16) for i in range(2)]
                    xt_b = [sb3b("xt_b%d" % i, [128, TOK], F32) for i in range(2)]
                    ot_b = [sb3b("ot_b%d" % i, [128, TOK], F32) for i in range(2)]

                    dbg_dump([("dbg_sqT", sqT[:]), ("dbg_wab", wa_b[:])])
                    barrier()
                    t_nrm = [0] * NTT
                    v_rn = [0] * NTT
                    t_A = {}
                    v_cb = {}
                    v_cbd = [0] * NTT
                    t_ct = {}
                    s_ct = {}
                    for tt in range(NTT):
                        t_nrm[tt] = inc(cT)
                        for h in range(NH):
                            t_A[(tt, h)] = inc(cT)
                        for g in range(4):
                            t_ct[(tt, g)] = inc(cT)
                    for tt in range(NTT):
                        v_rn[tt] = inc(cV)
                        for h in range(NH):
                            v_cb[(tt, h)] = inc(cV)
                        v_cbd[tt] = inc(cV)
                    for tt in range(NTT):
                        for g in range(4):
                            s_ct[(tt, g)] = inc(cS)
                    s_p2b_done = inc(cS)

                    def _p2b_sync(e):
                        emit_strips(e, NSTRIPS)
                    PQ.add("sync", _p2b_sync)

                    def _p2b_t(e):
                        for tt in range(NTT):
                            if tt >= 1:
                                e.wait_ge(cV.h, v_rn[tt - 1])
                            for h in range(NH):
                                sq_blk = sqT[:, h * TOK + tt * 128:
                                             h * TOK + (tt + 1) * 128]
                                e.matmul(pb[4][:, 4 * h:4 * h + 4], sq_blk,
                                         nrm16[:, 4 * h:4 * h + 4],
                                         start=True, stop=True)
                            e.drain().then_inc(cT.h, 1)
                            for h in range(NH):
                                step = tt * NH + h
                                if step >= 4:
                                    pvs = step - 4
                                    e.wait_ge(cV.h, v_cb[(pvs // NH, pvs % NH)])
                                sq_blk = sqT[:, h * TOK + tt * 128:
                                             h * TOK + (tt + 1) * 128]
                                e.matmul(pb[step % 4][:, 0:512], sq_blk,
                                         mp16m[:, h * 512:(h + 1) * 512],
                                         start=True, stop=True)
                                e.drain().then_inc(cT.h, 1)
                            e.wait_ge(cV.h, v_cbd[tt])
                            for g in range(4):
                                ctb = tt * 4 + g
                                bank = pb[6 + (ctb % 2)]
                                if ctb >= 2:
                                    pv = ctb - 2
                                    e.wait_ge(cS.h, s_ct[(pv // 4, pv % 4)])
                                for j in range(4):
                                    c = 4 * g + j
                                    e.matmul(bank[:, j * 128:(j + 1) * 128],
                                             comb16[tt % 2][:, c * 128:(c + 1) * 128],
                                             ident[:], start=True, stop=True)
                                e.drain().then_inc(cT.h, 1)
                    PQ.add("tensor", _p2b_t)

                    def _p2b_v(e):
                        for tt in range(NTT):
                            emit_reduces(e, 6)
                            e.wait_ge(cT.h, t_nrm[tt])
                            if tt >= 2:
                                e.wait_ge(cS.h, s_ct[(tt - 2, 3)])
                            e.tensor_scalar_max(rn64[:], pb[4][:, 0:64], 1e-6)
                            e.drain()
                            e.reciprocal(rn64b[:], rn64[:])
                            e.drain()
                            e.tensor_tensor(rn64[:], rn64b[:], wa_b[:], ALU.mult)
                            e.drain().then_inc(cV.h, 1)
                            for h in range(NH):
                                e.wait_ge(cT.h, t_A[(tt, h)])
                                pA = pb[(tt * NH + h) % 4]
                                e.tensor_scalar(cadd[:, 0:128], pA[:, 0:128],
                                                rn64[:, 4 * h:4 * h + 1], None,
                                                ALU.mult)
                                e.scalar_tensor_tensor(cadd[:, 128:256],
                                                       pA[:, 128:256],
                                                       rn64[:, 4 * h + 1:4 * h + 2],
                                                       cadd[:, 0:128],
                                                       ALU.mult, ALU.add)
                                e.scalar_tensor_tensor(cadd[:, 0:128],
                                                       pA[:, 256:384],
                                                       rn64[:, 4 * h + 2:4 * h + 3],
                                                       cadd[:, 128:256],
                                                       ALU.mult, ALU.add)
                                e.scalar_tensor_tensor(
                                    comb16[tt % 2][:, h * 128:(h + 1) * 128],
                                    pA[:, 384:512],
                                    rn64[:, 4 * h + 3:4 * h + 4],
                                    cadd[:, 0:128], ALU.mult, ALU.add
                                    ).then_inc(cV.h, 1)
                            e.drain().then_inc(cV.h, 1)
                    PQ.add("vector", _p2b_v)

                    def _p2b_s(e):
                        for tt in range(NTT):
                            for g in range(4):
                                e.wait_ge(cT.h, t_ct[(tt, g)])
                                bank = pb[6 + ((tt * 4 + g) % 2)]
                                dv = combT[:].rearrange("p (c t) -> p c t", t=TOK)
                                e.activation(
                                    dv[:, 4 * g:4 * g + 4, tt * 128:(tt + 1) * 128],
                                    bank[:, 0:512], AF.Identity
                                    ).then_inc(cS.h, 1)
                        e.drain().then_inc(cS.h, 1)
                    PQ.add("scalar", _p2b_s)

                    dbg_dump([("dbg_combT", combT[:])])
                    barrier()
                    d_wo = [0] * NFC
                    d_xt = [0] * NFC
                    t_wo = [0] * NFC
                    v_wo = [0] * NFC
                    d_park = [0] * NFC
                    for o in range(NFC):
                        dG.n += 16
                        d_wo[o] = dG.n
                        dS.n += 16
                        d_xt[o] = dS.n
                        dS.n += 16
                        d_park[o] = dS.n
                        t_wo[o] = inc(cT)
                        v_wo[o] = inc(cV)

                    def _p2c_gp(e):
                        for o in range(NFC):
                            if o >= 2:
                                e.wait_ge(cT.h, t_wo[o - 2])
                            e.dma_start(
                                out=wo_s[o % 2][:].rearrange("p (c o2) -> p c o2",
                                                             o2=128),
                                in_=dwo_c[:, :, o * 128:(o + 1) * 128]
                            ).then_inc(dG.h, 16)
                    PQ.add("gpsimd", _p2c_gp)

                    def _p2c_sync_ws(e):
                        emit_strips(e, NSTRIPS)
                    PQ.add("sync", _p2c_sync_ws)

                    def _p2c_sync(e):
                        for o in range(NFC):
                            if o >= 2:
                                e.wait_ge(cV.h, v_wo[o - 2])
                            e.dma_start(out=xt_b[o % 2][:],
                                        in_=dxT[o * 128:(o + 1) * 128, :]
                                        ).then_inc(dS.h, 16)
                            e.wait_ge(cV.h, v_wo[o])
                            e.dma_start(out=dout[o * 128:(o + 1) * 128, :],
                                        in_=ot_b[o % 2][:]).then_inc(dS.h, 16)
                    PQ.add("sync", _p2c_sync)

                    def _p2c_t(e):
                        for o in range(NFC):
                            e.wait_ge(dG.h, d_wo[o])
                            if o >= 2:
                                e.wait_ge(cV.h, v_wo[o - 2])
                            pa, pbk = pb[2 * (o % 2)], pb[2 * (o % 2) + 1]
                            for k in range(NFC):
                                lhsT = wo_s[o % 2][:, k * 128:(k + 1) * 128]
                                e.matmul(pa[:, 0:512], lhsT,
                                         combT[:, k * TOK:k * TOK + 512],
                                         start=(k == 0), stop=(k == NFC - 1))
                                e.matmul(pbk[:, 0:512], lhsT,
                                         combT[:, k * TOK + 512:(k + 1) * TOK],
                                         start=(k == 0), stop=(k == NFC - 1))
                            e.drain().then_inc(cT.h, 1)
                    PQ.add("tensor", _p2c_t)

                    def _p2c_v(e):
                        for o in range(NFC):
                            emit_reduces(e, 3)
                            e.wait_ge(cT.h, t_wo[o])
                            e.wait_ge(dS.h, d_xt[o])
                            if o >= 2:
                                e.wait_ge(dS.h, d_park[o - 2])
                            pa, pbk = pb[2 * (o % 2)], pb[2 * (o % 2) + 1]
                            e.tensor_tensor(ot_b[o % 2][:, 0:512],
                                            xt_b[o % 2][:, 0:512],
                                            pa[:, 0:512], ALU.add)
                            e.tensor_tensor(ot_b[o % 2][:, 512:1024],
                                            xt_b[o % 2][:, 512:1024],
                                            pbk[:, 0:512], ALU.add)
                            e.drain().then_inc(cV.h, 1)
                    PQ.add("vector", _p2c_v)

            L3c = ExitStack()
            with L3c:
                barrier()
                mranges = [(0, 48), (48, 96), (96, 140)]
                t_sum = [0] * 3
                v_sw1 = [0] * 3
                t_bc = [0] * 3
                v_sw2 = [0] * 3
                v_wsum = [0] * 3
                for m in range(3):
                    v_wsum[m] = inc(cV)
                    v_sw1[m] = inc(cV)
                    v_sw2[m] = inc(cV)
                for m in range(3):
                    t_sum[m] = inc(cT)
                    t_bc[m] = inc(cT)

                def _p3_v(e):
                    emit_reduces(e, NSTRIPS)
                    e.drain()
                    for m in range(3):
                        lo, hi = mranges[m]
                        e.reduce_sum(wsrun[:, 0:1], wsums[:, lo:hi], axis=AX.X)
                        e.drain().then_inc(cV.h, 1)
                        e.wait_ge(cT.h, t_sum[m])
                        e.tensor_scalar_mul(sc11[0:1, 0:1], pb[5][0:1, 0:1],
                                            1.0 / WMEAN_N)
                        e.drain()
                        e.tensor_scalar_max(sc11[0:1, 1:2], sc11[0:1, 0:1], 1e-5)
                        e.drain()
                        e.reciprocal(sc11[0:1, 2:3], sc11[0:1, 1:2])
                        e.drain().then_inc(cV.h, 1)
                        e.wait_ge(cT.h, t_bc[m])
                        e.tensor_copy(swcol[:, 3 * m:3 * m + 1], pb[5][:, 100:101])
                        e.tensor_scalar_mul(swcol[:, 3 * m + 1:3 * m + 2],
                                            pb[5][:, 100:101], -1.0)
                        e.tensor_copy(swcol[:, 3 * m + 2:3 * m + 3],
                                      pb[5][:, 101:102])
                        e.tensor_scalar_mul(swcol[:, 9 + 2 * m:10 + 2 * m],
                                            pb[5][:, 101:102], 0.5)
                        e.tensor_scalar_mul(swcol[:, 10 + 2 * m:11 + 2 * m],
                                            pb[5][:, 101:102], -0.5)
                        e.drain()
                        if m < 2:
                            e.tensor_scalar(stf[:, 56 + 8 * m:64 + 8 * m],
                                            stf[:, 24:32],
                                            swcol[:, 3 * m + 2:3 * m + 3], None,
                                            ALU.mult)
                        e.drain().then_inc(cV.h, 1)
                PQ.add("vector", _p3_v)

                def _p3_t(e):
                    for m in range(3):
                        e.wait_ge(cV.h, v_wsum[m])
                        e.matmul(pb[5][0:1, 0:1],
                                 wsrun[:, 0:1], onescol[:],
                                 start=True, stop=True)
                        e.drain().then_inc(cT.h, 1)
                        e.wait_ge(cV.h, v_sw1[m])
                        e.matmul(pb[5][:, 100:101], onesr[:], sc11[0:1, 2:3],
                                 start=True, stop=True)
                        e.matmul(pb[5][:, 101:102], onesr[:], sc11[0:1, 1:2],
                                 start=True, stop=True)
                        e.drain().then_inc(cT.h, 1)
                PQ.add("tensor", _p3_t)

            L4 = ExitStack()
            with L4:
                hTa = L4.enter_context(nc.sbuf_tensor("hTa", [128, 16 * TOK], F16))
                hTb = L4.enter_context(nc.sbuf_tensor("hTb", [128, 28 * TOK], F16))

                def hT(k):
                    if k < 16:
                        return hTa[:, k * TOK:(k + 1) * TOK]
                    return hTb[:, (k - 16) * TOK:(k - 15) * TOK]

                def hTpair(kk, tt):
                    base = hTa if kk < 16 else hTb
                    k0 = kk if kk < 16 else kk - 16
                    dv = base[:].rearrange("p (k t) -> p k t", t=TOK)
                    return dv[:, k0:k0 + 2, tt * 128:(tt + 1) * 128]

                L5 = ExitStack()
                with L5:
                    sb5 = lambda *a, **k: L5.enter_context(nc.sbuf_tensor(*a, **k))
                    t1s = [sb5("t1s%d" % i, [128, NFC * OS], BF16) for i in range(2)]
                    t2s = [sb5("t2s%d" % i, [128, NFC * OS], BF16) for i in range(2)]
                    raw1 = [sb5("raw1%d" % i, [128, 1024], F32) for i in range(2)]
                    raw2 = [sb5("raw2%d" % i, [128, 1024], F32) for i in range(2)]
                    qa = sb5("qa", [128, 512], BF16)
                    qb = sb5("qb", [128, 512], BF16)
                    pby1 = [pb[0], pb[1], pb[4]]
                    pby2 = [pb[2], pb[3], pb[5]]
                    tmp1 = [sb5("tmp1%d" % i, [128, OS], F32) for i in range(4)]
                    tmp2 = [sb5("tmp2%d" % i, [128, OS], F32) for i in range(4)]
                    hsl = [sb5("hsl%d" % i, [128, OS], F16) for i in range(6)]

                    barrier()
                    d_r1 = {}
                    d_r2 = {}
                    v_q = {}
                    t_y = {}
                    s_y = {}
                    v_h = {}
                    t_h = {}
                    s_h = {}
                    for o in range(NOS):
                        for dg in range(4):
                            dS.n += 16
                            d_r1[o * 4 + dg] = dS.n
                            dS2.n += 16
                            d_r2[o * 4 + dg] = dS2.n
                    for o in range(NOS + 1):
                        if o < NOS:
                            for g in range(8):
                                v_q[o * 8 + g] = inc(cV)
                        if o >= 1:
                            for tt in range(NTT):
                                v_h[(o - 1, tt)] = inc(cV)
                    for step in range(NOS * NTT):
                        o, tt = divmod(step, NTT)
                        t_y[(o, tt)] = inc(cT)
                        if step >= 5:
                            po, ptt = divmod(step - 5, NTT)
                            t_h[(po, ptt)] = inc(cT)
                    for ps in range(NOS * NTT - 5, NOS * NTT):
                        t_h[(ps // 8, ps % 8)] = inc(cT)
                    for step in range(NOS * NTT):
                        s_y[(step // 8, step % 8)] = inc(cS)
                        if step >= 4:
                            s_h[((step - 4) // 8, (step - 4) % 8)] = inc(cS)
                    for ps in range(NOS * NTT - 4, NOS * NTT):
                        s_h[(ps // 8, ps % 8)] = inc(cS)
                    s_p4_done = inc(cS)

                    def _p4_sync(e):
                        for o in range(NOS):
                            for dg in range(4):
                                dgidx = o * 4 + dg
                                if dgidx >= 2:
                                    e.wait_ge(cV.h, v_q[(dgidx - 2) * 2 + 1])
                                e.dma_start(
                                    out=raw1[dgidx % 2][:].rearrange(
                                        "p (c f) -> p c f", f=OS),
                                    in_=dw1_c[:, 4 * dg:4 * dg + 4,
                                              o * OS:(o + 1) * OS]
                                ).then_inc(dS.h, 16)
                    PQ.add("sync", _p4_sync)

                    def _p4_gp(e):
                        for o in range(NOS):
                            for dg in range(4):
                                dgidx = o * 4 + dg
                                if dgidx >= 2:
                                    e.wait_ge(cV.h, v_q[(dgidx - 2) * 2 + 1])
                                e.dma_start(
                                    out=raw2[dgidx % 2][:].rearrange(
                                        "p (c f) -> p c f", f=OS),
                                    in_=dw2_c[:, 4 * dg:4 * dg + 4,
                                              o * OS:(o + 1) * OS]
                                ).then_inc(dS2.h, 16)
                    PQ.add("gpsimd", _p4_gp)

                    def _p4_s(e):
                        nstep = NOS * NTT

                        def hTev(ps):
                            po, ptt = divmod(ps, NTT)
                            e.wait_ge(cT.h, t_h[(po, ptt)] +
                                      (1 if ps < nstep - 2 else 0))
                            tco = 256 * ((ps // 2) % 2)
                            bank = pb[6 + (ps % 2)]
                            e.activation(hTpair(2 * po, ptt),
                                         bank[:, tco:tco + 256],
                                         AF.Identity).then_inc(cS.h, 1)

                        for step in range(nstep):
                            o, tt = divmod(step, NTT)
                            if step < nstep - 1:
                                nx = step + 1
                                e.wait_ge(cT.h, t_y[(nx // 8, nx % 8)])
                            else:
                                e.wait_ge(cT.h, t_y[(o, tt)])
                            if step >= 4:
                                prev = step - 4
                                e.wait_ge(cV.h, v_h[(prev // 8, prev % 8)])
                            pY1 = pby1[step % 3]
                            pY2 = pby2[step % 3]
                            e.activation(tmp1[step % 4][:], pY1[:, 0:OS],
                                         AF.Silu, scale=stf[:, 56 + tt:57 + tt])
                            e.activation(tmp2[step % 4][:], pY2[:, 0:OS],
                                         AF.Identity,
                                         scale=stf[:, 64 + tt:65 + tt])
                            e.drain().then_inc(cS.h, 1)
                            if step >= 4:
                                hTev(step - 4)
                        for ps in range(nstep - 4, nstep):
                            hTev(ps)
                        e.drain().then_inc(cS.h, 1)
                    PQ.add("scalar", _p4_s)

                    def _p4_v(e):
                        for o in range(NOS + 1):
                            if o < NOS:
                                for g in range(8):
                                    gidx = o * 8 + g
                                    dgidx = o * 4 + g // 2
                                    if g % 2 == 0:
                                        e.wait_ge(dS.h, d_r1[dgidx])
                                        e.wait_ge(dS2.h, d_r2[dgidx])
                                    if o >= 2 and g == 0:
                                        e.wait_ge(cT.h, t_y[(o - 2, NTT - 1)])
                                    half = (g % 2) * 512
                                    for mat, rawb, tds, thrc in (
                                            (0, raw1[dgidx % 2], t1s[o % 2], 9),
                                            (1, raw2[dgidx % 2], t2s[o % 2], 11)):
                                        raw = rawb[:, half:half + 512]
                                        e.tensor_scalar(
                                            qa[:], raw,
                                            swcol[:, thrc + 1:thrc + 2],
                                            None, ALU.is_lt)
                                        ins = e.scalar_tensor_tensor(
                                            tds[:, 2 * g * OS:(2 * g + 2) * OS],
                                            raw, swcol[:, thrc:thrc + 1],
                                            qa[:], ALU.is_gt, ALU.subtract)
                                    if g == 7:
                                        e.drain().then_inc(cV.h, 1)
                                    else:
                                        ins.then_inc(cV.h, 1)
                            if o >= 1:
                                oo = o - 1
                                for tt in range(NTT):
                                    step = oo * 8 + tt
                                    e.wait_ge(cS.h, s_y[(oo, tt)])
                                    if step >= 6:
                                        prev = step - 6
                                        e.wait_ge(cS.h, s_h[(prev // 8, prev % 8)])
                                    e.tensor_tensor(hsl[step % 6][:],
                                                    tmp1[step % 4][:],
                                                    tmp2[step % 4][:], ALU.mult)
                                    e.reduce_max(
                                        hmx[:, tt * 22 + oo:tt * 22 + oo + 1],
                                        hsl[step % 6][:], axis=AX.X,
                                        apply_absolute_value=True)
                                    e.drain().then_inc(cV.h, 1)
                    PQ.add("vector", _p4_v)

                    def _p4_t(e):
                        nstep = NOS * NTT

                        def trans(ps):
                            po, ptt = divmod(ps, NTT)
                            e.wait_ge(cV.h, v_h[(po, ptt)])
                            if ps >= 2:
                                p2 = ps - 2
                                e.wait_ge(cS.h, s_h[(p2 // 8, p2 % 8)])
                            tco = 256 * ((ps // 2) % 2)
                            bank = pb[6 + (ps % 2)]
                            for j in range(2):
                                ins = e.matmul(
                                    bank[:, tco + j * 128:tco + (j + 1) * 128],
                                    hsl[ps % 6][:, j * 128:(j + 1) * 128],
                                    ident16[:], start=True, stop=True)
                            if ps >= nstep - 3:
                                e.drain().then_inc(cT.h, 1)
                            else:
                                ins.then_inc(cT.h, 1)

                        for step in range(nstep):
                            o, tt = divmod(step, NTT)
                            e.wait_ge(cV.h, v_q[o * 8 + 7])
                            if step >= 3:
                                prev = step - 3
                                e.wait_ge(cS.h, s_y[(prev // 8, prev % 8)])
                            pY1 = pby1[step % 3]
                            pY2 = pby2[step % 3]
                            for i in range(NFC):
                                lhsT = rqT[:, i * TOK + tt * 128:
                                           i * TOK + (tt + 1) * 128]
                                e.matmul(pY1[:, 0:OS], lhsT,
                                         t1s[o % 2][:, i * OS:(i + 1) * OS],
                                         start=(i == 0), stop=(i == NFC - 1))
                                ins = e.matmul(pY2[:, 0:OS], lhsT,
                                               t2s[o % 2][:, i * OS:(i + 1) * OS],
                                               start=(i == 0), stop=(i == NFC - 1))
                            if step == nstep - 1:
                                e.drain().then_inc(cT.h, 1)
                            else:
                                ins.then_inc(cT.h, 1)
                            if step >= 5:
                                trans(step - 5)
                        for ps in range(nstep - 5, nstep):
                            trans(ps)
                    PQ.add("tensor", _p4_t)

                dbg_dump([("dbg_hTa", hTa[:]), ("dbg_hTb", hTb[:]),
                          ("dbg_stf2", stf[:]), ("dbg_sw", swcol[:])])
                L6 = ExitStack()
                with L6:
                    sb6 = lambda *a, **k: L6.enter_context(nc.sbuf_tensor(*a, **k))
                    p56 = sb6("p56", [128, 5632], F32)
                    lad6 = sb6("lad6", [128, 2048], F32)
                    xt6 = [sb6("xt6_%d" % i, [128, 512], F32) for i in range(2)]
                    ev6 = sb6("ev6", [128, 512], F32)
                    fin6 = sb6("fin6", [128, 512], F32)
                    qa3 = sb6("qa3", [128, 2816], F16)
                    srow = sb6("srow", [1, 2048], F32)
                    s3r_b = sb6("s3r_b", [128, 1024], F32)
                    qs2r_b = sb6("qs2r_b", [128, 1024], F32)
                    t3s = [rqT[:, i * FF:(i + 1) * FF].bitcast(F16)
                           for i in range(2)]

                    def lad_chunk(e, k, c0, scr, bare=True):
                        u = scr[:, 0:512]
                        y0 = scr[:, 512:1024]
                        w1_ = scr[:, 1024:1536]
                        w2_ = scr[:, 1536:2048]
                        hk = hT(k)[:, c0:c0 + 512]
                        e.tensor_tensor(w1_[:], hk, qs2r_b[:, c0:c0 + 512],
                                        ALU.mult)
                        e.tensor_single_scalar(y0[:], w1_[:], 128.5, ALU.add)
                        srk = y0
                        dsts = [w1_, w2_]
                        for wi, P in enumerate(LADDER):
                            dst = dsts[wi % 2]
                            e.tensor_scalar(u[:], srk[:], P, -P,
                                            ALU.is_gt, ALU.mult)
                            e.tensor_tensor(dst[:], srk[:], u[:], ALU.add)
                            srk = dst
                        ins = e.scalar_tensor_tensor(hk, y0[:], 128.0, srk[:],
                                                     ALU.subtract, ALU.subtract)
                        if bare:
                            ins.then_inc(cV.h, 1)
                        else:
                            e.drain().then_inc(cV.h, 1)

                    barrier()
                    v_qs2 = inc(cV)
                    dG.n += 16 * 16
                    d_srow = dG.n
                    t_row = inc(cT)
                    v_rows = inc(cV)
                    for k in range(NKC):
                        inc(cV)
                    v_p5_done = inc(cV)
                    d_t3 = {}
                    dS.n += 16
                    d_t3[(0, 0, 0)] = dS.n
                    dS2.n += 16
                    d_t3[(0, 0, 1)] = dS2.n

                    def _p5_v(e):
                        for tt in range(NTT):
                            e.reduce_max(stf[:, 40 + tt:41 + tt],
                                         hmx[:, tt * 22:(tt + 1) * 22],
                                         axis=AX.X, apply_absolute_value=True)
                        e.drain()
                        e.tensor_scalar_max(stf[:, 88:96], stf[:, 40:48], 1e-5)
                        e.drain()
                        e.reciprocal(stf[:, 72:80], stf[:, 88:96])
                        e.drain()
                        e.tensor_scalar_mul(stf[:, 48:56], stf[:, 72:80], 127.0)
                        e.drain()
                        e.reciprocal(stf[:, 72:80], stf[:, 48:56])
                        e.drain()
                        e.tensor_scalar(stf[:, 80:88], stf[:, 72:80],
                                        swcol[:, 8:9], None, ALU.mult)
                        e.drain().then_inc(cV.h, 1)
                        e.wait_ge(cT.h, t_row)
                        e.tensor_copy(qs2r_b[:, 0:512], pb[4][:, 0:512])
                        e.tensor_copy(qs2r_b[:, 512:1024], pb[5][:, 0:512])
                        e.tensor_copy(s3r_b[:, 0:512], pb[6][:, 0:512])
                        e.tensor_copy(s3r_b[:, 512:1024], pb[7][:, 0:512])
                        e.drain().then_inc(cV.h, 1)
                        for k in range(NKC):
                            lad_chunk(e, k, 0, lad6)
                        e.drain().then_inc(cV.h, 1)
                    PQ.add("vector", _p5_v)

                    def _p5_gp(e):
                        e.dma_start(
                            out=p56[:, 2816:5632].rearrange(
                                "p (c f) -> p c f", f=128),
                            in_=dw3_c[:, 22:44, 0:128]).then_inc(dS2.h, 16)
                        e.wait_ge(cV.h, v_qs2)
                        for tt in range(NTT):
                            e.dma_start(out=srow[0:1, tt * 128:(tt + 1) * 128],
                                        in_=stf[:, 48 + tt:49 + tt],
                                        allow_slow_non_contiguous=True
                                        ).then_inc(dG.h, 16)
                            e.dma_start(out=srow[0:1, 1024 + tt * 128:
                                                1152 + tt * 128],
                                        in_=stf[:, 80 + tt:81 + tt],
                                        allow_slow_non_contiguous=True
                                        ).then_inc(dG.h, 16)
                    PQ.add("gpsimd", _p5_gp)

                    def _p5_sync(e):
                        e.dma_start(
                            out=p56[:, 0:2816].rearrange(
                                "p (c f) -> p c f", f=128),
                            in_=dw3_c[:, 0:22, 0:128]).then_inc(dS.h, 16)
                    PQ.add("sync", _p5_sync)

                    def _p5_t(e):
                        e.wait_ge(dG.h, d_srow)
                        e.matmul(pb[4][:, 0:512], onesr[:], srow[0:1, 0:512],
                                 start=True, stop=True)
                        e.matmul(pb[5][:, 0:512], onesr[:], srow[0:1, 512:1024],
                                 start=True, stop=True)
                        e.matmul(pb[6][:, 0:512], onesr[:], srow[0:1, 1024:1536],
                                 start=True, stop=True)
                        e.matmul(pb[7][:, 0:512], onesr[:], srow[0:1, 1536:2048],
                                 start=True, stop=True)
                        e.drain().then_inc(cT.h, 1)
                    PQ.add("tensor", _p5_t)

                    dbg_dump([("dbg_q2a", hTa[:]), ("dbg_q2b", hTb[:]),
                              ("dbg_qs2r", qs2r_b[:]), ("dbg_s3r", s3r_b[:])])
                    barrier()
                    sched = [3] * 12 + [2] * 4
                    slot_ks = []
                    kk = 0
                    for o3 in range(NFC):
                        slot_ks.append(list(range(kk, kk + sched[o3])))
                        kk += sched[o3]
                    assert kk == NKC
                    d_x6 = {}
                    d_fst = {}
                    v_q3 = {}
                    v_ladB = {}
                    v_f = {}
                    t_y3 = {}
                    for p in range(2):
                        for o3 in range(NFC):
                            if (p, o3) == (0, 0):
                                continue
                            dS.n += 16
                            d_t3[(p, o3, 0)] = dS.n
                            dS2.n += 16
                            d_t3[(p, o3, 1)] = dS2.n
                    for p in range(2):
                        for o3 in range(NFC):
                            dA.n += 16
                            d_x6[(p, o3)] = dA.n
                            dA.n += 16
                            d_fst[(p, o3)] = dA.n
                    n_out = d_fst[(1, NFC - 1)]
                    for p in range(2):
                        for o3 in range(NFC):
                            idx = p * NFC + o3
                            if idx == 0:
                                for j in range(2):
                                    v_q3[(0, 0, j)] = inc(cV)
                            if p == 0:
                                for k in slot_ks[o3]:
                                    v_ladB[k] = inc(cV)
                            if idx + 1 < 2 * NFC:
                                np_, no3 = (idx + 1) // NFC, (idx + 1) % NFC
                                for j in range(2):
                                    v_q3[(np_, no3, j)] = inc(cV)
                            v_f[(p, o3)] = inc(cV)
                    for p in range(2):
                        for o3 in range(NFC):
                            t_y3[(p, o3)] = inc(cT)

                    def skey(n):
                        return (n // 32, (n % 32) // 2, n % 2)

                    def _p6_sync(e):
                        for idx in range(1, 2 * NFC):
                            p, o3 = idx // NFC, idx % NFC
                            e.wait_ge(cV.h, v_q3[(( idx - 1) // NFC,
                                                  (idx - 1) % NFC, 0)])
                            e.dma_start(
                                out=p56[:, 0:2816].rearrange(
                                    "p (c f) -> p c f", f=128),
                                in_=dw3_c[:, 0:22,
                                          o3 * 128:(o3 + 1) * 128]
                            ).then_inc(dS.h, 16)
                    PQ.add("sync", _p6_sync)

                    def _p6_s(e):
                        for idx in range(2 * NFC):
                            p, o3 = idx // NFC, idx % NFC
                            if idx >= 2:
                                pi = idx - 2
                                e.wait_ge(cV.h, v_f[(pi // NFC, pi % NFC)])
                            e.dma_start(
                                out=xt6[idx % 2][:],
                                in_=dout[o3 * 128:(o3 + 1) * 128,
                                         p * 512:(p + 1) * 512]
                            ).then_inc(dA.h, 16)
                            e.wait_ge(cV.h, v_f[(p, o3)])
                            e.dma_start(
                                out=dout[o3 * 128:(o3 + 1) * 128,
                                         p * 512:(p + 1) * 512],
                                in_=fin6[:]).then_inc(dA.h, 16)
                        e.wait_ge(dA.h, n_out)
                    PQ.add("scalar", _p6_s)

                    def _p6_gp(e):
                        for idx in range(1, 2 * NFC):
                            p, o3 = idx // NFC, idx % NFC
                            e.wait_ge(cV.h, v_q3[((idx - 1) // NFC,
                                                  (idx - 1) % NFC, 1)])
                            e.dma_start(
                                out=p56[:, 2816:5632].rearrange(
                                    "p (c f) -> p c f", f=128),
                                in_=dw3_c[:, 22:44,
                                          o3 * 128:(o3 + 1) * 128]
                            ).then_inc(dS2.h, 16)
                    PQ.add("gpsimd", _p6_gp)

                    def _p6_v(e):
                        def quant(idx):
                            p, o3 = idx // NFC, idx % NFC
                            for j in range(2):
                                e.wait_ge(dS.h if j == 0 else dS2.h,
                                          d_t3[(p, o3, j)])
                                if j == 0 and idx >= 2:
                                    pi = idx - 2
                                    e.wait_ge(cT.h, t_y3[(pi // NFC,
                                                          pi % NFC)])
                                raw = p56[:, j * 2816:(j + 1) * 2816]
                                dst = t3s[idx % 2][:, j * 2816:(j + 1) * 2816]
                                e.tensor_scalar(qa3[:], raw, swcol[:, 14:15],
                                                None, ALU.is_lt)
                                ins = e.scalar_tensor_tensor(
                                    dst, raw, swcol[:, 13:14],
                                    qa3[:], ALU.is_gt, ALU.subtract)
                                if j == 0:
                                    ins.then_inc(cV.h, 1)
                                else:
                                    e.drain().then_inc(cV.h, 1)

                        for p in range(2):
                            for o3 in range(NFC):
                                idx = p * NFC + o3
                                if idx == 0:
                                    quant(0)
                                if p == 0:
                                    for ki, k in enumerate(slot_ks[o3]):
                                        last = (o3 == NFC - 1 and
                                                ki == len(slot_ks[o3]) - 1)
                                        lad_chunk(e, k, 512, lad6,
                                                  bare=not last)
                                if idx + 1 < 2 * NFC:
                                    quant(idx + 1)
                                e.wait_ge(cT.h, t_y3[(p, o3)])
                                e.wait_ge(dA.h, d_x6[(p, o3)])
                                if idx >= 1:
                                    pidx = idx - 1
                                    e.wait_ge(dA.h, d_fst[(pidx // NFC,
                                                           pidx % NFC)])
                                bank = pb[idx % 4]
                                e.tensor_tensor(ev6[:], bank[:, 0:512],
                                                s3r_b[:, p * 512:(p + 1) * 512],
                                                ALU.mult)
                                e.tensor_tensor(fin6[:], xt6[idx % 2][:],
                                                ev6[:], ALU.add)
                                e.drain().then_inc(cV.h, 1)
                    PQ.add("vector", _p6_v)

                    def _p6_t(e):
                        v_ladB_last = max(v_ladB.values())
                        for p in range(2):
                            for o3 in range(NFC):
                                idx = p * NFC + o3
                                e.wait_ge(cV.h, v_q3[(p, o3, 1)])
                                if p == 1 and o3 == 0:
                                    e.wait_ge(cV.h, v_ladB_last)
                                if idx >= 4:
                                    pidx = idx - 4
                                    e.wait_ge(cV.h, v_f[(pidx // NFC,
                                                         pidx % NFC)])
                                bank = pb[idx % 4]
                                for k in range(NKC):
                                    lhsT = t3s[idx % 2][:, k * 128:(k + 1) * 128]
                                    hk = hT(k)[:, p * 512:(p + 1) * 512]
                                    e.matmul(bank[:, 0:512], lhsT, hk,
                                             start=(k == 0), stop=(k == NKC - 1))
                                e.drain().then_inc(cT.h, 1)
                    PQ.add("tensor", _p6_t)

                    PQ.flush()
    return nc


def kernel(L0_attn_hidden_states, L0_attn_Wq, L0_attn_Wk, L0_attn_Wv, L0_attn_Wo,
           L0_attn_bank_weights, L0_attn_memories, L0_attn_memory_norms,
           L0_ffn_w1, L0_ffn_w2, L0_ffn_w3):
    f32 = np.float32
    x = np.ascontiguousarray(np.asarray(L0_attn_hidden_states, f32).reshape(8192, HID))
    wqT = np.ascontiguousarray(np.asarray(L0_attn_Wq, f32).T)
    woT = np.ascontiguousarray(np.asarray(L0_attn_Wo, f32).T)
    w1T = np.ascontiguousarray(np.asarray(L0_ffn_w1, f32).T)
    w2T = np.ascontiguousarray(np.asarray(L0_ffn_w2, f32).T)
    w3T = np.ascontiguousarray(np.asarray(L0_ffn_w3, f32).T)
    bw = np.ascontiguousarray(np.asarray(L0_attn_bank_weights, f32))
    mem = np.asarray(L0_attn_memories, f32)
    mn = np.asarray(L0_attn_memory_norms, f32)
    mpm = np.empty((NH, 128, 512), f32)
    mpn = np.empty((128, 64), f32)
    for h in range(NH):
        mpm[h] = np.concatenate([mem[0, h], mem[1, h], mem[2, h], mem[3, h]],
                                axis=1)
        for n in range(4):
            mpn[:, 4 * h + n] = mn[n, h]

    nc = build()
    in_maps = []
    for c in range(8):
        xs = x[c * TOK:(c + 1) * TOK]
        in_maps.append(dict(x_tok=np.ascontiguousarray(xs),
                            xT=np.ascontiguousarray(xs.T),
                            wqT=wqT, woT=woT, mpm=mpm, mpn=mpn, bw=bw,
                            w1T=w1T, w2T=w2T, w3T=w3T))
    res = run_bass_kernel_spmd(nc, in_maps, core_ids=list(range(8)))
    out = np.empty((8192, HID), f32)
    for c in range(8):
        out[c * TOK:(c + 1) * TOK] = np.asarray(res.results[c]["out"]).T
    return out.reshape(2, 4096, HID)


# version 25
# speedup vs baseline: 1.0613x; 1.0126x over previous
import numpy as np
import concourse.bass as bass
import concourse.mybir as mybir
from concourse.bass_utils import run_bass_kernel_spmd
from contextlib import ExitStack

F32 = mybir.dt.float32
BF16 = mybir.dt.bfloat16
F16 = mybir.dt.float16
AF = mybir.ActivationFunctionType
ALU = mybir.AluOpType
AX = mybir.AxisListType

TOK = 1024
NTT = 8
HID = 2048
NFC = 16
FF = 5632
NKC = 44
NH = 16
OS = 256
NOS = FF // OS
WMEAN_N = float(FF * HID)
LADDER = [128.0, 64.0, 32.0, 16.0, 8.0, 4.0, 2.0, 1.0]


class Ctr:
    def __init__(self, stack, nc, name):
        self.h = stack.enter_context(nc.semaphore(name))
        self.n = 0


class Q:

    def __init__(self, block):
        self.block = block
        self.qs = {k: [] for k in ("sync", "gpsimd", "vector", "scalar", "tensor")}

    def add(self, eng, fn):
        self.qs[eng].append(fn)

    def flush(self):
        for eng, cmds in self.qs.items():
            if not cmds:
                continue
            snapshot = list(cmds)
            cmds.clear()

            def body(e, snapshot=snapshot):
                for c in snapshot:
                    c(e)
            getattr(self.block, eng)(body)


def build(debug=False):
    nc = bass.Bass("TRN2", target_bir_lowering=False,
                   detect_race_conditions=False)
    dx = nc.dram_tensor("x_tok", [TOK, HID], F32, kind="ExternalInput").ap()
    dxT = nc.dram_tensor("xT", [HID, TOK], F32, kind="ExternalInput").ap()
    dwq = nc.dram_tensor("wqT", [HID, HID], F32, kind="ExternalInput").ap()
    dwo = nc.dram_tensor("woT", [HID, HID], F32, kind="ExternalInput").ap()
    dmpm = nc.dram_tensor("mpm", [NH, 128, 512], F32, kind="ExternalInput").ap()
    dmpn = nc.dram_tensor("mpn", [128, 64], F32, kind="ExternalInput").ap()
    dbw = nc.dram_tensor("bw", [16, 4], F32, kind="ExternalInput").ap()
    dw1 = nc.dram_tensor("w1T", [HID, FF], F32, kind="ExternalInput").ap()
    dw2 = nc.dram_tensor("w2T", [HID, FF], F32, kind="ExternalInput").ap()
    dw3 = nc.dram_tensor("w3T", [FF, HID], F32, kind="ExternalInput").ap()
    dout = nc.dram_tensor("out", [HID, TOK], F32, kind="ExternalOutput").ap()

    dwq_c = dwq.rearrange("(c p) o -> p c o", p=128)
    dwo_c = dwo.rearrange("(c p) o -> p c o", p=128)
    dw1_c = dw1.rearrange("(c p) f -> p c f", p=128)
    dw2_c = dw2.rearrange("(c p) f -> p c f", p=128)
    dw3_c = dw3.rearrange("(c p) o -> p c o", p=128)
    dmpm_t = dmpm.transpose([1, 0, 2])

    dbg = {}
    if debug:
        for nm, shape, dt_ in [
                ("dbg_xnT", [128, NFC * TOK], BF16),
                ("dbg_rqT", [128, NFC * TOK], BF16),
                ("dbg_stf1", [128, 96], F32),
                ("dbg_sqT", [128, NH * TOK], BF16),
                ("dbg_combT", [128, NFC * TOK], BF16),
                ("dbg_sw", [128, 16], F32),
                ("dbg_hTa", [128, 16 * TOK], F16),
                ("dbg_hTb", [128, 28 * TOK], F16),
                ("dbg_stf2", [128, 96], F32),
                ("dbg_q2a", [128, 16 * TOK], F16),
                ("dbg_q2b", [128, 28 * TOK], F16),
                ("dbg_qs2r", [128, TOK], F32),
                ("dbg_s3r", [128, TOK], F32),
                ("dbg_wab", [128, 64], F32)]:
            dbg[nm] = nc.dram_tensor(nm, shape, dt_, kind="ExternalOutput").ap()

    SG = ExitStack()
    with SG:
        sb = lambda *a, **k: SG.enter_context(nc.sbuf_tensor(*a, **k))
        ps = lambda *a: SG.enter_context(nc.psum_tensor(*a))
        ident = sb("ident", [128, 128], BF16)
        ident16 = sb("ident16", [128, 128], F16)
        iotaf = sb("iotaf", [128, 128], F32)
        onesr = sb("onesr", [1, 128], F32)
        onescol = sb("onescol", [128, 1], F32)
        wa_b = sb("wa_b", [128, 64], F32)
        stf = sb("stf", [128, 96], F32)
        sttmp = sb("sttmp", [128, 8], F32)
        swcol = sb("swcol", [128, 16], F32)
        hmx = sb("hmx", [128, 176], F32)
        wsums = sb("wsums", [128, 144], F32)
        wsp = [sb("wsp%d" % i, [128, 2048], F32) for i in range(2)]
        sm_t = sb("sm_t", [16, 32], F32)
        act4 = sb("act4", [128, 4], F32)
        smallrow = sb("smallrow", [1, 64], F32)
        wsrun = sb("wsrun", [128, 4], F32)
        rn64 = sb("rn64", [128, 64], F32)
        rn64b = sb("rn64b", [128, 64], F32)
        ccol = sb("ccol", [128, 2], F32)
        sc11 = sb("sc11", [1, 8], F32)
        pb = [ps("pb%d" % i, [128, 512], F32) for i in range(8)]

        dS = Ctr(SG, nc, "dS")
        dS2 = Ctr(SG, nc, "dS2")
        dG = Ctr(SG, nc, "dG")
        cV = Ctr(SG, nc, "cV")
        cS = Ctr(SG, nc, "cS")
        cT = Ctr(SG, nc, "cT")
        cG = Ctr(SG, nc, "cG")
        dA = Ctr(SG, nc, "dA")
        ALLC = [dS, dS2, dG, cV, cS, cT, cG, dA]
        dW = Ctr(SG, nc, "dW")
        dWb = Ctr(SG, nc, "dWb")
        cW = Ctr(SG, nc, "cW")
        block = SG.enter_context(nc.Block())
        PQ = Q(block)

        def inc(ctr, k=1):
            ctr.n += k
            return ctr.n

        def barrier():
            PQ.flush()
            snap = [(c.h, c.n) for c in ALLC if c.n > 0]

            def mk(engdec):
                @engdec
                def _(eng):
                    for h, v in snap:
                        eng.wait_ge(h, v)
            for ed in (block.sync, block.gpsimd, block.vector, block.scalar,
                       block.tensor):
                mk(ed)

        def dbg_dump(pairs):
            if not dbg:
                return
            PQ.flush()
            snap = [(c.h, c.n) for c in ALLC if c.n > 0]

            @block.sync
            def _(sync):
                for h, v in snap:
                    sync.wait_ge(h, v)
                for nm, srcap in pairs:
                    sync.dma_start(out=dbg[nm], in_=srcap).then_inc(dS.h, 16)
            dS.n += 16 * len(pairs)

        wstrips = []
        for m, (dmat, nchunks, width) in enumerate(
                [(dw1, NFC, FF), (dw2, NFC, FF), (dw3, NKC, HID)]):
            for c in range(nchunks):
                if width == FF:
                    wstrips.append((m, dmat, c, 0, 2048))
                    wstrips.append((m, dmat, c, 2048, 2048))
                    wstrips.append((m, dmat, c, 4096, 1536))
                else:
                    wstrips.append((m, dmat, c, 0, 2048))
        NSTRIPS = len(wstrips)
        ws_pos = [0]
        wg_pos = [0]

        def emit_strips(e, count):
            for _ in range(count):
                n = ws_pos[0]
                if n >= NSTRIPS:
                    return
                ws_pos[0] += 1
                m, dmat, c, off, wdt = wstrips[n]
                if n >= 2:
                    e.wait_ge(cW.h, n - 1)
                e.dma_start(out=wsp[n % 2][:, 0:wdt],
                            in_=dmat[c * 128:(c + 1) * 128, off:off + wdt]
                            ).then_inc((dW if n % 2 == 0 else dWb).h, 16)

        def emit_reduces(e, count, upto=None):
            cap = NSTRIPS if upto is None else min(upto, NSTRIPS)
            for _ in range(count):
                n = wg_pos[0]
                if n >= cap:
                    return
                wg_pos[0] += 1
                wdt = wstrips[n][4]
                e.wait_ge((dW if n % 2 == 0 else dWb).h,
                          16 * (n // 2 + 1))
                ins = e.reduce_sum(wsums[:, n:n + 1], wsp[n % 2][:, 0:wdt],
                                   axis=AX.X, apply_absolute_value=True)
                if n == NSTRIPS - 1:
                    e.drain().then_inc(cW.h, 1)
                else:
                    ins.then_inc(cW.h, 1)

        dS.n += 16
        p0_bw = dS.n
        PQ.add("sync", lambda e: e.dma_start(out=sm_t[0:16, 0:4], in_=dbw[:]
                                             ).then_inc(dS.h, 16))

        cG.n += 1
        p0_iota = cG.n
        dG.n += 16
        p0_norm = dG.n

        def _gp0(e):
            e.iota(iotaf[:], pattern=[[1, 128]], base=0, channel_multiplier=-1,
                   allow_small_or_imprecise_dtypes=True)
            e.drain().then_inc(cG.h, 1)
            e.dma_start(out=wa_b[:], in_=dmpn[:]).then_inc(dG.h, 16)
        PQ.add("gpsimd", _gp0)

        cV.n += 1
        p0_v1 = cV.n

        def _v0(e):
            e.wait_ge(cG.h, p0_iota)
            e.tensor_single_scalar(ident[:], iotaf[:], 0.0, ALU.is_equal)
            e.tensor_single_scalar(ident16[:], iotaf[:], 0.0, ALU.is_equal)
            e.memset(onesr[:], 1.0)
            e.memset(onescol[:], 1.0)
            e.memset(ccol[:, 0:1], 1e-5)
            e.memset(stf[:, 40:48], 0.0)
            e.memset(stf[:, 72:80], 0.0)
            e.wait_ge(dS.h, p0_bw)
            e.reduce_max(sm_t[0:16, 16:17], sm_t[0:16, 0:4], axis=AX.X)
            e.drain()
            e.tensor_scalar_mul(sm_t[0:16, 19:20], sm_t[0:16, 16:17], -1.0)
            e.wait_ge(dG.h, p0_norm)
            nb = wa_b[:].rearrange("p (h n) -> p h n", n=4)
            for n in range(4):
                e.reduce_sum(act4[:, n:n + 1], nb[:, :, n], axis=AX.X)
            e.drain().then_inc(cV.h, 1)
        PQ.add("vector", _v0)

        cS.n += 1
        p0_exp = cS.n

        def _s0(e):
            e.wait_ge(cV.h, p0_v1)
            e.activation(sm_t[0:16, 4:8], sm_t[0:16, 0:4], AF.Exp,
                         bias=sm_t[0:16, 19:20], scale=1.0)
            e.drain().then_inc(cS.h, 1)
        PQ.add("scalar", _s0)

        cT.n += 1
        p0_mm1 = cT.n

        def _t0(e):
            e.wait_ge(cV.h, p0_v1)
            e.matmul(pb[5][0:4, 0:1], act4[:], onescol[:], start=True, stop=True)
            e.drain().then_inc(cT.h, 1)
        PQ.add("tensor", _t0)

        cV.n += 1
        p0_v2 = cV.n

        def _v1(e):
            e.wait_ge(cS.h, p0_exp)
            e.reduce_sum(sm_t[0:16, 17:18], sm_t[0:16, 4:8], axis=AX.X)
            e.drain()
            e.reciprocal(sm_t[0:16, 18:19], sm_t[0:16, 17:18])
            e.drain()
            e.tensor_scalar(sm_t[0:16, 8:12], sm_t[0:16, 4:8],
                            sm_t[0:16, 18:19], None, ALU.mult)
            e.wait_ge(cT.h, p0_mm1)
            e.tensor_single_scalar(act4[0:4, 0:1], pb[5][0:4, 0:1], 1e-6,
                                   ALU.is_ge)
            e.drain().then_inc(cV.h, 1)
        PQ.add("vector", _v1)

        dG.n += 16
        p0_arow = dG.n

        def _gp1(e):
            e.wait_ge(cV.h, p0_v2)
            e.dma_start(out=smallrow[0:1, 0:4], in_=act4[0:4, 0:1],
                        allow_slow_non_contiguous=True).then_inc(dG.h, 16)
        PQ.add("gpsimd", _gp1)

        cT.n += 1
        p0_mm2 = cT.n

        def _t1(e):
            e.wait_ge(dG.h, p0_arow)
            e.matmul(pb[5][0:16, 4:8], onesr[0:1, 0:16], smallrow[0:1, 0:4],
                     start=True, stop=True)
            e.drain().then_inc(cT.h, 1)
        PQ.add("tensor", _t1)

        cV.n += 1
        p0_v3 = cV.n

        def _v2(e):
            e.wait_ge(cT.h, p0_mm2)
            e.tensor_tensor(sm_t[0:16, 12:16], sm_t[0:16, 8:12],
                            pb[5][0:16, 4:8], ALU.mult)
            e.drain().then_inc(cV.h, 1)
        PQ.add("vector", _v2)

        dG.n += 16
        p0_wrow = dG.n

        def _gp2(e):
            e.wait_ge(cV.h, p0_v3)
            e.dma_start(out=smallrow[0:1, 0:64], in_=sm_t[0:16, 12:16],
                        allow_slow_non_contiguous=True).then_inc(dG.h, 16)
        PQ.add("gpsimd", _gp2)

        cT.n += 1
        p0_mm3 = cT.n

        def _t2(e):
            e.wait_ge(dG.h, p0_wrow)
            e.matmul(pb[5][:, 8:72], onesr[:], smallrow[0:1, 0:64],
                     start=True, stop=True)
            e.drain().then_inc(cT.h, 1)
        PQ.add("tensor", _t2)

        cV.n += 1

        def _v3(e):
            e.wait_ge(cT.h, p0_mm3)
            e.tensor_copy(wa_b[:], pb[5][:, 8:72])
            e.drain().then_inc(cV.h, 1)
        PQ.add("vector", _v3)

        L1 = ExitStack()
        with L1:
            rqT = L1.enter_context(nc.sbuf_tensor("rqT", [128, NFC * TOK], BF16))
            L2 = ExitStack()
            with L2:
                sqT = L2.enter_context(nc.sbuf_tensor("sqT", [128, NH * TOK], BF16))
                mp16m = L2.enter_context(
                    nc.sbuf_tensor("mp16m", [128, NH * 512], BF16))
                nrm16 = L2.enter_context(nc.sbuf_tensor("nrm16", [128, 64], BF16))

                dG.n += 32
                d_mp = dG.n

                def _gpl(e):
                    e.dma_start(out=mp16m[:].rearrange("p (h c) -> p h c", c=512),
                                in_=dmpm_t[:]).then_inc(dG.h, 16)
                    e.dma_start(out=nrm16[:], in_=dmpn[:]).then_inc(dG.h, 16)
                PQ.add("gpsimd", _gpl)

                L3 = ExitStack()
                with L3:
                    sb3 = lambda *a, **k: L3.enter_context(nc.sbuf_tensor(*a, **k))
                    xa = sb3("xa", [128, HID], F32)
                    xsq = sb3("xsq", [128, HID], F32)
                    xn16 = [sb3("xn16_%d" % i, [128, HID], BF16) for i in range(2)]
                    rq16 = [sb3("rq16_%d" % i, [128, HID], BF16) for i in range(2)]
                    ub = sb3("ub", [128, HID], F32)
                    y0b = sb3("y0b", [128, HID], F32)
                    ywk = sb3("ywk", [128, HID], F32)
                    wrapc = xsq
                    xnT = sb3("xnT", [128, NFC * TOK], BF16)
                    wq_s = [sb3("wq_s%d" % i, [128, HID], BF16) for i in range(2)]
                    sigt = sb3("sigt", [128, 512], F32)
                    sige = sb3("sige", [128, 512], F32)

                    barrier()
                    d_x = [0] * NTT
                    v_var = [0] * NTT
                    s_ln = [0] * NTT
                    v_neg = [0] * NTT
                    s_rstd = [0] * NTT
                    v_p1 = [0] * NTT
                    t_tr = {}
                    s_tr = {}
                    for tt in range(NTT):
                        dS.n += 16
                        d_x[tt] = dS.n
                        v_var[tt] = inc(cV)
                        v_neg[tt] = inc(cV)
                        v_p1[tt] = inc(cV)
                    for tt in range(NTT):
                        for b_ in range(8):
                            t_tr[(tt, b_)] = inc(cT)
                    for tt in range(NTT):
                        s_ln[tt] = inc(cS)
                        s_rstd[tt] = inc(cS)
                        if tt >= 1:
                            for b_ in range(8):
                                s_tr[(tt - 1, b_)] = inc(cS)
                    for b_ in range(8):
                        s_tr[(NTT - 1, b_)] = inc(cS)
                    s_p1_done = inc(cS)

                    def _p1_sync(e):
                        for tt in range(NTT):
                            if tt >= 1:
                                e.wait_ge(cV.h, v_p1[tt - 1])
                            e.dma_start(out=xa[:],
                                        in_=dx[tt * 128:(tt + 1) * 128, :]
                                        ).then_inc(dS.h, 16)
                            if tt >= 1:
                                emit_strips(e, 6)
                        emit_strips(e, 4)
                    PQ.add("sync", _p1_sync)



                    def _p1_vec(e):
                        for tt in range(NTT):
                            e.wait_ge(dS.h, d_x[tt])
                            if tt >= 2:
                                e.wait_ge(cS.h, s_tr[(tt - 2, 7)])
                            e.tensor_tensor(xsq[:], xa[:], xa[:], ALU.mult)
                            e.reduce_sum(sttmp[:, 0:1], xa[:], axis=AX.X)
                            e.reduce_sum(sttmp[:, 1:2], xsq[:], axis=AX.X)
                            e.reduce_max(sttmp[:, 2:3], xa[:], axis=AX.X)
                            e.tensor_reduce(sttmp[:, 3:4], xa[:], AX.X, ALU.min)
                            e.drain()
                            e.tensor_scalar_mul(stf[:, 0 + tt:1 + tt],
                                                sttmp[:, 0:1], -1.0 / HID)
                            e.tensor_scalar_mul(sttmp[:, 6:7], sttmp[:, 1:2],
                                                1.0 / HID)
                            e.drain()
                            e.tensor_tensor(sttmp[:, 7:8], stf[:, 0 + tt:1 + tt],
                                            stf[:, 0 + tt:1 + tt], ALU.mult)
                            e.drain()
                            e.tensor_tensor(sttmp[:, 4:5], sttmp[:, 6:7],
                                            sttmp[:, 7:8], ALU.subtract)
                            e.drain().then_inc(cV.h, 1)
                            if tt >= 1:
                                emit_reduces(e, 6, upto=46)
                            e.wait_ge(cS.h, s_ln[tt])
                            e.tensor_scalar_mul(sttmp[:, 4:5], sttmp[:, 5:6], -0.5)
                            e.drain().then_inc(cV.h, 1)
                            e.wait_ge(cS.h, s_rstd[tt])
                            e.tensor_tensor(sttmp[:, 6:7], sttmp[:, 2:3],
                                            stf[:, 0 + tt:1 + tt], ALU.add)
                            e.tensor_tensor(sttmp[:, 7:8], sttmp[:, 3:4],
                                            stf[:, 0 + tt:1 + tt], ALU.add)
                            e.drain()
                            e.tensor_scalar_mul(sttmp[:, 5:6], sttmp[:, 7:8], -1.0)
                            e.drain()
                            e.tensor_tensor(sttmp[:, 2:3], sttmp[:, 6:7],
                                            sttmp[:, 5:6], ALU.max)
                            e.drain()
                            e.tensor_tensor(sttmp[:, 6:7], sttmp[:, 2:3],
                                            stf[:, 8 + tt:9 + tt], ALU.mult)
                            e.drain()
                            e.tensor_scalar_max(sttmp[:, 7:8], sttmp[:, 6:7], 1e-5)
                            e.drain()
                            e.reciprocal(sttmp[:, 6:7], sttmp[:, 7:8])
                            e.drain()
                            e.tensor_scalar_mul(stf[:, 16 + tt:17 + tt],
                                                sttmp[:, 6:7], 127.0)
                            e.drain()
                            e.reciprocal(stf[:, 24 + tt:25 + tt],
                                         stf[:, 16 + tt:17 + tt])
                            e.drain()
                            e.tensor_tensor(stf[:, 32 + tt:33 + tt],
                                            stf[:, 8 + tt:9 + tt],
                                            stf[:, 16 + tt:17 + tt], ALU.mult)
                            e.drain()
                            e.tensor_scalar(xn16[tt % 2][:], xa[:],
                                            stf[:, 0 + tt:1 + tt],
                                            stf[:, 8 + tt:9 + tt],
                                            ALU.add, ALU.mult)
                            e.tensor_scalar(ub[:], xa[:],
                                            stf[:, 0 + tt:1 + tt],
                                            stf[:, 32 + tt:33 + tt],
                                            ALU.add, ALU.mult)
                            e.tensor_single_scalar(y0b[:], ub[:], 128.5, ALU.add)
                            src = y0b
                            dsts = [ywk, ub]
                            for wi, P in enumerate(LADDER):
                                dst = dsts[wi % 2]
                                e.tensor_scalar(wrapc[:], src[:], P, -P,
                                                ALU.is_gt, ALU.mult)
                                e.tensor_tensor(dst[:], src[:], wrapc[:], ALU.add)
                                src = dst
                            e.scalar_tensor_tensor(rq16[tt % 2][:], y0b[:], 128.0,
                                                   src[:], ALU.subtract,
                                                   ALU.subtract)
                            e.drain().then_inc(cV.h, 1)
                    PQ.add("vector", _p1_vec)

                    def _p1_evac(e, tt):
                        for b_ in range(8):
                            srcsel, g = divmod(b_, 4)
                            dstT = xnT if srcsel == 0 else rqT
                            bank = pb[6 + ((tt * 8 + b_) % 2)]
                            e.wait_ge(cT.h, t_tr[(tt, b_)])
                            dv = dstT[:].rearrange("p (c t) -> p c t", t=TOK)
                            e.activation(
                                dv[:, 4 * g:4 * g + 4, tt * 128:(tt + 1) * 128],
                                bank[:, 0:512], AF.Identity
                                ).then_inc(cS.h, 1)

                    def _p1_scal(e):
                        for tt in range(NTT):
                            e.wait_ge(cV.h, v_var[tt])
                            e.activation(sttmp[:, 5:6], sttmp[:, 4:5], AF.Ln,
                                         bias=ccol[:, 0:1], scale=1.0)
                            e.drain().then_inc(cS.h, 1)
                            e.wait_ge(cV.h, v_neg[tt])
                            e.activation(stf[:, 8 + tt:9 + tt], sttmp[:, 4:5],
                                         AF.Exp, scale=1.0)
                            e.drain().then_inc(cS.h, 1)
                            if tt >= 1:
                                _p1_evac(e, tt - 1)
                        _p1_evac(e, NTT - 1)
                        e.drain().then_inc(cS.h, 1)
                    PQ.add("scalar", _p1_scal)

                    def _p1_tens(e):
                        for tt in range(NTT):
                            e.wait_ge(cV.h, v_p1[tt])
                            for b_ in range(8):
                                srcsel, g = divmod(b_, 4)
                                src16 = xn16[tt % 2] if srcsel == 0 else rq16[tt % 2]
                                bank = pb[6 + ((tt * 8 + b_) % 2)]
                                gb = tt * 8 + b_
                                if gb >= 2:
                                    pv = gb - 2
                                    e.wait_ge(cS.h, s_tr[(pv // 8, pv % 8)])
                                for j in range(4):
                                    c = 4 * g + j
                                    e.matmul(bank[:, j * 128:(j + 1) * 128],
                                             src16[:, c * 128:(c + 1) * 128],
                                             ident[:], start=True, stop=True)
                                e.drain().then_inc(cT.h, 1)
                    PQ.add("tensor", _p1_tens)

                    dbg_dump([("dbg_xnT", xnT[:]), ("dbg_rqT", rqT[:]),
                              ("dbg_stf1", stf[:])])
                    barrier()
                    d_wq = [0] * NFC
                    t_qp = [0] * NFC
                    v_min = {}
                    s_sig = {}
                    v_done = {}
                    for o in range(NFC):
                        dG.n += 16
                        d_wq[o] = dG.n
                        t_qp[o] = inc(cT)
                        for half in range(2):
                            v_min[(o, half)] = inc(cV)
                            v_done[(o, half)] = inc(cV)
                    v_p2a_done = inc(cV)
                    for o in range(NFC):
                        for half in range(2):
                            s_sig[(o, half)] = inc(cS)

                    def _p2a_gp(e):
                        for o in range(NFC):
                            if o >= 2:
                                e.wait_ge(cT.h, t_qp[o - 2])
                            e.dma_start(
                                out=wq_s[o % 2][:].rearrange("p (c o2) -> p c o2",
                                                             o2=128),
                                in_=dwq_c[:, :, o * 128:(o + 1) * 128]
                            ).then_inc(dG.h, 16)
                    PQ.add("gpsimd", _p2a_gp)

                    def _p2a_sync(e):
                        emit_strips(e, 50)
                    PQ.add("sync", _p2a_sync)

                    def _p2a_t(e):
                        for o in range(NFC):
                            e.wait_ge(dG.h, d_wq[o])
                            if o >= 2:
                                e.wait_ge(cV.h, v_done[(o - 2, 1)])
                            pa, pbk = pb[2 * (o % 2)], pb[2 * (o % 2) + 1]
                            for i in range(NFC):
                                lhsT = wq_s[o % 2][:, i * 128:(i + 1) * 128]
                                e.matmul(pa[:, 0:512], lhsT,
                                         xnT[:, i * TOK:i * TOK + 512],
                                         start=(i == 0), stop=(i == NFC - 1))
                                e.matmul(pbk[:, 0:512], lhsT,
                                         xnT[:, i * TOK + 512:(i + 1) * TOK],
                                         start=(i == 0), stop=(i == NFC - 1))
                            e.drain().then_inc(cT.h, 1)
                    PQ.add("tensor", _p2a_t)

                    def _p2a_v(e):
                        for o in range(NFC):
                            e.wait_ge(cT.h, t_qp[o])
                            for half in range(2):
                                pban = pb[2 * (o % 2) + half]
                                e.tensor_scalar_min(sigt[:], pban[:, 0:512], 0.0)
                                e.drain().then_inc(cV.h, 1)
                                emit_reduces(e, 3, upto=96)
                                e.wait_ge(cS.h, s_sig[(o, half)])
                                e.scalar_tensor_tensor(
                                    sqT[:, o * TOK + half * 512:
                                        o * TOK + (half + 1) * 512],
                                    pban[:, 0:512], 0.0, sige[:],
                                    ALU.max, ALU.add
                                    ).then_inc(cV.h, 1)
                        e.drain().then_inc(cV.h, 1)
                    PQ.add("vector", _p2a_v)

                    def _p2a_s(e):
                        for o in range(NFC):
                            for half in range(2):
                                e.wait_ge(cV.h, v_min[(o, half)])
                                e.activation(sige[:], sigt[:], AF.Exp)
                                e.drain().then_inc(cS.h, 1)
                    PQ.add("scalar", _p2a_s)

                L3b = ExitStack()
                with L3b:
                    sb3b = lambda *a, **k: L3b.enter_context(nc.sbuf_tensor(*a, **k))
                    combT = sb3b("combT", [128, NFC * TOK], BF16)
                    comb16 = [sb3b("comb16_%d" % i, [128, HID], BF16)
                              for i in range(2)]
                    cadd = sb3b("cadd", [128, 256], F32)
                    wo_s = [sb3b("wo_s%d" % i, [128, HID], BF16) for i in range(2)]
                    xt_b = [sb3b("xt_b%d" % i, [128, TOK], F32) for i in range(2)]
                    ot_b = [sb3b("ot_b%d" % i, [128, TOK], F32) for i in range(2)]

                    dbg_dump([("dbg_sqT", sqT[:]), ("dbg_wab", wa_b[:])])
                    barrier()
                    t_nrm = [0] * NTT
                    v_rn = [0] * NTT
                    t_A = {}
                    v_cb = {}
                    v_cbd = [0] * NTT
                    t_ct = {}
                    s_ct = {}
                    for tt in range(NTT):
                        t_nrm[tt] = inc(cT)
                        for h in range(NH):
                            t_A[(tt, h)] = inc(cT)
                        for g in range(4):
                            t_ct[(tt, g)] = inc(cT)
                    for tt in range(NTT):
                        v_rn[tt] = inc(cV)
                        for h in range(NH):
                            v_cb[(tt, h)] = inc(cV)
                        v_cbd[tt] = inc(cV)
                    for tt in range(NTT):
                        for g in range(4):
                            s_ct[(tt, g)] = inc(cS)
                    s_p2b_done = inc(cS)

                    def _p2b_sync(e):
                        emit_strips(e, NSTRIPS)
                    PQ.add("sync", _p2b_sync)

                    def _p2b_t(e):
                        for tt in range(NTT):
                            if tt >= 1:
                                e.wait_ge(cV.h, v_rn[tt - 1])
                            for h in range(NH):
                                sq_blk = sqT[:, h * TOK + tt * 128:
                                             h * TOK + (tt + 1) * 128]
                                e.matmul(pb[4][:, 4 * h:4 * h + 4], sq_blk,
                                         nrm16[:, 4 * h:4 * h + 4],
                                         start=True, stop=True)
                            e.drain().then_inc(cT.h, 1)
                            for h in range(NH):
                                step = tt * NH + h
                                if step >= 4:
                                    pvs = step - 4
                                    e.wait_ge(cV.h, v_cb[(pvs // NH, pvs % NH)])
                                sq_blk = sqT[:, h * TOK + tt * 128:
                                             h * TOK + (tt + 1) * 128]
                                e.matmul(pb[step % 4][:, 0:512], sq_blk,
                                         mp16m[:, h * 512:(h + 1) * 512],
                                         start=True, stop=True)
                                e.drain().then_inc(cT.h, 1)
                            e.wait_ge(cV.h, v_cbd[tt])
                            for g in range(4):
                                ctb = tt * 4 + g
                                bank = pb[6 + (ctb % 2)]
                                if ctb >= 2:
                                    pv = ctb - 2
                                    e.wait_ge(cS.h, s_ct[(pv // 4, pv % 4)])
                                for j in range(4):
                                    c = 4 * g + j
                                    e.matmul(bank[:, j * 128:(j + 1) * 128],
                                             comb16[tt % 2][:, c * 128:(c + 1) * 128],
                                             ident[:], start=True, stop=True)
                                e.drain().then_inc(cT.h, 1)
                    PQ.add("tensor", _p2b_t)

                    def _p2b_v(e):
                        for tt in range(NTT):
                            emit_reduces(e, 6)
                            e.wait_ge(cT.h, t_nrm[tt])
                            if tt >= 2:
                                e.wait_ge(cS.h, s_ct[(tt - 2, 3)])
                            e.tensor_scalar_max(rn64[:], pb[4][:, 0:64], 1e-6)
                            e.drain()
                            e.reciprocal(rn64b[:], rn64[:])
                            e.drain()
                            e.tensor_tensor(rn64[:], rn64b[:], wa_b[:], ALU.mult)
                            e.drain().then_inc(cV.h, 1)
                            for h in range(NH):
                                e.wait_ge(cT.h, t_A[(tt, h)])
                                pA = pb[(tt * NH + h) % 4]
                                e.tensor_scalar(cadd[:, 0:128], pA[:, 0:128],
                                                rn64[:, 4 * h:4 * h + 1], None,
                                                ALU.mult)
                                e.scalar_tensor_tensor(cadd[:, 128:256],
                                                       pA[:, 128:256],
                                                       rn64[:, 4 * h + 1:4 * h + 2],
                                                       cadd[:, 0:128],
                                                       ALU.mult, ALU.add)
                                e.scalar_tensor_tensor(cadd[:, 0:128],
                                                       pA[:, 256:384],
                                                       rn64[:, 4 * h + 2:4 * h + 3],
                                                       cadd[:, 128:256],
                                                       ALU.mult, ALU.add)
                                e.scalar_tensor_tensor(
                                    comb16[tt % 2][:, h * 128:(h + 1) * 128],
                                    pA[:, 384:512],
                                    rn64[:, 4 * h + 3:4 * h + 4],
                                    cadd[:, 0:128], ALU.mult, ALU.add
                                    ).then_inc(cV.h, 1)
                            e.drain().then_inc(cV.h, 1)
                    PQ.add("vector", _p2b_v)

                    def _p2b_s(e):
                        for tt in range(NTT):
                            for g in range(4):
                                e.wait_ge(cT.h, t_ct[(tt, g)])
                                bank = pb[6 + ((tt * 4 + g) % 2)]
                                dv = combT[:].rearrange("p (c t) -> p c t", t=TOK)
                                e.activation(
                                    dv[:, 4 * g:4 * g + 4, tt * 128:(tt + 1) * 128],
                                    bank[:, 0:512], AF.Identity
                                    ).then_inc(cS.h, 1)
                        e.drain().then_inc(cS.h, 1)
                    PQ.add("scalar", _p2b_s)

                    dbg_dump([("dbg_combT", combT[:])])
                    barrier()
                    d_wo = [0] * NFC
                    d_xt = [0] * NFC
                    t_wo = [0] * NFC
                    v_wo = [0] * NFC
                    d_park = [0] * NFC
                    for o in range(NFC):
                        dG.n += 16
                        d_wo[o] = dG.n
                        dS.n += 16
                        d_xt[o] = dS.n
                        dS.n += 16
                        d_park[o] = dS.n
                        t_wo[o] = inc(cT)
                        v_wo[o] = inc(cV)

                    def _p2c_gp(e):
                        for o in range(NFC):
                            if o >= 2:
                                e.wait_ge(cT.h, t_wo[o - 2])
                            e.dma_start(
                                out=wo_s[o % 2][:].rearrange("p (c o2) -> p c o2",
                                                             o2=128),
                                in_=dwo_c[:, :, o * 128:(o + 1) * 128]
                            ).then_inc(dG.h, 16)
                    PQ.add("gpsimd", _p2c_gp)

                    def _p2c_sync_ws(e):
                        emit_strips(e, NSTRIPS)
                    PQ.add("sync", _p2c_sync_ws)

                    def _p2c_sync(e):
                        for o in range(NFC):
                            if o >= 2:
                                e.wait_ge(cV.h, v_wo[o - 2])
                            e.dma_start(out=xt_b[o % 2][:],
                                        in_=dxT[o * 128:(o + 1) * 128, :]
                                        ).then_inc(dS.h, 16)
                            e.wait_ge(cV.h, v_wo[o])
                            e.dma_start(out=dout[o * 128:(o + 1) * 128, :],
                                        in_=ot_b[o % 2][:]).then_inc(dS.h, 16)
                    PQ.add("sync", _p2c_sync)

                    def _p2c_t(e):
                        for o in range(NFC):
                            e.wait_ge(dG.h, d_wo[o])
                            if o >= 2:
                                e.wait_ge(cV.h, v_wo[o - 2])
                            pa, pbk = pb[2 * (o % 2)], pb[2 * (o % 2) + 1]
                            for k in range(NFC):
                                lhsT = wo_s[o % 2][:, k * 128:(k + 1) * 128]
                                e.matmul(pa[:, 0:512], lhsT,
                                         combT[:, k * TOK:k * TOK + 512],
                                         start=(k == 0), stop=(k == NFC - 1))
                                e.matmul(pbk[:, 0:512], lhsT,
                                         combT[:, k * TOK + 512:(k + 1) * TOK],
                                         start=(k == 0), stop=(k == NFC - 1))
                            e.drain().then_inc(cT.h, 1)
                    PQ.add("tensor", _p2c_t)

                    def _p2c_v(e):
                        for o in range(NFC):
                            emit_reduces(e, 3)
                            e.wait_ge(cT.h, t_wo[o])
                            e.wait_ge(dS.h, d_xt[o])
                            if o >= 2:
                                e.wait_ge(dS.h, d_park[o - 2])
                            pa, pbk = pb[2 * (o % 2)], pb[2 * (o % 2) + 1]
                            e.tensor_tensor(ot_b[o % 2][:, 0:512],
                                            xt_b[o % 2][:, 0:512],
                                            pa[:, 0:512], ALU.add)
                            e.tensor_tensor(ot_b[o % 2][:, 512:1024],
                                            xt_b[o % 2][:, 512:1024],
                                            pbk[:, 0:512], ALU.add)
                            e.drain().then_inc(cV.h, 1)
                    PQ.add("vector", _p2c_v)

            L3c = ExitStack()
            with L3c:
                barrier()
                mranges = [(0, 48), (48, 96), (96, 140)]
                t_sum = [0] * 3
                v_sw1 = [0] * 3
                t_bc = [0] * 3
                v_sw2 = [0] * 3
                v_wsum = [0] * 3
                for m in range(3):
                    v_wsum[m] = inc(cV)
                    v_sw1[m] = inc(cV)
                    v_sw2[m] = inc(cV)
                for m in range(3):
                    t_sum[m] = inc(cT)
                    t_bc[m] = inc(cT)

                def _p3_v(e):
                    emit_reduces(e, NSTRIPS)
                    e.drain()
                    for m in range(3):
                        lo, hi = mranges[m]
                        e.reduce_sum(wsrun[:, 0:1], wsums[:, lo:hi], axis=AX.X)
                        e.drain().then_inc(cV.h, 1)
                        e.wait_ge(cT.h, t_sum[m])
                        e.tensor_scalar_mul(sc11[0:1, 0:1], pb[5][0:1, 0:1],
                                            1.0 / WMEAN_N)
                        e.drain()
                        e.tensor_scalar_max(sc11[0:1, 1:2], sc11[0:1, 0:1], 1e-5)
                        e.drain()
                        e.reciprocal(sc11[0:1, 2:3], sc11[0:1, 1:2])
                        e.drain().then_inc(cV.h, 1)
                        e.wait_ge(cT.h, t_bc[m])
                        e.tensor_copy(swcol[:, 3 * m:3 * m + 1], pb[5][:, 100:101])
                        e.tensor_scalar_mul(swcol[:, 3 * m + 1:3 * m + 2],
                                            pb[5][:, 100:101], -1.0)
                        e.tensor_copy(swcol[:, 3 * m + 2:3 * m + 3],
                                      pb[5][:, 101:102])
                        e.tensor_scalar_mul(swcol[:, 9 + 2 * m:10 + 2 * m],
                                            pb[5][:, 101:102], 0.5)
                        e.tensor_scalar_mul(swcol[:, 10 + 2 * m:11 + 2 * m],
                                            pb[5][:, 101:102], -0.5)
                        e.drain()
                        if m < 2:
                            e.tensor_scalar(stf[:, 56 + 8 * m:64 + 8 * m],
                                            stf[:, 24:32],
                                            swcol[:, 3 * m + 2:3 * m + 3], None,
                                            ALU.mult)
                        e.drain().then_inc(cV.h, 1)
                PQ.add("vector", _p3_v)

                def _p3_t(e):
                    for m in range(3):
                        e.wait_ge(cV.h, v_wsum[m])
                        e.matmul(pb[5][0:1, 0:1],
                                 wsrun[:, 0:1], onescol[:],
                                 start=True, stop=True)
                        e.drain().then_inc(cT.h, 1)
                        e.wait_ge(cV.h, v_sw1[m])
                        e.matmul(pb[5][:, 100:101], onesr[:], sc11[0:1, 2:3],
                                 start=True, stop=True)
                        e.matmul(pb[5][:, 101:102], onesr[:], sc11[0:1, 1:2],
                                 start=True, stop=True)
                        e.drain().then_inc(cT.h, 1)
                PQ.add("tensor", _p3_t)

            L4 = ExitStack()
            with L4:
                hTa = L4.enter_context(nc.sbuf_tensor("hTa", [128, 16 * TOK], F16))
                hTb = L4.enter_context(nc.sbuf_tensor("hTb", [128, 28 * TOK], F16))

                def hT(k):
                    if k < 16:
                        return hTa[:, k * TOK:(k + 1) * TOK]
                    return hTb[:, (k - 16) * TOK:(k - 15) * TOK]

                def hTpair(kk, tt):
                    base = hTa if kk < 16 else hTb
                    k0 = kk if kk < 16 else kk - 16
                    dv = base[:].rearrange("p (k t) -> p k t", t=TOK)
                    return dv[:, k0:k0 + 2, tt * 128:(tt + 1) * 128]

                L5 = ExitStack()
                with L5:
                    sb5 = lambda *a, **k: L5.enter_context(nc.sbuf_tensor(*a, **k))
                    t1s = [sb5("t1s%d" % i, [128, NFC * OS], BF16) for i in range(2)]
                    t2s = [sb5("t2s%d" % i, [128, NFC * OS], BF16) for i in range(2)]
                    raw1 = [sb5("raw1%d" % i, [128, 1024], F32) for i in range(2)]
                    raw2 = [sb5("raw2%d" % i, [128, 1024], F32) for i in range(2)]
                    qa = sb5("qa", [128, 512], BF16)
                    qb = sb5("qb", [128, 512], BF16)
                    pby1 = [pb[0], pb[1], pb[4]]
                    pby2 = [pb[2], pb[3], pb[5]]
                    tmp1 = [sb5("tmp1%d" % i, [128, OS], F32) for i in range(4)]
                    tmp2 = [sb5("tmp2%d" % i, [128, OS], F32) for i in range(4)]
                    hsl = [sb5("hsl%d" % i, [128, OS], F16) for i in range(6)]

                    barrier()
                    d_r1 = {}
                    d_r2 = {}
                    v_q = {}
                    t_y = {}
                    s_y = {}
                    v_h = {}
                    t_h = {}
                    s_h = {}
                    for o in range(NOS):
                        for dg in range(4):
                            dS.n += 16
                            d_r1[o * 4 + dg] = dS.n
                            dS2.n += 16
                            d_r2[o * 4 + dg] = dS2.n
                    for o in range(NOS + 1):
                        if o < NOS:
                            for g in range(8):
                                v_q[o * 8 + g] = inc(cV)
                        if o >= 1:
                            for tt in range(NTT):
                                v_h[(o - 1, tt)] = inc(cV)
                    for step in range(NOS * NTT):
                        o, tt = divmod(step, NTT)
                        t_y[(o, tt)] = inc(cT)
                        if step >= 5:
                            po, ptt = divmod(step - 5, NTT)
                            t_h[(po, ptt)] = inc(cT)
                    for ps in range(NOS * NTT - 5, NOS * NTT):
                        t_h[(ps // 8, ps % 8)] = inc(cT)
                    for step in range(NOS * NTT):
                        s_y[(step // 8, step % 8)] = inc(cS)
                        if step >= 4:
                            s_h[((step - 4) // 8, (step - 4) % 8)] = inc(cS)
                    for ps in range(NOS * NTT - 4, NOS * NTT):
                        s_h[(ps // 8, ps % 8)] = inc(cS)
                    s_p4_done = inc(cS)

                    def _p4_sync(e):
                        for o in range(NOS):
                            for dg in range(4):
                                dgidx = o * 4 + dg
                                if dgidx >= 2:
                                    e.wait_ge(cV.h, v_q[(dgidx - 2) * 2 + 1])
                                e.dma_start(
                                    out=raw1[dgidx % 2][:].rearrange(
                                        "p (c f) -> p c f", f=OS),
                                    in_=dw1_c[:, 4 * dg:4 * dg + 4,
                                              o * OS:(o + 1) * OS]
                                ).then_inc(dS.h, 16)
                    PQ.add("sync", _p4_sync)

                    def _p4_gp(e):
                        for o in range(NOS):
                            for dg in range(4):
                                dgidx = o * 4 + dg
                                if dgidx >= 2:
                                    e.wait_ge(cV.h, v_q[(dgidx - 2) * 2 + 1])
                                e.dma_start(
                                    out=raw2[dgidx % 2][:].rearrange(
                                        "p (c f) -> p c f", f=OS),
                                    in_=dw2_c[:, 4 * dg:4 * dg + 4,
                                              o * OS:(o + 1) * OS]
                                ).then_inc(dS2.h, 16)
                    PQ.add("gpsimd", _p4_gp)

                    def _p4_s(e):
                        nstep = NOS * NTT

                        def hTev(ps):
                            po, ptt = divmod(ps, NTT)
                            e.wait_ge(cT.h, t_h[(po, ptt)] +
                                      (1 if ps < nstep - 2 else 0))
                            tco = 256 * ((ps // 2) % 2)
                            bank = pb[6 + (ps % 2)]
                            e.activation(hTpair(2 * po, ptt),
                                         bank[:, tco:tco + 256],
                                         AF.Identity).then_inc(cS.h, 1)

                        for step in range(nstep):
                            o, tt = divmod(step, NTT)
                            if step < nstep - 1:
                                nx = step + 1
                                e.wait_ge(cT.h, t_y[(nx // 8, nx % 8)])
                            else:
                                e.wait_ge(cT.h, t_y[(o, tt)])
                            if step >= 4:
                                prev = step - 4
                                e.wait_ge(cV.h, v_h[(prev // 8, prev % 8)])
                            pY1 = pby1[step % 3]
                            pY2 = pby2[step % 3]
                            e.activation(tmp1[step % 4][:], pY1[:, 0:OS],
                                         AF.Silu, scale=stf[:, 56 + tt:57 + tt])
                            e.activation(tmp2[step % 4][:], pY2[:, 0:OS],
                                         AF.Identity,
                                         scale=stf[:, 64 + tt:65 + tt])
                            e.drain().then_inc(cS.h, 1)
                            if step >= 4:
                                hTev(step - 4)
                        for ps in range(nstep - 4, nstep):
                            hTev(ps)
                        e.drain().then_inc(cS.h, 1)
                    PQ.add("scalar", _p4_s)

                    def _p4_v(e):
                        for o in range(NOS + 1):
                            if o < NOS:
                                for g in range(8):
                                    gidx = o * 8 + g
                                    dgidx = o * 4 + g // 2
                                    if g % 2 == 0:
                                        e.wait_ge(dS.h, d_r1[dgidx])
                                        e.wait_ge(dS2.h, d_r2[dgidx])
                                    if o >= 2 and g == 0:
                                        e.wait_ge(cT.h, t_y[(o - 2, NTT - 1)])
                                    half = (g % 2) * 512
                                    for mat, rawb, tds, thrc in (
                                            (0, raw1[dgidx % 2], t1s[o % 2], 9),
                                            (1, raw2[dgidx % 2], t2s[o % 2], 11)):
                                        raw = rawb[:, half:half + 512]
                                        e.tensor_scalar(
                                            qa[:], raw,
                                            swcol[:, thrc + 1:thrc + 2],
                                            None, ALU.is_lt)
                                        ins = e.scalar_tensor_tensor(
                                            tds[:, 2 * g * OS:(2 * g + 2) * OS],
                                            raw, swcol[:, thrc:thrc + 1],
                                            qa[:], ALU.is_gt, ALU.subtract)
                                    if g == 7:
                                        e.drain().then_inc(cV.h, 1)
                                    else:
                                        ins.then_inc(cV.h, 1)
                            if o >= 1:
                                oo = o - 1
                                for tt in range(NTT):
                                    step = oo * 8 + tt
                                    e.wait_ge(cS.h, s_y[(oo, tt)])
                                    if step >= 6:
                                        prev = step - 6
                                        e.wait_ge(cS.h, s_h[(prev // 8, prev % 8)])
                                    e.tensor_tensor(hsl[step % 6][:],
                                                    tmp1[step % 4][:],
                                                    tmp2[step % 4][:], ALU.mult)
                                    e.reduce_max(
                                        hmx[:, tt * 22 + oo:tt * 22 + oo + 1],
                                        hsl[step % 6][:], axis=AX.X,
                                        apply_absolute_value=True)
                                    e.drain().then_inc(cV.h, 1)
                    PQ.add("vector", _p4_v)

                    def _p4_t(e):
                        nstep = NOS * NTT

                        def trans(ps):
                            po, ptt = divmod(ps, NTT)
                            e.wait_ge(cV.h, v_h[(po, ptt)])
                            if ps >= 2:
                                p2 = ps - 2
                                e.wait_ge(cS.h, s_h[(p2 // 8, p2 % 8)])
                            tco = 256 * ((ps // 2) % 2)
                            bank = pb[6 + (ps % 2)]
                            for j in range(2):
                                ins = e.matmul(
                                    bank[:, tco + j * 128:tco + (j + 1) * 128],
                                    hsl[ps % 6][:, j * 128:(j + 1) * 128],
                                    ident16[:], start=True, stop=True)
                            if ps >= nstep - 3:
                                e.drain().then_inc(cT.h, 1)
                            else:
                                ins.then_inc(cT.h, 1)

                        for step in range(nstep):
                            o, tt = divmod(step, NTT)
                            e.wait_ge(cV.h, v_q[o * 8 + 7])
                            if step >= 3:
                                prev = step - 3
                                e.wait_ge(cS.h, s_y[(prev // 8, prev % 8)])
                            pY1 = pby1[step % 3]
                            pY2 = pby2[step % 3]
                            for i in range(NFC):
                                lhsT = rqT[:, i * TOK + tt * 128:
                                           i * TOK + (tt + 1) * 128]
                                e.matmul(pY1[:, 0:OS], lhsT,
                                         t1s[o % 2][:, i * OS:(i + 1) * OS],
                                         start=(i == 0), stop=(i == NFC - 1))
                                ins = e.matmul(pY2[:, 0:OS], lhsT,
                                               t2s[o % 2][:, i * OS:(i + 1) * OS],
                                               start=(i == 0), stop=(i == NFC - 1))
                            if step == nstep - 1:
                                e.drain().then_inc(cT.h, 1)
                            else:
                                ins.then_inc(cT.h, 1)
                            if step >= 5:
                                trans(step - 5)
                        for ps in range(nstep - 5, nstep):
                            trans(ps)
                    PQ.add("tensor", _p4_t)

                dbg_dump([("dbg_hTa", hTa[:]), ("dbg_hTb", hTb[:]),
                          ("dbg_stf2", stf[:]), ("dbg_sw", swcol[:])])
                L6 = ExitStack()
                with L6:
                    sb6 = lambda *a, **k: L6.enter_context(nc.sbuf_tensor(*a, **k))
                    p56 = sb6("p56", [128, 5632], F32)
                    lad6 = sb6("lad6", [128, 2048], F32)
                    xt6 = [sb6("xt6_%d" % i, [128, 512], F32) for i in range(2)]
                    ev6 = sb6("ev6", [128, 512], F32)
                    fin6 = sb6("fin6", [128, 512], F32)
                    qa3 = sb6("qa3", [128, 2816], F16)
                    srow = sb6("srow", [1, 2048], F32)
                    s3r_b = sb6("s3r_b", [128, 1024], F32)
                    qs2r_b = sb6("qs2r_b", [128, 1024], F32)
                    t3s = [rqT[:, i * FF:(i + 1) * FF].bitcast(F16)
                           for i in range(2)]

                    def lad_chunk(e, k, c0, scr, bare=True):
                        u = scr[:, 0:512]
                        y0 = scr[:, 512:1024]
                        w1_ = scr[:, 1024:1536]
                        w2_ = scr[:, 1536:2048]
                        hk = hT(k)[:, c0:c0 + 512]
                        e.tensor_tensor(w1_[:], hk, qs2r_b[:, c0:c0 + 512],
                                        ALU.mult)
                        e.tensor_single_scalar(y0[:], w1_[:], 128.5, ALU.add)
                        srk = y0
                        dsts = [w1_, w2_]
                        for wi, P in enumerate(LADDER):
                            dst = dsts[wi % 2]
                            e.tensor_scalar(u[:], srk[:], P, -P,
                                            ALU.is_gt, ALU.mult)
                            e.tensor_tensor(dst[:], srk[:], u[:], ALU.add)
                            srk = dst
                        ins = e.scalar_tensor_tensor(hk, y0[:], 128.0, srk[:],
                                                     ALU.subtract, ALU.subtract)
                        if bare:
                            ins.then_inc(cV.h, 1)
                        else:
                            e.drain().then_inc(cV.h, 1)

                    barrier()
                    v_qs2 = inc(cV)
                    dG.n += 16 * 16
                    d_srow = dG.n
                    t_row = inc(cT)
                    v_rows = inc(cV)
                    for k in range(NKC):
                        inc(cV)
                    v_p5_done = inc(cV)
                    d_t3 = {}
                    dS.n += 16
                    d_t3[(0, 0, 0)] = dS.n
                    dS2.n += 16
                    d_t3[(0, 0, 1)] = dS2.n

                    def _p5_v(e):
                        for tt in range(NTT):
                            e.reduce_max(stf[:, 40 + tt:41 + tt],
                                         hmx[:, tt * 22:(tt + 1) * 22],
                                         axis=AX.X, apply_absolute_value=True)
                        e.drain()
                        e.tensor_scalar_max(stf[:, 88:96], stf[:, 40:48], 1e-5)
                        e.drain()
                        e.reciprocal(stf[:, 72:80], stf[:, 88:96])
                        e.drain()
                        e.tensor_scalar_mul(stf[:, 48:56], stf[:, 72:80], 127.0)
                        e.drain()
                        e.reciprocal(stf[:, 72:80], stf[:, 48:56])
                        e.drain()
                        e.tensor_scalar(stf[:, 80:88], stf[:, 72:80],
                                        swcol[:, 8:9], None, ALU.mult)
                        e.drain().then_inc(cV.h, 1)
                        e.wait_ge(cT.h, t_row)
                        e.tensor_copy(qs2r_b[:, 0:512], pb[4][:, 0:512])
                        e.tensor_copy(qs2r_b[:, 512:1024], pb[5][:, 0:512])
                        e.tensor_copy(s3r_b[:, 0:512], pb[6][:, 0:512])
                        e.tensor_copy(s3r_b[:, 512:1024], pb[7][:, 0:512])
                        e.drain().then_inc(cV.h, 1)
                        for k in range(NKC):
                            lad_chunk(e, k, 0, lad6)
                        e.drain().then_inc(cV.h, 1)
                    PQ.add("vector", _p5_v)

                    def _p5_gp(e):
                        e.dma_start(
                            out=p56[:, 2816:5632].rearrange(
                                "p (c f) -> p c f", f=128),
                            in_=dw3_c[:, 22:44, 0:128]).then_inc(dS2.h, 16)
                        e.wait_ge(cV.h, v_qs2)
                        for tt in range(NTT):
                            e.dma_start(out=srow[0:1, tt * 128:(tt + 1) * 128],
                                        in_=stf[:, 48 + tt:49 + tt],
                                        allow_slow_non_contiguous=True
                                        ).then_inc(dG.h, 16)
                            e.dma_start(out=srow[0:1, 1024 + tt * 128:
                                                1152 + tt * 128],
                                        in_=stf[:, 80 + tt:81 + tt],
                                        allow_slow_non_contiguous=True
                                        ).then_inc(dG.h, 16)
                    PQ.add("gpsimd", _p5_gp)

                    def _p5_sync(e):
                        e.dma_start(
                            out=p56[:, 0:2816].rearrange(
                                "p (c f) -> p c f", f=128),
                            in_=dw3_c[:, 0:22, 0:128]).then_inc(dS.h, 16)
                    PQ.add("sync", _p5_sync)

                    def _p5_t(e):
                        e.wait_ge(dG.h, d_srow)
                        e.matmul(pb[4][:, 0:512], onesr[:], srow[0:1, 0:512],
                                 start=True, stop=True)
                        e.matmul(pb[5][:, 0:512], onesr[:], srow[0:1, 512:1024],
                                 start=True, stop=True)
                        e.matmul(pb[6][:, 0:512], onesr[:], srow[0:1, 1024:1536],
                                 start=True, stop=True)
                        e.matmul(pb[7][:, 0:512], onesr[:], srow[0:1, 1536:2048],
                                 start=True, stop=True)
                        e.drain().then_inc(cT.h, 1)
                    PQ.add("tensor", _p5_t)

                    dbg_dump([("dbg_q2a", hTa[:]), ("dbg_q2b", hTb[:]),
                              ("dbg_qs2r", qs2r_b[:]), ("dbg_s3r", s3r_b[:])])
                    barrier()
                    sched = [3] * 12 + [2] * 4
                    slot_ks = []
                    kk = 0
                    for o3 in range(NFC):
                        slot_ks.append(list(range(kk, kk + sched[o3])))
                        kk += sched[o3]
                    assert kk == NKC
                    d_x6 = {}
                    d_fst = {}
                    v_q3 = {}
                    v_ladB = {}
                    v_f = {}
                    t_y3 = {}
                    for p in range(2):
                        for o3 in range(NFC):
                            if (p, o3) == (0, 0):
                                continue
                            dS.n += 16
                            d_t3[(p, o3, 0)] = dS.n
                            dS2.n += 16
                            d_t3[(p, o3, 1)] = dS2.n
                    for p in range(2):
                        for o3 in range(NFC):
                            dA.n += 16
                            d_x6[(p, o3)] = dA.n
                            dA.n += 16
                            d_fst[(p, o3)] = dA.n
                    n_out = d_fst[(1, NFC - 1)]
                    for p in range(2):
                        for o3 in range(NFC):
                            idx = p * NFC + o3
                            if idx == 0:
                                for j in range(2):
                                    v_q3[(0, 0, j)] = inc(cV)
                            if p == 0:
                                for k in slot_ks[o3]:
                                    v_ladB[k] = inc(cV)
                            if idx + 1 < 2 * NFC:
                                np_, no3 = (idx + 1) // NFC, (idx + 1) % NFC
                                for j in range(2):
                                    v_q3[(np_, no3, j)] = inc(cV)
                            v_f[(p, o3)] = inc(cV)
                    for p in range(2):
                        for o3 in range(NFC):
                            t_y3[(p, o3)] = inc(cT)

                    def skey(n):
                        return (n // 32, (n % 32) // 2, n % 2)

                    def _p6_sync(e):
                        for idx in range(1, 2 * NFC):
                            p, o3 = idx // NFC, idx % NFC
                            e.wait_ge(cV.h, v_q3[(( idx - 1) // NFC,
                                                  (idx - 1) % NFC, 0)])
                            e.dma_start(
                                out=p56[:, 0:2816].rearrange(
                                    "p (c f) -> p c f", f=128),
                                in_=dw3_c[:, 0:22,
                                          o3 * 128:(o3 + 1) * 128]
                            ).then_inc(dS.h, 16)
                    PQ.add("sync", _p6_sync)

                    def _p6_s(e):
                        for idx in range(2 * NFC):
                            p, o3 = idx // NFC, idx % NFC
                            if idx >= 2:
                                pi = idx - 2
                                e.wait_ge(cV.h, v_f[(pi // NFC, pi % NFC)])
                            e.dma_start(
                                out=xt6[idx % 2][:],
                                in_=dout[o3 * 128:(o3 + 1) * 128,
                                         p * 512:(p + 1) * 512]
                            ).then_inc(dA.h, 16)
                            e.wait_ge(cV.h, v_f[(p, o3)])
                            e.dma_start(
                                out=dout[o3 * 128:(o3 + 1) * 128,
                                         p * 512:(p + 1) * 512],
                                in_=fin6[:]).then_inc(dA.h, 16)
                        e.wait_ge(dA.h, n_out)
                    PQ.add("scalar", _p6_s)

                    def _p6_gp(e):
                        for idx in range(1, 2 * NFC):
                            p, o3 = idx // NFC, idx % NFC
                            e.wait_ge(cV.h, v_q3[((idx - 1) // NFC,
                                                  (idx - 1) % NFC, 1)])
                            e.dma_start(
                                out=p56[:, 2816:5632].rearrange(
                                    "p (c f) -> p c f", f=128),
                                in_=dw3_c[:, 22:44,
                                          o3 * 128:(o3 + 1) * 128]
                            ).then_inc(dS2.h, 16)
                    PQ.add("gpsimd", _p6_gp)

                    def _p6_v(e):
                        def quant(idx):
                            p, o3 = idx // NFC, idx % NFC
                            for j in range(2):
                                e.wait_ge(dS.h if j == 0 else dS2.h,
                                          d_t3[(p, o3, j)])
                                if j == 0 and idx >= 2:
                                    pi = idx - 2
                                    e.wait_ge(cT.h, t_y3[(pi // NFC,
                                                          pi % NFC)])
                                raw = p56[:, j * 2816:(j + 1) * 2816]
                                dst = t3s[idx % 2][:, j * 2816:(j + 1) * 2816]
                                e.tensor_scalar(qa3[:], raw, swcol[:, 14:15],
                                                None, ALU.is_lt)
                                ins = e.scalar_tensor_tensor(
                                    dst, raw, swcol[:, 13:14],
                                    qa3[:], ALU.is_gt, ALU.subtract)
                                if j == 0:
                                    ins.then_inc(cV.h, 1)
                                else:
                                    e.drain().then_inc(cV.h, 1)

                        for p in range(2):
                            for o3 in range(NFC):
                                idx = p * NFC + o3
                                if idx == 0:
                                    quant(0)
                                if p == 0:
                                    for ki, k in enumerate(slot_ks[o3]):
                                        last = (o3 == NFC - 1 and
                                                ki == len(slot_ks[o3]) - 1)
                                        lad_chunk(e, k, 512, lad6,
                                                  bare=not last)
                                if idx + 1 < 2 * NFC:
                                    quant(idx + 1)
                                e.wait_ge(cT.h, t_y3[(p, o3)])
                                e.wait_ge(dA.h, d_x6[(p, o3)])
                                if idx >= 1:
                                    pidx = idx - 1
                                    e.wait_ge(dA.h, d_fst[(pidx // NFC,
                                                           pidx % NFC)])
                                bank = pb[idx % 4]
                                e.tensor_tensor(ev6[:], bank[:, 0:512],
                                                s3r_b[:, p * 512:(p + 1) * 512],
                                                ALU.mult)
                                e.tensor_tensor(fin6[:], xt6[idx % 2][:],
                                                ev6[:], ALU.add)
                                e.drain().then_inc(cV.h, 1)
                    PQ.add("vector", _p6_v)

                    def _p6_t(e):
                        v_ladB_last = max(v_ladB.values())
                        for p in range(2):
                            for o3 in range(NFC):
                                idx = p * NFC + o3
                                e.wait_ge(cV.h, v_q3[(p, o3, 1)])
                                if p == 1 and o3 == 0:
                                    e.wait_ge(cV.h, v_ladB_last)
                                if idx >= 4:
                                    pidx = idx - 4
                                    e.wait_ge(cV.h, v_f[(pidx // NFC,
                                                         pidx % NFC)])
                                bank = pb[idx % 4]
                                for k in range(NKC):
                                    lhsT = t3s[idx % 2][:, k * 128:(k + 1) * 128]
                                    hk = hT(k)[:, p * 512:(p + 1) * 512]
                                    e.matmul(bank[:, 0:512], lhsT, hk,
                                             start=(k == 0), stop=(k == NKC - 1))
                                e.drain().then_inc(cT.h, 1)
                    PQ.add("tensor", _p6_t)

                    PQ.flush()
    return nc


def kernel(L0_attn_hidden_states, L0_attn_Wq, L0_attn_Wk, L0_attn_Wv, L0_attn_Wo,
           L0_attn_bank_weights, L0_attn_memories, L0_attn_memory_norms,
           L0_ffn_w1, L0_ffn_w2, L0_ffn_w3):
    f32 = np.float32
    x = np.ascontiguousarray(np.asarray(L0_attn_hidden_states, f32).reshape(8192, HID))
    wqT = np.ascontiguousarray(np.asarray(L0_attn_Wq, f32).T)
    woT = np.ascontiguousarray(np.asarray(L0_attn_Wo, f32).T)
    w1T = np.ascontiguousarray(np.asarray(L0_ffn_w1, f32).T)
    w2T = np.ascontiguousarray(np.asarray(L0_ffn_w2, f32).T)
    w3T = np.ascontiguousarray(np.asarray(L0_ffn_w3, f32).T)
    bw = np.ascontiguousarray(np.asarray(L0_attn_bank_weights, f32))
    mem = np.asarray(L0_attn_memories, f32)
    mn = np.asarray(L0_attn_memory_norms, f32)
    mpm = np.empty((NH, 128, 512), f32)
    mpn = np.empty((128, 64), f32)
    for h in range(NH):
        mpm[h] = np.concatenate([mem[0, h], mem[1, h], mem[2, h], mem[3, h]],
                                axis=1)
        for n in range(4):
            mpn[:, 4 * h + n] = mn[n, h]

    nc = build()
    in_maps = []
    for c in range(8):
        xs = x[c * TOK:(c + 1) * TOK]
        in_maps.append(dict(x_tok=np.ascontiguousarray(xs),
                            xT=np.ascontiguousarray(xs.T),
                            wqT=wqT, woT=woT, mpm=mpm, mpn=mpn, bw=bw,
                            w1T=w1T, w2T=w2T, w3T=w3T))
    res = run_bass_kernel_spmd(nc, in_maps, core_ids=list(range(8)))
    out = np.empty((8192, HID), f32)
    for c in range(8):
        out[c * TOK:(c + 1) * TOK] = np.asarray(res.results[c]["out"]).T
    return out.reshape(2, 4096, HID)


# version 27 (final)
# speedup vs baseline: 1.0633x; 1.0019x over previous
import numpy as np
import concourse.bass as bass
import concourse.mybir as mybir
from concourse.bass_utils import run_bass_kernel_spmd
from contextlib import ExitStack

F32 = mybir.dt.float32
BF16 = mybir.dt.bfloat16
F16 = mybir.dt.float16
AF = mybir.ActivationFunctionType
ALU = mybir.AluOpType
AX = mybir.AxisListType

TOK = 1024
NTT = 8
HID = 2048
NFC = 16
FF = 5632
NKC = 44
NH = 16
OS = 256
NOS = FF // OS
WMEAN_N = float(FF * HID)
LADDER = [128.0, 64.0, 32.0, 16.0, 8.0, 4.0, 2.0, 1.0]


class Ctr:
    def __init__(self, stack, nc, name):
        self.h = stack.enter_context(nc.semaphore(name))
        self.n = 0


class Q:

    def __init__(self, block):
        self.block = block
        self.qs = {k: [] for k in ("sync", "gpsimd", "vector", "scalar", "tensor")}

    def add(self, eng, fn):
        self.qs[eng].append(fn)

    def flush(self):
        for eng, cmds in self.qs.items():
            if not cmds:
                continue
            snapshot = list(cmds)
            cmds.clear()

            def body(e, snapshot=snapshot):
                for c in snapshot:
                    c(e)
            getattr(self.block, eng)(body)


def build(debug=False):
    nc = bass.Bass("TRN2", target_bir_lowering=False,
                   detect_race_conditions=False)
    dx = nc.dram_tensor("x_tok", [TOK, HID], F32, kind="ExternalInput").ap()
    dxT = nc.dram_tensor("xT", [HID, TOK], F32, kind="ExternalInput").ap()
    dwq = nc.dram_tensor("wqT", [HID, HID], F32, kind="ExternalInput").ap()
    dwo = nc.dram_tensor("woT", [HID, HID], F32, kind="ExternalInput").ap()
    dmpm = nc.dram_tensor("mpm", [NH, 128, 512], F32, kind="ExternalInput").ap()
    dmpn = nc.dram_tensor("mpn", [128, 64], F32, kind="ExternalInput").ap()
    dbw = nc.dram_tensor("bw", [16, 4], F32, kind="ExternalInput").ap()
    dw1 = nc.dram_tensor("w1T", [HID, FF], F32, kind="ExternalInput").ap()
    dw2 = nc.dram_tensor("w2T", [HID, FF], F32, kind="ExternalInput").ap()
    dw3 = nc.dram_tensor("w3T", [FF, HID], F32, kind="ExternalInput").ap()
    dout = nc.dram_tensor("out", [HID, TOK], F32, kind="ExternalOutput").ap()

    dwq_c = dwq.rearrange("(c p) o -> p c o", p=128)
    dwo_c = dwo.rearrange("(c p) o -> p c o", p=128)
    dw1_c = dw1.rearrange("(c p) f -> p c f", p=128)
    dw2_c = dw2.rearrange("(c p) f -> p c f", p=128)
    dw3_c = dw3.rearrange("(c p) o -> p c o", p=128)
    dmpm_t = dmpm.transpose([1, 0, 2])

    dbg = {}
    if debug:
        for nm, shape, dt_ in [
                ("dbg_xnT", [128, NFC * TOK], BF16),
                ("dbg_rqT", [128, NFC * TOK], BF16),
                ("dbg_stf1", [128, 96], F32),
                ("dbg_sqT", [128, NH * TOK], BF16),
                ("dbg_combT", [128, NFC * TOK], BF16),
                ("dbg_sw", [128, 16], F32),
                ("dbg_hTa", [128, 16 * TOK], F16),
                ("dbg_hTb", [128, 28 * TOK], F16),
                ("dbg_stf2", [128, 96], F32),
                ("dbg_q2a", [128, 16 * TOK], F16),
                ("dbg_q2b", [128, 28 * TOK], F16),
                ("dbg_qs2r", [128, TOK], F32),
                ("dbg_s3r", [128, TOK], F32),
                ("dbg_wab", [128, 64], F32)]:
            dbg[nm] = nc.dram_tensor(nm, shape, dt_, kind="ExternalOutput").ap()

    SG = ExitStack()
    with SG:
        sb = lambda *a, **k: SG.enter_context(nc.sbuf_tensor(*a, **k))
        ps = lambda *a: SG.enter_context(nc.psum_tensor(*a))
        ident = sb("ident", [128, 128], BF16)
        ident16 = sb("ident16", [128, 128], F16)
        iotaf = sb("iotaf", [128, 128], F32)
        onesr = sb("onesr", [1, 128], F32)
        onescol = sb("onescol", [128, 1], F32)
        wa_b = sb("wa_b", [128, 64], F32)
        stf = sb("stf", [128, 96], F32)
        sttmp = sb("sttmp", [128, 8], F32)
        swcol = sb("swcol", [128, 16], F32)
        hmx = sb("hmx", [128, 176], F32)
        wsums = sb("wsums", [128, 144], F32)
        wsp = [sb("wsp%d" % i, [128, 2048], F32) for i in range(2)]
        sm_t = sb("sm_t", [16, 32], F32)
        act4 = sb("act4", [128, 4], F32)
        smallrow = sb("smallrow", [1, 64], F32)
        wsrun = sb("wsrun", [128, 4], F32)
        rn64 = sb("rn64", [128, 64], F32)
        rn64b = sb("rn64b", [128, 64], F32)
        ccol = sb("ccol", [128, 2], F32)
        sc11 = sb("sc11", [1, 8], F32)
        pb = [ps("pb%d" % i, [128, 512], F32) for i in range(8)]

        dS = Ctr(SG, nc, "dS")
        dS2 = Ctr(SG, nc, "dS2")
        dG = Ctr(SG, nc, "dG")
        cV = Ctr(SG, nc, "cV")
        cS = Ctr(SG, nc, "cS")
        cT = Ctr(SG, nc, "cT")
        cG = Ctr(SG, nc, "cG")
        dA = Ctr(SG, nc, "dA")
        dX = Ctr(SG, nc, "dX")
        ALLC = [dS, dS2, dG, cV, cS, cT, cG, dA, dX]
        dW = Ctr(SG, nc, "dW")
        dWb = Ctr(SG, nc, "dWb")
        cW = Ctr(SG, nc, "cW")
        block = SG.enter_context(nc.Block())
        PQ = Q(block)

        def inc(ctr, k=1):
            ctr.n += k
            return ctr.n

        def barrier():
            PQ.flush()
            snap = [(c.h, c.n) for c in ALLC if c.n > 0]

            def mk(engdec):
                @engdec
                def _(eng):
                    for h, v in snap:
                        eng.wait_ge(h, v)
            for ed in (block.sync, block.gpsimd, block.vector, block.scalar,
                       block.tensor):
                mk(ed)

        def dbg_dump(pairs):
            if not dbg:
                return
            PQ.flush()
            snap = [(c.h, c.n) for c in ALLC if c.n > 0]

            @block.sync
            def _(sync):
                for h, v in snap:
                    sync.wait_ge(h, v)
                for nm, srcap in pairs:
                    sync.dma_start(out=dbg[nm], in_=srcap).then_inc(dS.h, 16)
            dS.n += 16 * len(pairs)

        wstrips = []
        for m, (dmat, nchunks, width) in enumerate(
                [(dw1, NFC, FF), (dw2, NFC, FF), (dw3, NKC, HID)]):
            for c in range(nchunks):
                if width == FF:
                    wstrips.append((m, dmat, c, 0, 2048))
                    wstrips.append((m, dmat, c, 2048, 2048))
                    wstrips.append((m, dmat, c, 4096, 1536))
                else:
                    wstrips.append((m, dmat, c, 0, 2048))
        NSTRIPS = len(wstrips)
        ws_pos = [0]
        wg_pos = [0]

        def emit_strips(e, count):
            for _ in range(count):
                n = ws_pos[0]
                if n >= NSTRIPS:
                    return
                ws_pos[0] += 1
                m, dmat, c, off, wdt = wstrips[n]
                if n >= 2:
                    e.wait_ge(cW.h, n - 1)
                e.dma_start(out=wsp[n % 2][:, 0:wdt],
                            in_=dmat[c * 128:(c + 1) * 128, off:off + wdt]
                            ).then_inc((dW if n % 2 == 0 else dWb).h, 16)

        def emit_reduces(e, count, upto=None):
            cap = NSTRIPS if upto is None else min(upto, NSTRIPS)
            for _ in range(count):
                n = wg_pos[0]
                if n >= cap:
                    return
                wg_pos[0] += 1
                wdt = wstrips[n][4]
                e.wait_ge((dW if n % 2 == 0 else dWb).h,
                          16 * (n // 2 + 1))
                ins = e.reduce_sum(wsums[:, n:n + 1], wsp[n % 2][:, 0:wdt],
                                   axis=AX.X, apply_absolute_value=True)
                if n == NSTRIPS - 1:
                    e.drain().then_inc(cW.h, 1)
                else:
                    ins.then_inc(cW.h, 1)

        dS.n += 16
        p0_bw = dS.n
        PQ.add("sync", lambda e: e.dma_start(out=sm_t[0:16, 0:4], in_=dbw[:]
                                             ).then_inc(dS.h, 16))

        cG.n += 1
        p0_iota = cG.n
        dG.n += 16
        p0_norm = dG.n

        def _gp0(e):
            e.iota(iotaf[:], pattern=[[1, 128]], base=0, channel_multiplier=-1,
                   allow_small_or_imprecise_dtypes=True)
            e.drain().then_inc(cG.h, 1)
            e.dma_start(out=wa_b[:], in_=dmpn[:]).then_inc(dG.h, 16)
        PQ.add("gpsimd", _gp0)

        cV.n += 1
        p0_v1 = cV.n

        def _v0(e):
            e.wait_ge(cG.h, p0_iota)
            e.tensor_single_scalar(ident[:], iotaf[:], 0.0, ALU.is_equal)
            e.tensor_single_scalar(ident16[:], iotaf[:], 0.0, ALU.is_equal)
            e.memset(onesr[:], 1.0)
            e.memset(onescol[:], 1.0)
            e.memset(ccol[:, 0:1], 1e-5)
            e.memset(stf[:, 40:48], 0.0)
            e.memset(stf[:, 72:80], 0.0)
            e.wait_ge(dS.h, p0_bw)
            e.reduce_max(sm_t[0:16, 16:17], sm_t[0:16, 0:4], axis=AX.X)
            e.drain()
            e.tensor_scalar_mul(sm_t[0:16, 19:20], sm_t[0:16, 16:17], -1.0)
            e.wait_ge(dG.h, p0_norm)
            nb = wa_b[:].rearrange("p (h n) -> p h n", n=4)
            for n in range(4):
                e.reduce_sum(act4[:, n:n + 1], nb[:, :, n], axis=AX.X)
            e.drain().then_inc(cV.h, 1)
        PQ.add("vector", _v0)

        cS.n += 1
        p0_exp = cS.n

        def _s0(e):
            e.wait_ge(cV.h, p0_v1)
            e.activation(sm_t[0:16, 4:8], sm_t[0:16, 0:4], AF.Exp,
                         bias=sm_t[0:16, 19:20], scale=1.0)
            e.drain().then_inc(cS.h, 1)
        PQ.add("scalar", _s0)

        cT.n += 1
        p0_mm1 = cT.n

        def _t0(e):
            e.wait_ge(cV.h, p0_v1)
            e.matmul(pb[5][0:4, 0:1], act4[:], onescol[:], start=True, stop=True)
            e.drain().then_inc(cT.h, 1)
        PQ.add("tensor", _t0)

        cV.n += 1
        p0_v2 = cV.n

        def _v1(e):
            e.wait_ge(cS.h, p0_exp)
            e.reduce_sum(sm_t[0:16, 17:18], sm_t[0:16, 4:8], axis=AX.X)
            e.drain()
            e.reciprocal(sm_t[0:16, 18:19], sm_t[0:16, 17:18])
            e.drain()
            e.tensor_scalar(sm_t[0:16, 8:12], sm_t[0:16, 4:8],
                            sm_t[0:16, 18:19], None, ALU.mult)
            e.wait_ge(cT.h, p0_mm1)
            e.tensor_single_scalar(act4[0:4, 0:1], pb[5][0:4, 0:1], 1e-6,
                                   ALU.is_ge)
            e.drain().then_inc(cV.h, 1)
        PQ.add("vector", _v1)

        dG.n += 16
        p0_arow = dG.n

        def _gp1(e):
            e.wait_ge(cV.h, p0_v2)
            e.dma_start(out=smallrow[0:1, 0:4], in_=act4[0:4, 0:1],
                        allow_slow_non_contiguous=True).then_inc(dG.h, 16)
        PQ.add("gpsimd", _gp1)

        cT.n += 1
        p0_mm2 = cT.n

        def _t1(e):
            e.wait_ge(dG.h, p0_arow)
            e.matmul(pb[5][0:16, 4:8], onesr[0:1, 0:16], smallrow[0:1, 0:4],
                     start=True, stop=True)
            e.drain().then_inc(cT.h, 1)
        PQ.add("tensor", _t1)

        cV.n += 1
        p0_v3 = cV.n

        def _v2(e):
            e.wait_ge(cT.h, p0_mm2)
            e.tensor_tensor(sm_t[0:16, 12:16], sm_t[0:16, 8:12],
                            pb[5][0:16, 4:8], ALU.mult)
            e.drain().then_inc(cV.h, 1)
        PQ.add("vector", _v2)

        dG.n += 16
        p0_wrow = dG.n

        def _gp2(e):
            e.wait_ge(cV.h, p0_v3)
            e.dma_start(out=smallrow[0:1, 0:64], in_=sm_t[0:16, 12:16],
                        allow_slow_non_contiguous=True).then_inc(dG.h, 16)
        PQ.add("gpsimd", _gp2)

        cT.n += 1
        p0_mm3 = cT.n

        def _t2(e):
            e.wait_ge(dG.h, p0_wrow)
            e.matmul(pb[5][:, 8:72], onesr[:], smallrow[0:1, 0:64],
                     start=True, stop=True)
            e.drain().then_inc(cT.h, 1)
        PQ.add("tensor", _t2)

        cV.n += 1

        def _v3(e):
            e.wait_ge(cT.h, p0_mm3)
            e.tensor_copy(wa_b[:], pb[5][:, 8:72])
            e.drain().then_inc(cV.h, 1)
        PQ.add("vector", _v3)

        L1 = ExitStack()
        with L1:
            rqT = L1.enter_context(nc.sbuf_tensor("rqT", [128, NFC * TOK], BF16))
            L2 = ExitStack()
            with L2:
                sqT = L2.enter_context(nc.sbuf_tensor("sqT", [128, NH * TOK], BF16))
                mp16m = L2.enter_context(
                    nc.sbuf_tensor("mp16m", [128, NH * 512], BF16))
                nrm16 = L2.enter_context(nc.sbuf_tensor("nrm16", [128, 64], BF16))

                dG.n += 32
                d_mp = dG.n

                def _gpl(e):
                    e.dma_start(out=mp16m[:].rearrange("p (h c) -> p h c", c=512),
                                in_=dmpm_t[:]).then_inc(dG.h, 16)
                    e.dma_start(out=nrm16[:], in_=dmpn[:]).then_inc(dG.h, 16)
                PQ.add("gpsimd", _gpl)

                L3 = ExitStack()
                with L3:
                    sb3 = lambda *a, **k: L3.enter_context(nc.sbuf_tensor(*a, **k))
                    xa2 = [sb3("xa%d" % i, [128, HID], F32) for i in range(2)]
                    xsq = sb3("xsq", [128, HID], F32)
                    xn16 = [sb3("xn16_%d" % i, [128, HID], BF16) for i in range(2)]
                    rq16 = [sb3("rq16_%d" % i, [128, HID], BF16) for i in range(2)]
                    ub = sb3("ub", [128, HID], F32)
                    y0b = sb3("y0b", [128, HID], F32)
                    ywk = sb3("ywk", [128, HID], F32)
                    wrapc = xsq
                    xnT = sb3("xnT", [128, NFC * TOK], BF16)
                    wq_s = [sb3("wq_s%d" % i, [128, HID], BF16) for i in range(2)]
                    sigt = ywk[:, 0:512]
                    sige = ywk[:, 512:1024]

                    barrier()
                    d_x = [0] * NTT
                    v_var = [0] * NTT
                    s_ln = [0] * NTT
                    v_neg = [0] * NTT
                    s_rstd = [0] * NTT
                    v_p1 = [0] * NTT
                    t_tr = {}
                    s_tr = {}
                    for tt in range(NTT):
                        if tt % 2 == 0:
                            dS.n += 16
                            d_x[tt] = dS.n
                        else:
                            dX.n += 16
                            d_x[tt] = dX.n
                        v_var[tt] = inc(cV)
                        v_neg[tt] = inc(cV)
                        v_p1[tt] = inc(cV)
                    for tt in range(NTT):
                        for b_ in range(8):
                            t_tr[(tt, b_)] = inc(cT)
                    for tt in range(NTT):
                        s_ln[tt] = inc(cS)
                        s_rstd[tt] = inc(cS)
                        if tt >= 1:
                            for b_ in range(8):
                                s_tr[(tt - 1, b_)] = inc(cS)
                    for b_ in range(8):
                        s_tr[(NTT - 1, b_)] = inc(cS)
                    s_p1_done = inc(cS)

                    def _p1_sync(e):
                        for tt in range(NTT):
                            if tt >= 2:
                                e.wait_ge(cV.h, v_p1[tt - 2])
                            e.dma_start(out=xa2[tt % 2][:],
                                        in_=dx[tt * 128:(tt + 1) * 128, :]
                                        ).then_inc((dS if tt % 2 == 0
                                                    else dX).h, 16)
                            if tt >= 1:
                                emit_strips(e, 6)
                        emit_strips(e, 4)
                    PQ.add("sync", _p1_sync)



                    def _p1_vec(e):
                        for tt in range(NTT):
                            e.wait_ge((dS if tt % 2 == 0 else dX).h, d_x[tt])
                            if tt >= 2:
                                e.wait_ge(cS.h, s_tr[(tt - 2, 7)])
                            e.tensor_tensor(xsq[:], xa2[tt % 2][:], xa2[tt % 2][:], ALU.mult)
                            e.reduce_sum(sttmp[:, 0:1], xa2[tt % 2][:], axis=AX.X)
                            e.reduce_sum(sttmp[:, 1:2], xsq[:], axis=AX.X)
                            e.reduce_max(sttmp[:, 2:3], xa2[tt % 2][:], axis=AX.X)
                            e.tensor_reduce(sttmp[:, 3:4], xa2[tt % 2][:], AX.X, ALU.min)
                            e.drain()
                            e.tensor_scalar_mul(stf[:, 0 + tt:1 + tt],
                                                sttmp[:, 0:1], -1.0 / HID)
                            e.tensor_scalar_mul(sttmp[:, 6:7], sttmp[:, 1:2],
                                                1.0 / HID)
                            e.drain()
                            e.tensor_tensor(sttmp[:, 7:8], stf[:, 0 + tt:1 + tt],
                                            stf[:, 0 + tt:1 + tt], ALU.mult)
                            e.drain()
                            e.tensor_tensor(sttmp[:, 4:5], sttmp[:, 6:7],
                                            sttmp[:, 7:8], ALU.subtract)
                            e.drain().then_inc(cV.h, 1)
                            if tt >= 1:
                                emit_reduces(e, 6, upto=46)
                            e.wait_ge(cS.h, s_ln[tt])
                            e.tensor_scalar_mul(sttmp[:, 4:5], sttmp[:, 5:6], -0.5)
                            e.drain().then_inc(cV.h, 1)
                            e.wait_ge(cS.h, s_rstd[tt])
                            e.tensor_tensor(sttmp[:, 6:7], sttmp[:, 2:3],
                                            stf[:, 0 + tt:1 + tt], ALU.add)
                            e.tensor_tensor(sttmp[:, 7:8], sttmp[:, 3:4],
                                            stf[:, 0 + tt:1 + tt], ALU.add)
                            e.drain()
                            e.tensor_scalar_mul(sttmp[:, 5:6], sttmp[:, 7:8], -1.0)
                            e.drain()
                            e.tensor_tensor(sttmp[:, 2:3], sttmp[:, 6:7],
                                            sttmp[:, 5:6], ALU.max)
                            e.drain()
                            e.tensor_tensor(sttmp[:, 6:7], sttmp[:, 2:3],
                                            stf[:, 8 + tt:9 + tt], ALU.mult)
                            e.drain()
                            e.tensor_scalar_max(sttmp[:, 7:8], sttmp[:, 6:7], 1e-5)
                            e.drain()
                            e.reciprocal(sttmp[:, 6:7], sttmp[:, 7:8])
                            e.drain()
                            e.tensor_scalar_mul(stf[:, 16 + tt:17 + tt],
                                                sttmp[:, 6:7], 127.0)
                            e.drain()
                            e.reciprocal(stf[:, 24 + tt:25 + tt],
                                         stf[:, 16 + tt:17 + tt])
                            e.drain()
                            e.tensor_tensor(stf[:, 32 + tt:33 + tt],
                                            stf[:, 8 + tt:9 + tt],
                                            stf[:, 16 + tt:17 + tt], ALU.mult)
                            e.drain()
                            e.tensor_scalar(xn16[tt % 2][:], xa2[tt % 2][:],
                                            stf[:, 0 + tt:1 + tt],
                                            stf[:, 8 + tt:9 + tt],
                                            ALU.add, ALU.mult)
                            e.tensor_scalar(ub[:], xa2[tt % 2][:],
                                            stf[:, 0 + tt:1 + tt],
                                            stf[:, 32 + tt:33 + tt],
                                            ALU.add, ALU.mult)
                            e.tensor_single_scalar(y0b[:], ub[:], 128.5, ALU.add)
                            src = y0b
                            dsts = [ywk, ub]
                            for wi, P in enumerate(LADDER):
                                dst = dsts[wi % 2]
                                e.tensor_scalar(wrapc[:], src[:], P, -P,
                                                ALU.is_gt, ALU.mult)
                                e.tensor_tensor(dst[:], src[:], wrapc[:], ALU.add)
                                src = dst
                            e.scalar_tensor_tensor(rq16[tt % 2][:], y0b[:], 128.0,
                                                   src[:], ALU.subtract,
                                                   ALU.subtract)
                            e.drain().then_inc(cV.h, 1)
                    PQ.add("vector", _p1_vec)

                    def _p1_evac(e, tt):
                        for b_ in range(8):
                            srcsel, g = divmod(b_, 4)
                            dstT = xnT if srcsel == 0 else rqT
                            bank = pb[6 + ((tt * 8 + b_) % 2)]
                            e.wait_ge(cT.h, t_tr[(tt, b_)])
                            dv = dstT[:].rearrange("p (c t) -> p c t", t=TOK)
                            e.activation(
                                dv[:, 4 * g:4 * g + 4, tt * 128:(tt + 1) * 128],
                                bank[:, 0:512], AF.Identity
                                ).then_inc(cS.h, 1)

                    def _p1_scal(e):
                        for tt in range(NTT):
                            e.wait_ge(cV.h, v_var[tt])
                            e.activation(sttmp[:, 5:6], sttmp[:, 4:5], AF.Ln,
                                         bias=ccol[:, 0:1], scale=1.0)
                            e.drain().then_inc(cS.h, 1)
                            e.wait_ge(cV.h, v_neg[tt])
                            e.activation(stf[:, 8 + tt:9 + tt], sttmp[:, 4:5],
                                         AF.Exp, scale=1.0)
                            e.drain().then_inc(cS.h, 1)
                            if tt >= 1:
                                _p1_evac(e, tt - 1)
                        _p1_evac(e, NTT - 1)
                        e.drain().then_inc(cS.h, 1)
                    PQ.add("scalar", _p1_scal)

                    def _p1_tens(e):
                        for tt in range(NTT):
                            e.wait_ge(cV.h, v_p1[tt])
                            for b_ in range(8):
                                srcsel, g = divmod(b_, 4)
                                src16 = xn16[tt % 2] if srcsel == 0 else rq16[tt % 2]
                                bank = pb[6 + ((tt * 8 + b_) % 2)]
                                gb = tt * 8 + b_
                                if gb >= 2:
                                    pv = gb - 2
                                    e.wait_ge(cS.h, s_tr[(pv // 8, pv % 8)])
                                for j in range(4):
                                    c = 4 * g + j
                                    e.matmul(bank[:, j * 128:(j + 1) * 128],
                                             src16[:, c * 128:(c + 1) * 128],
                                             ident[:], start=True, stop=True)
                                e.drain().then_inc(cT.h, 1)
                    PQ.add("tensor", _p1_tens)

                    dbg_dump([("dbg_xnT", xnT[:]), ("dbg_rqT", rqT[:]),
                              ("dbg_stf1", stf[:])])
                    barrier()
                    d_wq = [0] * NFC
                    t_qp = [0] * NFC
                    v_min = {}
                    s_sig = {}
                    v_done = {}
                    for o in range(NFC):
                        dG.n += 16
                        d_wq[o] = dG.n
                        t_qp[o] = inc(cT)
                        for half in range(2):
                            v_min[(o, half)] = inc(cV)
                            v_done[(o, half)] = inc(cV)
                    v_p2a_done = inc(cV)
                    for o in range(NFC):
                        for half in range(2):
                            s_sig[(o, half)] = inc(cS)

                    def _p2a_gp(e):
                        for o in range(NFC):
                            if o >= 2:
                                e.wait_ge(cT.h, t_qp[o - 2])
                            e.dma_start(
                                out=wq_s[o % 2][:].rearrange("p (c o2) -> p c o2",
                                                             o2=128),
                                in_=dwq_c[:, :, o * 128:(o + 1) * 128]
                            ).then_inc(dG.h, 16)
                    PQ.add("gpsimd", _p2a_gp)

                    def _p2a_sync(e):
                        emit_strips(e, 50)
                    PQ.add("sync", _p2a_sync)

                    def _p2a_t(e):
                        for o in range(NFC):
                            e.wait_ge(dG.h, d_wq[o])
                            if o >= 2:
                                e.wait_ge(cV.h, v_done[(o - 2, 1)])
                            pa, pbk = pb[2 * (o % 2)], pb[2 * (o % 2) + 1]
                            for i in range(NFC):
                                lhsT = wq_s[o % 2][:, i * 128:(i + 1) * 128]
                                e.matmul(pa[:, 0:512], lhsT,
                                         xnT[:, i * TOK:i * TOK + 512],
                                         start=(i == 0), stop=(i == NFC - 1))
                                e.matmul(pbk[:, 0:512], lhsT,
                                         xnT[:, i * TOK + 512:(i + 1) * TOK],
                                         start=(i == 0), stop=(i == NFC - 1))
                            e.drain().then_inc(cT.h, 1)
                    PQ.add("tensor", _p2a_t)

                    def _p2a_v(e):
                        for o in range(NFC):
                            e.wait_ge(cT.h, t_qp[o])
                            for half in range(2):
                                pban = pb[2 * (o % 2) + half]
                                e.tensor_scalar_min(sigt, pban[:, 0:512], 0.0)
                                e.drain().then_inc(cV.h, 1)
                                emit_reduces(e, 3, upto=96)
                                e.wait_ge(cS.h, s_sig[(o, half)])
                                e.scalar_tensor_tensor(
                                    sqT[:, o * TOK + half * 512:
                                        o * TOK + (half + 1) * 512],
                                    pban[:, 0:512], 0.0, sige,
                                    ALU.max, ALU.add
                                    ).then_inc(cV.h, 1)
                        e.drain().then_inc(cV.h, 1)
                    PQ.add("vector", _p2a_v)

                    def _p2a_s(e):
                        for o in range(NFC):
                            for half in range(2):
                                e.wait_ge(cV.h, v_min[(o, half)])
                                e.activation(sige, sigt, AF.Exp)
                                e.drain().then_inc(cS.h, 1)
                    PQ.add("scalar", _p2a_s)

                L3b = ExitStack()
                with L3b:
                    sb3b = lambda *a, **k: L3b.enter_context(nc.sbuf_tensor(*a, **k))
                    combT = sb3b("combT", [128, NFC * TOK], BF16)
                    comb16 = [sb3b("comb16_%d" % i, [128, HID], BF16)
                              for i in range(2)]
                    cadd = sb3b("cadd", [128, 256], F32)
                    wo_s = [sb3b("wo_s%d" % i, [128, HID], BF16) for i in range(2)]
                    xt_b = [sb3b("xt_b%d" % i, [128, TOK], F32) for i in range(2)]
                    ot_b = [sb3b("ot_b%d" % i, [128, TOK], F32) for i in range(2)]

                    dbg_dump([("dbg_sqT", sqT[:]), ("dbg_wab", wa_b[:])])
                    barrier()
                    t_nrm = [0] * NTT
                    v_rn = [0] * NTT
                    t_A = {}
                    v_cb = {}
                    v_cbd = [0] * NTT
                    t_ct = {}
                    s_ct = {}
                    for tt in range(NTT):
                        t_nrm[tt] = inc(cT)
                        for h in range(NH):
                            t_A[(tt, h)] = inc(cT)
                        for g in range(4):
                            t_ct[(tt, g)] = inc(cT)
                    for tt in range(NTT):
                        v_rn[tt] = inc(cV)
                        for h in range(NH):
                            v_cb[(tt, h)] = inc(cV)
                        v_cbd[tt] = inc(cV)
                    for tt in range(NTT):
                        for g in range(4):
                            s_ct[(tt, g)] = inc(cS)
                    s_p2b_done = inc(cS)

                    def _p2b_sync(e):
                        emit_strips(e, NSTRIPS)
                    PQ.add("sync", _p2b_sync)

                    def _p2b_t(e):
                        for tt in range(NTT):
                            if tt >= 1:
                                e.wait_ge(cV.h, v_rn[tt - 1])
                            for h in range(NH):
                                sq_blk = sqT[:, h * TOK + tt * 128:
                                             h * TOK + (tt + 1) * 128]
                                e.matmul(pb[4][:, 4 * h:4 * h + 4], sq_blk,
                                         nrm16[:, 4 * h:4 * h + 4],
                                         start=True, stop=True)
                            e.drain().then_inc(cT.h, 1)
                            for h in range(NH):
                                step = tt * NH + h
                                if step >= 4:
                                    pvs = step - 4
                                    e.wait_ge(cV.h, v_cb[(pvs // NH, pvs % NH)])
                                sq_blk = sqT[:, h * TOK + tt * 128:
                                             h * TOK + (tt + 1) * 128]
                                e.matmul(pb[step % 4][:, 0:512], sq_blk,
                                         mp16m[:, h * 512:(h + 1) * 512],
                                         start=True, stop=True)
                                e.drain().then_inc(cT.h, 1)
                            e.wait_ge(cV.h, v_cbd[tt])
                            for g in range(4):
                                ctb = tt * 4 + g
                                bank = pb[6 + (ctb % 2)]
                                if ctb >= 2:
                                    pv = ctb - 2
                                    e.wait_ge(cS.h, s_ct[(pv // 4, pv % 4)])
                                for j in range(4):
                                    c = 4 * g + j
                                    e.matmul(bank[:, j * 128:(j + 1) * 128],
                                             comb16[tt % 2][:, c * 128:(c + 1) * 128],
                                             ident[:], start=True, stop=True)
                                e.drain().then_inc(cT.h, 1)
                    PQ.add("tensor", _p2b_t)

                    def _p2b_v(e):
                        for tt in range(NTT):
                            emit_reduces(e, 6)
                            e.wait_ge(cT.h, t_nrm[tt])
                            if tt >= 2:
                                e.wait_ge(cS.h, s_ct[(tt - 2, 3)])
                            e.tensor_scalar_max(rn64[:], pb[4][:, 0:64], 1e-6)
                            e.drain()
                            e.reciprocal(rn64b[:], rn64[:])
                            e.drain()
                            e.tensor_tensor(rn64[:], rn64b[:], wa_b[:], ALU.mult)
                            e.drain().then_inc(cV.h, 1)
                            for h in range(NH):
                                e.wait_ge(cT.h, t_A[(tt, h)])
                                pA = pb[(tt * NH + h) % 4]
                                e.tensor_scalar(cadd[:, 0:128], pA[:, 0:128],
                                                rn64[:, 4 * h:4 * h + 1], None,
                                                ALU.mult)
                                e.scalar_tensor_tensor(cadd[:, 128:256],
                                                       pA[:, 128:256],
                                                       rn64[:, 4 * h + 1:4 * h + 2],
                                                       cadd[:, 0:128],
                                                       ALU.mult, ALU.add)
                                e.scalar_tensor_tensor(cadd[:, 0:128],
                                                       pA[:, 256:384],
                                                       rn64[:, 4 * h + 2:4 * h + 3],
                                                       cadd[:, 128:256],
                                                       ALU.mult, ALU.add)
                                e.scalar_tensor_tensor(
                                    comb16[tt % 2][:, h * 128:(h + 1) * 128],
                                    pA[:, 384:512],
                                    rn64[:, 4 * h + 3:4 * h + 4],
                                    cadd[:, 0:128], ALU.mult, ALU.add
                                    ).then_inc(cV.h, 1)
                            e.drain().then_inc(cV.h, 1)
                    PQ.add("vector", _p2b_v)

                    def _p2b_s(e):
                        for tt in range(NTT):
                            for g in range(4):
                                e.wait_ge(cT.h, t_ct[(tt, g)])
                                bank = pb[6 + ((tt * 4 + g) % 2)]
                                dv = combT[:].rearrange("p (c t) -> p c t", t=TOK)
                                e.activation(
                                    dv[:, 4 * g:4 * g + 4, tt * 128:(tt + 1) * 128],
                                    bank[:, 0:512], AF.Identity
                                    ).then_inc(cS.h, 1)
                        e.drain().then_inc(cS.h, 1)
                    PQ.add("scalar", _p2b_s)

                    dbg_dump([("dbg_combT", combT[:])])
                    barrier()
                    d_wo = [0] * NFC
                    d_xt = [0] * NFC
                    t_wo = [0] * NFC
                    v_wo = [0] * NFC
                    d_park = [0] * NFC
                    for o in range(NFC):
                        dG.n += 16
                        d_wo[o] = dG.n
                        dS.n += 16
                        d_xt[o] = dS.n
                        dS.n += 16
                        d_park[o] = dS.n
                        t_wo[o] = inc(cT)
                        v_wo[o] = inc(cV)

                    def _p2c_gp(e):
                        for o in range(NFC):
                            if o >= 2:
                                e.wait_ge(cT.h, t_wo[o - 2])
                            e.dma_start(
                                out=wo_s[o % 2][:].rearrange("p (c o2) -> p c o2",
                                                             o2=128),
                                in_=dwo_c[:, :, o * 128:(o + 1) * 128]
                            ).then_inc(dG.h, 16)
                    PQ.add("gpsimd", _p2c_gp)

                    def _p2c_sync_ws(e):
                        emit_strips(e, NSTRIPS)
                    PQ.add("sync", _p2c_sync_ws)

                    def _p2c_sync(e):
                        for o in range(NFC):
                            if o >= 2:
                                e.wait_ge(cV.h, v_wo[o - 2])
                            e.dma_start(out=xt_b[o % 2][:],
                                        in_=dxT[o * 128:(o + 1) * 128, :]
                                        ).then_inc(dS.h, 16)
                            e.wait_ge(cV.h, v_wo[o])
                            e.dma_start(out=dout[o * 128:(o + 1) * 128, :],
                                        in_=ot_b[o % 2][:]).then_inc(dS.h, 16)
                    PQ.add("sync", _p2c_sync)

                    def _p2c_t(e):
                        for o in range(NFC):
                            e.wait_ge(dG.h, d_wo[o])
                            if o >= 2:
                                e.wait_ge(cV.h, v_wo[o - 2])
                            pa, pbk = pb[2 * (o % 2)], pb[2 * (o % 2) + 1]
                            for k in range(NFC):
                                lhsT = wo_s[o % 2][:, k * 128:(k + 1) * 128]
                                e.matmul(pa[:, 0:512], lhsT,
                                         combT[:, k * TOK:k * TOK + 512],
                                         start=(k == 0), stop=(k == NFC - 1))
                                e.matmul(pbk[:, 0:512], lhsT,
                                         combT[:, k * TOK + 512:(k + 1) * TOK],
                                         start=(k == 0), stop=(k == NFC - 1))
                            e.drain().then_inc(cT.h, 1)
                    PQ.add("tensor", _p2c_t)

                    def _p2c_v(e):
                        for o in range(NFC):
                            emit_reduces(e, 3)
                            e.wait_ge(cT.h, t_wo[o])
                            e.wait_ge(dS.h, d_xt[o])
                            if o >= 2:
                                e.wait_ge(dS.h, d_park[o - 2])
                            pa, pbk = pb[2 * (o % 2)], pb[2 * (o % 2) + 1]
                            e.tensor_tensor(ot_b[o % 2][:, 0:512],
                                            xt_b[o % 2][:, 0:512],
                                            pa[:, 0:512], ALU.add)
                            e.tensor_tensor(ot_b[o % 2][:, 512:1024],
                                            xt_b[o % 2][:, 512:1024],
                                            pbk[:, 0:512], ALU.add)
                            e.drain().then_inc(cV.h, 1)
                    PQ.add("vector", _p2c_v)

            L3c = ExitStack()
            with L3c:
                barrier()
                mranges = [(0, 48), (48, 96), (96, 140)]
                t_sum = [0] * 3
                v_sw1 = [0] * 3
                t_bc = [0] * 3
                v_sw2 = [0] * 3
                v_wsum = [0] * 3
                for m in range(3):
                    v_wsum[m] = inc(cV)
                    v_sw1[m] = inc(cV)
                    v_sw2[m] = inc(cV)
                for m in range(3):
                    t_sum[m] = inc(cT)
                    t_bc[m] = inc(cT)

                def _p3_v(e):
                    emit_reduces(e, NSTRIPS)
                    e.drain()
                    for m in range(3):
                        lo, hi = mranges[m]
                        e.reduce_sum(wsrun[:, 0:1], wsums[:, lo:hi], axis=AX.X)
                        e.drain().then_inc(cV.h, 1)
                        e.wait_ge(cT.h, t_sum[m])
                        e.tensor_scalar_mul(sc11[0:1, 0:1], pb[5][0:1, 0:1],
                                            1.0 / WMEAN_N)
                        e.drain()
                        e.tensor_scalar_max(sc11[0:1, 1:2], sc11[0:1, 0:1], 1e-5)
                        e.drain()
                        e.reciprocal(sc11[0:1, 2:3], sc11[0:1, 1:2])
                        e.drain().then_inc(cV.h, 1)
                        e.wait_ge(cT.h, t_bc[m])
                        e.tensor_copy(swcol[:, 3 * m:3 * m + 1], pb[5][:, 100:101])
                        e.tensor_scalar_mul(swcol[:, 3 * m + 1:3 * m + 2],
                                            pb[5][:, 100:101], -1.0)
                        e.tensor_copy(swcol[:, 3 * m + 2:3 * m + 3],
                                      pb[5][:, 101:102])
                        e.tensor_scalar_mul(swcol[:, 9 + 2 * m:10 + 2 * m],
                                            pb[5][:, 101:102], 0.5)
                        e.tensor_scalar_mul(swcol[:, 10 + 2 * m:11 + 2 * m],
                                            pb[5][:, 101:102], -0.5)
                        e.drain()
                        if m < 2:
                            e.tensor_scalar(stf[:, 56 + 8 * m:64 + 8 * m],
                                            stf[:, 24:32],
                                            swcol[:, 3 * m + 2:3 * m + 3], None,
                                            ALU.mult)
                        e.drain().then_inc(cV.h, 1)
                PQ.add("vector", _p3_v)

                def _p3_t(e):
                    for m in range(3):
                        e.wait_ge(cV.h, v_wsum[m])
                        e.matmul(pb[5][0:1, 0:1],
                                 wsrun[:, 0:1], onescol[:],
                                 start=True, stop=True)
                        e.drain().then_inc(cT.h, 1)
                        e.wait_ge(cV.h, v_sw1[m])
                        e.matmul(pb[5][:, 100:101], onesr[:], sc11[0:1, 2:3],
                                 start=True, stop=True)
                        e.matmul(pb[5][:, 101:102], onesr[:], sc11[0:1, 1:2],
                                 start=True, stop=True)
                        e.drain().then_inc(cT.h, 1)
                PQ.add("tensor", _p3_t)

            L4 = ExitStack()
            with L4:
                hTa = L4.enter_context(nc.sbuf_tensor("hTa", [128, 16 * TOK], F16))
                hTb = L4.enter_context(nc.sbuf_tensor("hTb", [128, 28 * TOK], F16))

                def hT(k):
                    if k < 16:
                        return hTa[:, k * TOK:(k + 1) * TOK]
                    return hTb[:, (k - 16) * TOK:(k - 15) * TOK]

                def hTpair(kk, tt):
                    base = hTa if kk < 16 else hTb
                    k0 = kk if kk < 16 else kk - 16
                    dv = base[:].rearrange("p (k t) -> p k t", t=TOK)
                    return dv[:, k0:k0 + 2, tt * 128:(tt + 1) * 128]

                L5 = ExitStack()
                with L5:
                    sb5 = lambda *a, **k: L5.enter_context(nc.sbuf_tensor(*a, **k))
                    t1s = [sb5("t1s%d" % i, [128, NFC * OS], BF16) for i in range(2)]
                    t2s = [sb5("t2s%d" % i, [128, NFC * OS], BF16) for i in range(2)]
                    raw1 = [sb5("raw1%d" % i, [128, 1024], F32) for i in range(2)]
                    raw2 = [sb5("raw2%d" % i, [128, 1024], F32) for i in range(2)]
                    qa = sb5("qa", [128, 512], BF16)
                    qb = sb5("qb", [128, 512], BF16)
                    pby1 = [pb[0], pb[1], pb[4]]
                    pby2 = [pb[2], pb[3], pb[5]]
                    tmp1 = [sb5("tmp1%d" % i, [128, OS], F32) for i in range(4)]
                    tmp2 = [sb5("tmp2%d" % i, [128, OS], F32) for i in range(4)]
                    hsl = [sb5("hsl%d" % i, [128, OS], F16) for i in range(6)]

                    barrier()
                    d_r1 = {}
                    d_r2 = {}
                    v_q = {}
                    t_y = {}
                    s_y = {}
                    v_h = {}
                    t_h = {}
                    s_h = {}
                    for o in range(NOS):
                        for dg in range(4):
                            dS.n += 16
                            d_r1[o * 4 + dg] = dS.n
                            dS2.n += 16
                            d_r2[o * 4 + dg] = dS2.n
                    for o in range(NOS + 1):
                        if o < NOS:
                            for g in range(8):
                                v_q[o * 8 + g] = inc(cV)
                        if o >= 1:
                            for tt in range(NTT):
                                v_h[(o - 1, tt)] = inc(cV)
                    for step in range(NOS * NTT):
                        o, tt = divmod(step, NTT)
                        t_y[(o, tt)] = inc(cT)
                        if step >= 5:
                            po, ptt = divmod(step - 5, NTT)
                            t_h[(po, ptt)] = inc(cT)
                    for ps in range(NOS * NTT - 5, NOS * NTT):
                        t_h[(ps // 8, ps % 8)] = inc(cT)
                    for step in range(NOS * NTT):
                        s_y[(step // 8, step % 8)] = inc(cS)
                        if step >= 4:
                            s_h[((step - 4) // 8, (step - 4) % 8)] = inc(cS)
                    for ps in range(NOS * NTT - 4, NOS * NTT):
                        s_h[(ps // 8, ps % 8)] = inc(cS)
                    s_p4_done = inc(cS)

                    def _p4_sync(e):
                        for o in range(NOS):
                            for dg in range(4):
                                dgidx = o * 4 + dg
                                if dgidx >= 2:
                                    e.wait_ge(cV.h, v_q[(dgidx - 2) * 2 + 1])
                                e.dma_start(
                                    out=raw1[dgidx % 2][:].rearrange(
                                        "p (c f) -> p c f", f=OS),
                                    in_=dw1_c[:, 4 * dg:4 * dg + 4,
                                              o * OS:(o + 1) * OS]
                                ).then_inc(dS.h, 16)
                    PQ.add("sync", _p4_sync)

                    def _p4_gp(e):
                        for o in range(NOS):
                            for dg in range(4):
                                dgidx = o * 4 + dg
                                if dgidx >= 2:
                                    e.wait_ge(cV.h, v_q[(dgidx - 2) * 2 + 1])
                                e.dma_start(
                                    out=raw2[dgidx % 2][:].rearrange(
                                        "p (c f) -> p c f", f=OS),
                                    in_=dw2_c[:, 4 * dg:4 * dg + 4,
                                              o * OS:(o + 1) * OS]
                                ).then_inc(dS2.h, 16)
                    PQ.add("gpsimd", _p4_gp)

                    def _p4_s(e):
                        nstep = NOS * NTT

                        def hTev(ps):
                            po, ptt = divmod(ps, NTT)
                            e.wait_ge(cT.h, t_h[(po, ptt)] +
                                      (1 if ps < nstep - 2 else 0))
                            tco = 256 * ((ps // 2) % 2)
                            bank = pb[6 + (ps % 2)]
                            e.activation(hTpair(2 * po, ptt),
                                         bank[:, tco:tco + 256],
                                         AF.Identity).then_inc(cS.h, 1)

                        for step in range(nstep):
                            o, tt = divmod(step, NTT)
                            if step < nstep - 1:
                                nx = step + 1
                                e.wait_ge(cT.h, t_y[(nx // 8, nx % 8)])
                            else:
                                e.wait_ge(cT.h, t_y[(o, tt)])
                            if step >= 4:
                                prev = step - 4
                                e.wait_ge(cV.h, v_h[(prev // 8, prev % 8)])
                            pY1 = pby1[step % 3]
                            pY2 = pby2[step % 3]
                            e.activation(tmp1[step % 4][:], pY1[:, 0:OS],
                                         AF.Silu, scale=stf[:, 56 + tt:57 + tt])
                            e.activation(tmp2[step % 4][:], pY2[:, 0:OS],
                                         AF.Identity,
                                         scale=stf[:, 64 + tt:65 + tt])
                            e.drain().then_inc(cS.h, 1)
                            if step >= 4:
                                hTev(step - 4)
                        for ps in range(nstep - 4, nstep):
                            hTev(ps)
                        e.drain().then_inc(cS.h, 1)
                    PQ.add("scalar", _p4_s)

                    def _p4_v(e):
                        for o in range(NOS + 1):
                            if o < NOS:
                                for g in range(8):
                                    gidx = o * 8 + g
                                    dgidx = o * 4 + g // 2
                                    if g % 2 == 0:
                                        e.wait_ge(dS.h, d_r1[dgidx])
                                        e.wait_ge(dS2.h, d_r2[dgidx])
                                    if o >= 2 and g == 0:
                                        e.wait_ge(cT.h, t_y[(o - 2, NTT - 1)])
                                    half = (g % 2) * 512
                                    for mat, rawb, tds, thrc in (
                                            (0, raw1[dgidx % 2], t1s[o % 2], 9),
                                            (1, raw2[dgidx % 2], t2s[o % 2], 11)):
                                        raw = rawb[:, half:half + 512]
                                        e.tensor_scalar(
                                            qa[:], raw,
                                            swcol[:, thrc + 1:thrc + 2],
                                            None, ALU.is_lt)
                                        ins = e.scalar_tensor_tensor(
                                            tds[:, 2 * g * OS:(2 * g + 2) * OS],
                                            raw, swcol[:, thrc:thrc + 1],
                                            qa[:], ALU.is_gt, ALU.subtract)
                                    if g == 7:
                                        e.drain().then_inc(cV.h, 1)
                                    else:
                                        ins.then_inc(cV.h, 1)
                            if o >= 1:
                                oo = o - 1
                                for tt in range(NTT):
                                    step = oo * 8 + tt
                                    e.wait_ge(cS.h, s_y[(oo, tt)])
                                    if step >= 6:
                                        prev = step - 6
                                        e.wait_ge(cS.h, s_h[(prev // 8, prev % 8)])
                                    e.tensor_tensor(hsl[step % 6][:],
                                                    tmp1[step % 4][:],
                                                    tmp2[step % 4][:], ALU.mult)
                                    e.reduce_max(
                                        hmx[:, tt * 22 + oo:tt * 22 + oo + 1],
                                        hsl[step % 6][:], axis=AX.X,
                                        apply_absolute_value=True)
                                    e.drain().then_inc(cV.h, 1)
                    PQ.add("vector", _p4_v)

                    def _p4_t(e):
                        nstep = NOS * NTT

                        def trans(ps):
                            po, ptt = divmod(ps, NTT)
                            e.wait_ge(cV.h, v_h[(po, ptt)])
                            if ps >= 2:
                                p2 = ps - 2
                                e.wait_ge(cS.h, s_h[(p2 // 8, p2 % 8)])
                            tco = 256 * ((ps // 2) % 2)
                            bank = pb[6 + (ps % 2)]
                            for j in range(2):
                                ins = e.matmul(
                                    bank[:, tco + j * 128:tco + (j + 1) * 128],
                                    hsl[ps % 6][:, j * 128:(j + 1) * 128],
                                    ident16[:], start=True, stop=True)
                            if ps >= nstep - 3:
                                e.drain().then_inc(cT.h, 1)
                            else:
                                ins.then_inc(cT.h, 1)

                        for step in range(nstep):
                            o, tt = divmod(step, NTT)
                            e.wait_ge(cV.h, v_q[o * 8 + 7])
                            if step >= 3:
                                prev = step - 3
                                e.wait_ge(cS.h, s_y[(prev // 8, prev % 8)])
                            pY1 = pby1[step % 3]
                            pY2 = pby2[step % 3]
                            for i in range(NFC):
                                lhsT = rqT[:, i * TOK + tt * 128:
                                           i * TOK + (tt + 1) * 128]
                                e.matmul(pY1[:, 0:OS], lhsT,
                                         t1s[o % 2][:, i * OS:(i + 1) * OS],
                                         start=(i == 0), stop=(i == NFC - 1))
                                ins = e.matmul(pY2[:, 0:OS], lhsT,
                                               t2s[o % 2][:, i * OS:(i + 1) * OS],
                                               start=(i == 0), stop=(i == NFC - 1))
                            if step == nstep - 1:
                                e.drain().then_inc(cT.h, 1)
                            else:
                                ins.then_inc(cT.h, 1)
                            if step >= 5:
                                trans(step - 5)
                        for ps in range(nstep - 5, nstep):
                            trans(ps)
                    PQ.add("tensor", _p4_t)

                dbg_dump([("dbg_hTa", hTa[:]), ("dbg_hTb", hTb[:]),
                          ("dbg_stf2", stf[:]), ("dbg_sw", swcol[:])])
                L6 = ExitStack()
                with L6:
                    sb6 = lambda *a, **k: L6.enter_context(nc.sbuf_tensor(*a, **k))
                    p56 = sb6("p56", [128, 5632], F32)
                    lad6 = sb6("lad6", [128, 2048], F32)
                    xt6 = [sb6("xt6_%d" % i, [128, 512], F32) for i in range(2)]
                    ev6 = sb6("ev6", [128, 512], F32)
                    fin6 = sb6("fin6", [128, 512], F32)
                    qa3 = sb6("qa3", [128, 2816], F16)
                    srow = sb6("srow", [1, 2048], F32)
                    s3r_b = sb6("s3r_b", [128, 1024], F32)
                    qs2r_b = sb6("qs2r_b", [128, 1024], F32)
                    t3s = [rqT[:, i * FF:(i + 1) * FF].bitcast(F16)
                           for i in range(2)]

                    def lad_chunk(e, k, c0, scr, bare=True):
                        u = scr[:, 0:512]
                        y0 = scr[:, 512:1024]
                        w1_ = scr[:, 1024:1536]
                        w2_ = scr[:, 1536:2048]
                        hk = hT(k)[:, c0:c0 + 512]
                        e.tensor_tensor(w1_[:], hk, qs2r_b[:, c0:c0 + 512],
                                        ALU.mult)
                        e.tensor_single_scalar(y0[:], w1_[:], 128.5, ALU.add)
                        srk = y0
                        dsts = [w1_, w2_]
                        for wi, P in enumerate(LADDER):
                            dst = dsts[wi % 2]
                            e.tensor_scalar(u[:], srk[:], P, -P,
                                            ALU.is_gt, ALU.mult)
                            e.tensor_tensor(dst[:], srk[:], u[:], ALU.add)
                            srk = dst
                        ins = e.scalar_tensor_tensor(hk, y0[:], 128.0, srk[:],
                                                     ALU.subtract, ALU.subtract)
                        if bare:
                            ins.then_inc(cV.h, 1)
                        else:
                            e.drain().then_inc(cV.h, 1)

                    barrier()
                    v_qs2 = inc(cV)
                    dG.n += 16 * 16
                    d_srow = dG.n
                    t_row = inc(cT)
                    v_rows = inc(cV)
                    for k in range(NKC):
                        inc(cV)
                    v_p5_done = inc(cV)
                    d_t3 = {}
                    dS.n += 16
                    d_t3[(0, 0, 0)] = dS.n
                    dS2.n += 16
                    d_t3[(0, 0, 1)] = dS2.n

                    def _p5_v(e):
                        for tt in range(NTT):
                            e.reduce_max(stf[:, 40 + tt:41 + tt],
                                         hmx[:, tt * 22:(tt + 1) * 22],
                                         axis=AX.X, apply_absolute_value=True)
                        e.drain()
                        e.tensor_scalar_max(stf[:, 88:96], stf[:, 40:48], 1e-5)
                        e.drain()
                        e.reciprocal(stf[:, 72:80], stf[:, 88:96])
                        e.drain()
                        e.tensor_scalar_mul(stf[:, 48:56], stf[:, 72:80], 127.0)
                        e.drain()
                        e.reciprocal(stf[:, 72:80], stf[:, 48:56])
                        e.drain()
                        e.tensor_scalar(stf[:, 80:88], stf[:, 72:80],
                                        swcol[:, 8:9], None, ALU.mult)
                        e.drain().then_inc(cV.h, 1)
                        e.wait_ge(cT.h, t_row)
                        e.tensor_copy(qs2r_b[:, 0:512], pb[4][:, 0:512])
                        e.tensor_copy(qs2r_b[:, 512:1024], pb[5][:, 0:512])
                        e.tensor_copy(s3r_b[:, 0:512], pb[6][:, 0:512])
                        e.tensor_copy(s3r_b[:, 512:1024], pb[7][:, 0:512])
                        e.drain().then_inc(cV.h, 1)
                        for k in range(NKC):
                            lad_chunk(e, k, 0, lad6)
                        e.drain().then_inc(cV.h, 1)
                    PQ.add("vector", _p5_v)

                    def _p5_gp(e):
                        e.dma_start(
                            out=p56[:, 2816:5632].rearrange(
                                "p (c f) -> p c f", f=128),
                            in_=dw3_c[:, 22:44, 0:128]).then_inc(dS2.h, 16)
                        e.wait_ge(cV.h, v_qs2)
                        for tt in range(NTT):
                            e.dma_start(out=srow[0:1, tt * 128:(tt + 1) * 128],
                                        in_=stf[:, 48 + tt:49 + tt],
                                        allow_slow_non_contiguous=True
                                        ).then_inc(dG.h, 16)
                            e.dma_start(out=srow[0:1, 1024 + tt * 128:
                                                1152 + tt * 128],
                                        in_=stf[:, 80 + tt:81 + tt],
                                        allow_slow_non_contiguous=True
                                        ).then_inc(dG.h, 16)
                    PQ.add("gpsimd", _p5_gp)

                    def _p5_sync(e):
                        e.dma_start(
                            out=p56[:, 0:2816].rearrange(
                                "p (c f) -> p c f", f=128),
                            in_=dw3_c[:, 0:22, 0:128]).then_inc(dS.h, 16)
                    PQ.add("sync", _p5_sync)

                    def _p5_t(e):
                        e.wait_ge(dG.h, d_srow)
                        e.matmul(pb[4][:, 0:512], onesr[:], srow[0:1, 0:512],
                                 start=True, stop=True)
                        e.matmul(pb[5][:, 0:512], onesr[:], srow[0:1, 512:1024],
                                 start=True, stop=True)
                        e.matmul(pb[6][:, 0:512], onesr[:], srow[0:1, 1024:1536],
                                 start=True, stop=True)
                        e.matmul(pb[7][:, 0:512], onesr[:], srow[0:1, 1536:2048],
                                 start=True, stop=True)
                        e.drain().then_inc(cT.h, 1)
                    PQ.add("tensor", _p5_t)

                    dbg_dump([("dbg_q2a", hTa[:]), ("dbg_q2b", hTb[:]),
                              ("dbg_qs2r", qs2r_b[:]), ("dbg_s3r", s3r_b[:])])
                    barrier()
                    sched = [3] * 12 + [2] * 4
                    slot_ks = []
                    kk = 0
                    for o3 in range(NFC):
                        slot_ks.append(list(range(kk, kk + sched[o3])))
                        kk += sched[o3]
                    assert kk == NKC
                    d_x6 = {}
                    d_fst = {}
                    v_q3 = {}
                    v_ladB = {}
                    v_f = {}
                    t_y3 = {}
                    for p in range(2):
                        for o3 in range(NFC):
                            if (p, o3) == (0, 0):
                                continue
                            dS.n += 16
                            d_t3[(p, o3, 0)] = dS.n
                            dS2.n += 16
                            d_t3[(p, o3, 1)] = dS2.n
                    for p in range(2):
                        for o3 in range(NFC):
                            dA.n += 16
                            d_x6[(p, o3)] = dA.n
                            dA.n += 16
                            d_fst[(p, o3)] = dA.n
                    n_out = d_fst[(1, NFC - 1)]
                    for p in range(2):
                        for o3 in range(NFC):
                            idx = p * NFC + o3
                            if idx == 0:
                                for j in range(2):
                                    v_q3[(0, 0, j)] = inc(cV)
                            if p == 0:
                                for k in slot_ks[o3]:
                                    v_ladB[k] = inc(cV)
                            if idx + 1 < 2 * NFC:
                                np_, no3 = (idx + 1) // NFC, (idx + 1) % NFC
                                for j in range(2):
                                    v_q3[(np_, no3, j)] = inc(cV)
                            v_f[(p, o3)] = inc(cV)
                    for p in range(2):
                        for o3 in range(NFC):
                            t_y3[(p, o3)] = inc(cT)

                    def skey(n):
                        return (n // 32, (n % 32) // 2, n % 2)

                    def _p6_sync(e):
                        for idx in range(1, 2 * NFC):
                            p, o3 = idx // NFC, idx % NFC
                            e.wait_ge(cV.h, v_q3[(( idx - 1) // NFC,
                                                  (idx - 1) % NFC, 0)])
                            e.dma_start(
                                out=p56[:, 0:2816].rearrange(
                                    "p (c f) -> p c f", f=128),
                                in_=dw3_c[:, 0:22,
                                          o3 * 128:(o3 + 1) * 128]
                            ).then_inc(dS.h, 16)
                    PQ.add("sync", _p6_sync)

                    def _p6_s(e):
                        for idx in range(2 * NFC):
                            p, o3 = idx // NFC, idx % NFC
                            if idx >= 2:
                                pi = idx - 2
                                e.wait_ge(cV.h, v_f[(pi // NFC, pi % NFC)])
                            e.dma_start(
                                out=xt6[idx % 2][:],
                                in_=dout[o3 * 128:(o3 + 1) * 128,
                                         p * 512:(p + 1) * 512]
                            ).then_inc(dA.h, 16)
                            e.wait_ge(cV.h, v_f[(p, o3)])
                            e.dma_start(
                                out=dout[o3 * 128:(o3 + 1) * 128,
                                         p * 512:(p + 1) * 512],
                                in_=fin6[:]).then_inc(dA.h, 16)
                        e.wait_ge(dA.h, n_out)
                    PQ.add("scalar", _p6_s)

                    def _p6_gp(e):
                        for idx in range(1, 2 * NFC):
                            p, o3 = idx // NFC, idx % NFC
                            e.wait_ge(cV.h, v_q3[((idx - 1) // NFC,
                                                  (idx - 1) % NFC, 1)])
                            e.dma_start(
                                out=p56[:, 2816:5632].rearrange(
                                    "p (c f) -> p c f", f=128),
                                in_=dw3_c[:, 22:44,
                                          o3 * 128:(o3 + 1) * 128]
                            ).then_inc(dS2.h, 16)
                    PQ.add("gpsimd", _p6_gp)

                    def _p6_v(e):
                        def quant(idx):
                            p, o3 = idx // NFC, idx % NFC
                            for j in range(2):
                                e.wait_ge(dS.h if j == 0 else dS2.h,
                                          d_t3[(p, o3, j)])
                                if j == 0 and idx >= 2:
                                    pi = idx - 2
                                    e.wait_ge(cT.h, t_y3[(pi // NFC,
                                                          pi % NFC)])
                                raw = p56[:, j * 2816:(j + 1) * 2816]
                                dst = t3s[idx % 2][:, j * 2816:(j + 1) * 2816]
                                e.tensor_scalar(qa3[:], raw, swcol[:, 14:15],
                                                None, ALU.is_lt)
                                ins = e.scalar_tensor_tensor(
                                    dst, raw, swcol[:, 13:14],
                                    qa3[:], ALU.is_gt, ALU.subtract)
                                if j == 0:
                                    ins.then_inc(cV.h, 1)
                                else:
                                    e.drain().then_inc(cV.h, 1)

                        for p in range(2):
                            for o3 in range(NFC):
                                idx = p * NFC + o3
                                if idx == 0:
                                    quant(0)
                                if p == 0:
                                    for ki, k in enumerate(slot_ks[o3]):
                                        last = (o3 == NFC - 1 and
                                                ki == len(slot_ks[o3]) - 1)
                                        lad_chunk(e, k, 512, lad6,
                                                  bare=not last)
                                if idx + 1 < 2 * NFC:
                                    quant(idx + 1)
                                e.wait_ge(cT.h, t_y3[(p, o3)])
                                e.wait_ge(dA.h, d_x6[(p, o3)])
                                if idx >= 1:
                                    pidx = idx - 1
                                    e.wait_ge(dA.h, d_fst[(pidx // NFC,
                                                           pidx % NFC)])
                                bank = pb[idx % 4]
                                e.tensor_tensor(ev6[:], bank[:, 0:512],
                                                s3r_b[:, p * 512:(p + 1) * 512],
                                                ALU.mult)
                                e.tensor_tensor(fin6[:], xt6[idx % 2][:],
                                                ev6[:], ALU.add)
                                e.drain().then_inc(cV.h, 1)
                    PQ.add("vector", _p6_v)

                    def _p6_t(e):
                        v_ladB_last = max(v_ladB.values())
                        for p in range(2):
                            for o3 in range(NFC):
                                idx = p * NFC + o3
                                e.wait_ge(cV.h, v_q3[(p, o3, 1)])
                                if p == 1 and o3 == 0:
                                    e.wait_ge(cV.h, v_ladB_last)
                                if idx >= 4:
                                    pidx = idx - 4
                                    e.wait_ge(cV.h, v_f[(pidx // NFC,
                                                         pidx % NFC)])
                                bank = pb[idx % 4]
                                for k in range(NKC):
                                    lhsT = t3s[idx % 2][:, k * 128:(k + 1) * 128]
                                    hk = hT(k)[:, p * 512:(p + 1) * 512]
                                    e.matmul(bank[:, 0:512], lhsT, hk,
                                             start=(k == 0), stop=(k == NKC - 1))
                                e.drain().then_inc(cT.h, 1)
                    PQ.add("tensor", _p6_t)

                    PQ.flush()
    return nc


def kernel(L0_attn_hidden_states, L0_attn_Wq, L0_attn_Wk, L0_attn_Wv, L0_attn_Wo,
           L0_attn_bank_weights, L0_attn_memories, L0_attn_memory_norms,
           L0_ffn_w1, L0_ffn_w2, L0_ffn_w3):
    f32 = np.float32
    x = np.ascontiguousarray(np.asarray(L0_attn_hidden_states, f32).reshape(8192, HID))
    wqT = np.ascontiguousarray(np.asarray(L0_attn_Wq, f32).T)
    woT = np.ascontiguousarray(np.asarray(L0_attn_Wo, f32).T)
    w1T = np.ascontiguousarray(np.asarray(L0_ffn_w1, f32).T)
    w2T = np.ascontiguousarray(np.asarray(L0_ffn_w2, f32).T)
    w3T = np.ascontiguousarray(np.asarray(L0_ffn_w3, f32).T)
    bw = np.ascontiguousarray(np.asarray(L0_attn_bank_weights, f32))
    mem = np.asarray(L0_attn_memories, f32)
    mn = np.asarray(L0_attn_memory_norms, f32)
    mpm = np.empty((NH, 128, 512), f32)
    mpn = np.empty((128, 64), f32)
    for h in range(NH):
        mpm[h] = np.concatenate([mem[0, h], mem[1, h], mem[2, h], mem[3, h]],
                                axis=1)
        for n in range(4):
            mpn[:, 4 * h + n] = mn[n, h]

    nc = build()
    in_maps = []
    for c in range(8):
        xs = x[c * TOK:(c + 1) * TOK]
        in_maps.append(dict(x_tok=np.ascontiguousarray(xs),
                            xT=np.ascontiguousarray(xs.T),
                            wqT=wqT, woT=woT, mpm=mpm, mpn=mpn, bw=bw,
                            w1T=w1T, w2T=w2T, w3T=w3T))
    res = run_bass_kernel_spmd(nc, in_maps, core_ids=list(range(8)))
    out = np.empty((8192, HID), f32)
    for c in range(8):
        out[c * TOK:(c + 1) * TOK] = np.asarray(res.results[c]["out"]).T
    return out.reshape(2, 4096, HID)
